# version 11
# speedup vs baseline: 1.0506x; 1.0037x over previous
import numpy as np
import ml_dtypes
import concourse.bass as bass
import concourse.mybir as mybir
from concourse.bass_utils import run_bass_kernel_spmd
from contextlib import ExitStack

BF = ml_dtypes.bfloat16
F32 = mybir.dt.float32
BF16 = mybir.dt.bfloat16
AF = mybir.ActivationFunctionType
ALU = mybir.AluOpType
AX = mybir.AxisListType

NCORE = 8
T = 8192
D = 2048
NCH = D // 128
TOK = T // NCORE
INTER = 5632
NICH = INTER // 128
LN_EPS = 1e-5
EXP_SHIFT = -20.0
MASK_NEG = -30000.0

ENGINES = ("sync", "tensor", "vector", "scalar", "gpsimd")


class Sem:
    __slots__ = ("h", "n", "name")
    def __init__(self, h, name):
        self.h = h; self.n = 0; self.name = name
    def at(self):
        return (self, self.n)


class Prog:
    def __init__(self, nc, ctx):
        self.nc = nc
        self.ctx = ctx
        self.ops = {e: [] for e in ENGINES}
        self.nsem = 0
        self.all_sems = []
        self.floor = None

    def sem(self, name=None):
        name = f"{name or 's'}_{self.nsem}"
        self.nsem += 1
        s = Sem(self.ctx.enter_context(self.nc.semaphore(name)), name)
        self.all_sems.append(s)
        return s

    def emit(self, eng, fn, wait=(), inc=()):
        wait = tuple(wait) + ((self.floor,) if self.floor else ())
        for s, v in wait:
            assert v <= s.n, f"wait on future value of {s.name}: {v} > {s.n}"
        self.ops[eng].append((fn, wait, tuple(inc)))
        for s, step in inc:
            s.n += step

    def full_barrier(self, name="bar"):
        bar = self.sem(name)
        waits = [(s, s.n) for s in self.all_sems if s.n > 0 and s is not bar]
        self.floor = None
        for eng in ENGINES:
            self.emit(eng, lambda e: e.nop(), wait=waits, inc=[(bar, 1)])
        self.floor = (bar, len(ENGINES))
        return bar

    def dma(self, eng, out, in_, sem, wait=()):
        self.emit(eng, lambda e, o=out, i=in_: e.dma_start(out=o, in_=i),
                  wait=wait, inc=((sem, 16),))

    def finalize(self, block):
        hw = {}
        for eng in ENGINES:
            eng_ops = self.ops[eng]
            def body(e, eng=eng, eng_ops=eng_ops):
                for fn, waits, incs in eng_ops:
                    for s, v in waits:
                        k = (eng, s.name)
                        if hw.get(k, -1) < v:
                            e.wait_ge(s.h, v)
                            hw[k] = v
                    inst = fn(e)
                    for i, (s, step) in enumerate(incs):
                        assert inst is not None
                        if i == 0:
                            inst.then_inc(s.h, step)
                        else:
                            e.nop().then_inc(s.h, step)
            getattr(block, eng)(body)


def emit_mm_acc(p, psum_ap, pairs, wait=(), inc=(), start=True, stop=True):
    n = len(pairs)
    for i, (lh, rh) in enumerate(pairs):
        st = start and (i == 0)
        sp = stop and (i == n - 1)
        p.emit("tensor",
               lambda e, o=psum_ap, l=lh, r=rh, a=st, b=sp:
                   e.matmul(o, l, r, start=a, stop=b, skip_group_check=True),
               wait=(wait if i == 0 else ()),
               inc=(inc if i == n - 1 else ()))



def stage_lhsT_tiles(W):
    In, Out = W.shape
    nin, nout = In // 128, Out // 128
    return np.ascontiguousarray(
        np.asarray(W).reshape(nin, 128, nout, 128).transpose(2, 1, 0, 3)
        .reshape(nout, 128, nin * 128))

def rope_tables_l0():
    half = 64
    inv = 1.0 / (1000000.0 ** (np.arange(half, dtype=np.float64) / half))
    ang = np.outer(np.arange(T, dtype=np.float64), inv)
    c, s = np.cos(ang).T, np.sin(ang).T
    return (np.ascontiguousarray(np.concatenate([c, c], 0)).astype(BF),
            np.ascontiguousarray(np.concatenate([s, s], 0)).astype(BF))

def rope_tables_l1():
    inv = 1.0 / (10000.0 ** (np.arange(0, 64, 2, dtype=np.float64) / 64.0))
    ang = np.outer(np.arange(T, dtype=np.float64), inv)
    c, s = np.cos(ang).T, np.sin(ang).T
    return (np.ascontiguousarray(np.concatenate([c, c], 0)).astype(BF),
            np.ascontiguousarray(np.concatenate([s, s], 0)).astype(BF))

def perm_rh_T(n):
    h = n // 2
    P = np.zeros((n, n), dtype=np.float32)
    P[:h, h:] = -np.eye(h)
    P[h:, :h] = np.eye(h)
    return np.ascontiguousarray(P.T).astype(BF)

def tri_masks():
    tri = np.where(np.arange(128)[:, None] > np.arange(128)[None, :],
                   np.float32(MASK_NEG), np.float32(0.0))
    dead = np.full((128, 128), np.float32(MASK_NEG), dtype=np.float32)
    live = np.zeros((128, 128), dtype=np.float32)
    out = []
    for i in range(4):
        out.append(np.concatenate([dead] * i + [tri] + [live] * (3 - i), axis=1))
    return np.stack(out).astype(BF)

def fm(x):
    return np.ascontiguousarray(np.asarray(x).T)

def chunk_col(vec):
    return np.ascontiguousarray(np.asarray(vec).reshape(-1, 128).T)

def consts_common():
    consts = np.zeros((128, 2), dtype=np.float32)
    consts[:, 0] = LN_EPS
    consts[:, 1] = EXP_SHIFT
    return {
        "ONE1": np.ones((1, 128), dtype=BF),
        "ONEC": np.ones((128, 1), dtype=BF),
        "CONSTS": consts,
    }



def emit_ln_apply(p, xbf_tile, put_out, rowx, rowsq, bcr, bcb, rows, rhl, one1,
                  tmp32, s_row, s_bc, s_ap, wait_rows, wait_bc_extra=(),
                  nfeat=D, out_chunks=NCH):
    inv_n = 1.0 / nfeat
    S0, S1, S2 = rows[:, 0:512], rows[:, 512:1024], rows[:, 1024:1536]
    def step(eng, fn, extra=()):
        p.emit(eng, fn, wait=[s_row.at()] + list(extra), inc=[(s_row, 1)])
    step("vector", lambda e: e.tensor_scalar_mul(S0, rowx, inv_n), wait_rows)
    step("vector", lambda e: e.tensor_scalar_mul(S1, rowsq, inv_n))
    step("vector", lambda e: e.tensor_mul(S2, S0, S0))
    step("vector", lambda e: e.tensor_sub(S1, S1, S2))
    step("scalar", lambda e: e.activation(S2, S1, AF.Ln, bias=float(LN_EPS)))
    step("scalar", lambda e: e.activation(S2, S2, AF.Exp, scale=-0.5))
    step("vector", lambda e: e.tensor_mul(S1, S0, S2))
    step("vector", lambda e: e.tensor_scalar_mul(S1, S1, -1.0))
    step("vector", lambda e: e.tensor_copy(rhl[:, 0:512], S2))
    step("vector", lambda e: e.tensor_sub(S0, S2, rhl[:, 0:512]))
    step("vector", lambda e: e.tensor_copy(rhl[:, 512:1024], S0))
    step("vector", lambda e: e.tensor_copy(rhl[:, 1024:1536], S1))
    step("vector", lambda e: e.tensor_sub(S0, S1, rhl[:, 1024:1536]))
    step("vector", lambda e: e.tensor_copy(rhl[:, 1536:2048], S0))
    rv = s_row.at()
    emit_mm_acc(p, bcr, [(one1, rhl[:, 0:512]), (one1, rhl[:, 512:1024])],
                wait=[rv] + list(wait_bc_extra), inc=[(s_bc, 1)])
    emit_mm_acc(p, bcb, [(one1, rhl[:, 1024:1536]), (one1, rhl[:, 1536:2048])],
                inc=[(s_bc, 1)])
    bv = s_bc.at()
    apvals = []
    for ec in range(out_chunks):
        xt = xbf_tile(ec)
        ot = put_out(ec)
        p.emit("vector", lambda e, xt=xt: e.tensor_mul(tmp32[:], xt, bcr),
               wait=[bv, s_ap.at()], inc=[(s_ap, 1)])
        p.emit("vector", lambda e, ot=ot: e.tensor_add(ot, tmp32[:], bcb),
               wait=[s_ap.at()], inc=[(s_ap, 1)])
        apvals.append(s_ap.n)
    return apvals




def declare_globals(nc):
    return {
        "ONE1": nc.dram_tensor("ONE1", [1, 128], BF16, kind="ExternalInput").ap(),
        "ONEC": nc.dram_tensor("ONEC", [128, 1], BF16, kind="ExternalInput").ap(),
        "CONSTS": nc.dram_tensor("CONSTS", [128, 2], F32, kind="ExternalInput").ap(),
        "MASKS": nc.dram_tensor("MASKS", [4, 128, 512], BF16, kind="ExternalInput").ap(),
        "IDT": nc.dram_tensor("IDT", [128, 128], BF16, kind="ExternalInput").ap(),
    }

def stage_globals():
    cst = consts_common()
    return dict(ONE1=cst["ONE1"], ONEC=cst["ONEC"], CONSTS=cst["CONSTS"],
                MASKS=tri_masks(), IDT=np.eye(128, dtype=BF))

def emit_globals(p, nc, ctx, dr, with_masks=True):
    sb = lambda n, s, d=BF16: ctx.enter_context(nc.sbuf_tensor(n, s, d))
    g = {
        "one1": sb("g_one1", [1, 128]),
        "onec": sb("g_onec", [128, 1]),
        "csts": sb("g_csts", [128, 2], F32),
    }
    nc.const_aps.register(F32, float(LN_EPS), g["csts"][:, 0:1])
    nc.const_aps.register(F32, float(EXP_SHIFT), g["csts"][:, 1:2])
    s_gld = p.sem("gld")
    p.dma("sync", g["one1"][:], dr["ONE1"][:, :], s_gld)
    p.dma("sync", g["onec"][:], dr["ONEC"][:, :], s_gld)
    p.dma("sync", g["csts"][:], dr["CONSTS"][:, :], s_gld)
    if with_masks:
        g["masks"] = sb("g_masks", [128, 4 * 512])
        g["idt"] = sb("g_idt", [128, 128])
        p.dma("sync", g["masks"][:].rearrange("p (a c) -> p a c", a=4),
              dr["MASKS"].rearrange("a p c -> p a c"), s_gld)
        p.dma("sync", g["idt"][:], dr["IDT"][:, :], s_gld)
    g["ld"] = s_gld.at()
    return g



def emit_B(p, nc, ctx, g, dr, x_ap=None, out_writer=None, tag="b"):
    GW, UW, DW, GS, US, DS = dr["GW"], dr["UW"], dr["DW"], dr["GS"], dr["US"], dr["DS"]
    if True:
        sb = lambda n, s, d=BF16: ctx.enter_context(nc.sbuf_tensor(tag + n, s, d))
        psb = lambda n: ctx.enter_context(nc.psum_tensor(tag + n, [128, 512], F32))

        x = sb("x", [128, NCH * TOK]) if x_ap is None else x_ap
        h = sb("h", [128, NICH * TOK])
        wf = [sb(f"wf{i}", [128, 22 * 128], F32) for i in range(2)]
        wb = [sb(f"wb{i}", [128, 22 * 128]) for i in range(3)]
        gs = sb("gs", [128, NICH], F32)
        us = sb("us", [128, NICH], F32)
        ds = sb("ds", [128, NCH], F32)
        one1, onec = g["one1"], g["onec"]
        sg = [sb(f"sg{i}", [128, 512]) for i in range(2)]
        xbf = [sb(f"xbf{i}", [128, NCH * 512]) for i in range(2)]
        sq = sg
        rows = wf[0][0:1, 0:1536]
        rhl = wf[0][0:1, 1536:2560].bitcast(BF16)
        tmp32 = wf[1][:, 0:512]
        outsl = [h[:, 0:NCH * 512], h[:, NCH * 512:2 * NCH * 512]]

        BNK = [psb(f"B{i}") for i in range(8)]
        pg = [BNK[0], BNK[1]]
        pu = [BNK[2], BNK[3]]
        pd = [BNK[0], BNK[1]]
        rowx = [BNK[2][0:1, 0:512], BNK[6][0:1, 0:512]]
        rowsq = [BNK[3][0:1, 0:512], BNK[7][0:1, 0:512]]
        bcr, bcb = BNK[4], BNK[5]

        s_ld = p.sem("ld")
        s_w2 = [p.sem("w0"), p.sem("w1")]
        s_cva = p.sem("cva"); s_cvb = p.sem("cvb")
        s_mm = p.sem("mm"); s_si = p.sem("si")
        consumed_mm = {}
        s_ep = p.sem("ep"); s_row = p.sem("row"); s_bc = p.sem("bc")
        s_ap = p.sem("ap"); s_st = p.sem("st")

        if x_ap is None:
            p.dma("sync", x[:], dr["X"][:, :], s_ld)
        p.dma("sync", gs[:], GS[:, :], s_ld)
        p.dma("sync", us[:], US[:, :], s_ld)
        p.dma("sync", ds[:], DS[:, :], s_ld)
        ld0 = s_ld.at()
        gld = g["ld"]

        wlist = []
        for ic in range(NICH):
            wlist.append((GW[ic, :, :], NCH * 128))
            wlist.append((UW[ic, :, :], NCH * 128))
        for ec in range(NCH):
            wlist.append((DW[2 * ec, :, :], 22 * 128))
            wlist.append((DW[2 * ec + 1, :, :], 22 * 128))
        nW = len(wlist)

        emitted_w = set()
        def record_wload(wi):
            if wi >= nW or wi in emitted_w:
                return
            emitted_w.add(wi)
            src, ncols = wlist[wi]
            fb = wi % 2
            buf = wi % 3
            wait = []
            if wi >= 2:
                wait.append((s_cva, wi - 1))
                wait.append((s_cvb, wi - 1))
            if wi >= 3:
                wait.append((s_mm, consumed_mm[wi - 3]))
            p.dma("sync", wf[fb][:, 0:ncols], src, s_w2[fb], wait=wait)
            half = ncols // 2
            wv = (s_w2[fb], (wi // 2 + 1) * 16)
            cw = [wv] + ([(s_mm, consumed_mm[wi - 3])] if wi >= 3 else [])
            p.emit("scalar", lambda e, b=buf, f=fb, h2=half: e.copy(wb[b][:, 0:h2], wf[f][:, 0:h2]),
                   wait=cw, inc=[(s_cva, 1)])
            p.emit("gpsimd", lambda e, b=buf, f=fb, h2=half, n=ncols: e.tensor_copy(wb[b][:, h2:n], wf[f][:, h2:n]),
                   wait=cw, inc=[(s_cvb, 1)])

        for wi in range(3):
            record_wload(wi)

        for ic in range(NICH):
            gbuf = (2 * ic) % 3
            ubuf = (2 * ic + 1) % 3
            for sl in range(2):
                t0 = sl * 512
                wait = [(s_cva, 2 * ic + 1), (s_cvb, 2 * ic + 1), ld0, gld]
                if ic >= 1:
                    wait.append((s_si, 2 * (ic - 1) + sl + 1))
                pairs = [(wb[gbuf][:, hc * 128:(hc + 1) * 128],
                          x[:, hc * TOK + t0: hc * TOK + t0 + 512]) for hc in range(NCH)]
                emit_mm_acc(p, pg[sl][:], pairs, wait=wait, inc=[(s_mm, 1)])
                if sl == 1:
                    consumed_mm[2 * ic] = s_mm.n
            for sl in range(2):
                t0 = sl * 512
                wait = [(s_cva, 2 * ic + 2), (s_cvb, 2 * ic + 2)]
                if ic >= 1:
                    wait.append((s_ep, 2 * (ic - 1) + sl + 1))
                pairs = [(wb[ubuf][:, hc * 128:(hc + 1) * 128],
                          x[:, hc * TOK + t0: hc * TOK + t0 + 512]) for hc in range(NCH)]
                emit_mm_acc(p, pu[sl][:], pairs, wait=wait, inc=[(s_mm, 1)])
                if sl == 1:
                    consumed_mm[2 * ic + 1] = s_mm.n
            for sl in range(2):
                wait = [(s_mm, 4 * ic + sl + 1)]
                if ic >= 1:
                    wait.append((s_ep, 2 * (ic - 1) + sl + 1))
                p.emit("scalar",
                       lambda e, sl=sl, ic=ic: e.activation(sg[sl][:], pg[sl][:], AF.Silu,
                                                            scale=gs[:, ic:ic + 1]),
                       wait=wait, inc=[(s_si, 1)])
            for sl in range(2):
                dst = h[:, ic * TOK + sl * 512: ic * TOK + sl * 512 + 512]
                p.emit("vector",
                       lambda e, sl=sl, ic=ic, dst=dst:
                           e.scalar_tensor_tensor(dst, pu[sl][:], us[:, ic:ic + 1],
                                                  sg[sl][:], ALU.mult, ALU.mult),
                       wait=[(s_si, 2 * ic + sl + 1), (s_mm, 4 * ic + 2 + sl + 1)],
                       inc=[(s_ep, 1)])
            record_wload(2 * ic + 3)
            record_wload(2 * ic + 4)

        ep_all = s_ep.at()
        si_all = s_si.at()
        mm_d0 = s_mm.n
        dg_mm = {}
        ep_after_stt = {}
        si_after_sq = {}
        rsq_mm = {}
        def emit_rows_b(ec):
            for sl in range(2):
                emit_mm_acc(p, rowx[sl], [(onec[:], xbf[sl][:, ec * 512:(ec + 1) * 512])],
                            wait=[ep_after_stt[(ec, sl)]], inc=[(s_mm, 1)],
                            start=(ec == 0), stop=(ec == NCH - 1))
                emit_mm_acc(p, rowsq[sl], [(onec[:], sq[sl][:])],
                            wait=[si_after_sq[(ec, sl)]], inc=[(s_mm, 1)],
                            start=(ec == 0), stop=(ec == NCH - 1))
                rsq_mm[(ec, sl)] = s_mm.n
        for ec in range(NCH):
            w0 = 88 + 2 * ec
            b0 = w0 % 3
            b1 = (w0 + 1) % 3
            for sl in range(2):
                t0 = sl * 512
                wait = [(s_cva, w0 + 2), (s_cvb, w0 + 2), ep_all, si_all]
                if ec >= 1:
                    wait.append((s_ep, 88 + 2 * (ec - 1) + sl + 1))
                pairs = ([(wb[b0][:, k * 128:(k + 1) * 128],
                           h[:, k * TOK + t0: k * TOK + t0 + 512]) for k in range(22)] +
                         [(wb[b1][:, (k - 22) * 128:(k - 21) * 128],
                           h[:, k * TOK + t0: k * TOK + t0 + 512]) for k in range(22, 44)])
                emit_mm_acc(p, pd[sl][:], pairs, wait=wait, inc=[(s_mm, 1)])
                dg_mm[(ec, sl)] = s_mm.n
                if sl == 1:
                    consumed_mm[w0] = consumed_mm[w0 + 1] = s_mm.n
            record_wload(w0 + 2)
            record_wload(w0 + 3)
            if ec >= 1:
                emit_rows_b(ec - 1)
            for sl in range(2):
                mmv = (s_mm, dg_mm[(ec, sl)])
                dst = xbf[sl][:, ec * 512:(ec + 1) * 512]
                xres = x[:, ec * TOK + sl * 512: ec * TOK + sl * 512 + 512]
                p.emit("vector",
                       lambda e, sl=sl, ec=ec, dst=dst, xres=xres:
                           e.scalar_tensor_tensor(dst, pd[sl][:], ds[:, ec:ec + 1],
                                                  xres, ALU.mult, ALU.add),
                       wait=[mmv], inc=[(s_ep, 1)])
                ep_after_stt[(ec, sl)] = s_ep.at()
                sqw = [ep_after_stt[(ec, sl)]]
                if ec >= 1:
                    sqw.append((s_mm, rsq_mm[(ec - 1, sl)]))
                p.emit("scalar",
                       lambda e, sl=sl, ec=ec: e.square(sq[sl][:], xbf[sl][:, ec * 512:(ec + 1) * 512]),
                       wait=sqw, inc=[(s_si, 1)])
                si_after_sq[(ec, sl)] = s_si.at()
        emit_rows_b(NCH - 1)

        for sl in range(2):
            wait_rows = [s_mm.at()]
            wait_bc_extra = [s_ap.at()] if sl == 1 else []
            apv = emit_ln_apply(p,
                          xbf_tile=lambda ec, sl=sl: xbf[sl][:, ec * 512:(ec + 1) * 512],
                          put_out=lambda ec, sl=sl: outsl[sl][:, ec * 512:(ec + 1) * 512],
                          rowx=rowx[sl], rowsq=rowsq[sl], bcr=bcr[:], bcb=bcb[:],
                          rows=rows, rhl=rhl, one1=one1[:], tmp32=tmp32,
                          s_row=s_row, s_bc=s_bc, s_ap=s_ap,
                          wait_rows=wait_rows, wait_bc_extra=wait_bc_extra)
            if out_writer is not None:
                for g4 in range(4):
                    out_writer(sl, outsl[sl], [(s_ap, apv[4 * g4 + 3])],
                               ec0=4 * g4, ec1=4 * g4 + 4)
            else:
                dst = dr["XOUT"][:].rearrange("p (c t) -> p c t", t=TOK)[:, :, sl * 512:(sl + 1) * 512]
                srcap = outsl[sl][:].rearrange("p (c t) -> p c t", t=512)
                p.dma("gpsimd", dst, srcap, s_st, wait=[s_ap.at()])

        p.emit("gpsimd", lambda e: e.nop(), wait=[s_st.at()])
    return s_ap.at()


def declare_B(nc):
    return dict(
        GW=nc.dram_tensor("GW", [NICH, 128, NCH * 128], F32, kind="ExternalInput").ap(),
        UW=nc.dram_tensor("UW", [NICH, 128, NCH * 128], F32, kind="ExternalInput").ap(),
        DW=nc.dram_tensor("DW", [NCH * 2, 128, 22 * 128], F32, kind="ExternalInput").ap(),
        GS=nc.dram_tensor("GS", [128, NICH], F32, kind="ExternalInput").ap(),
        US=nc.dram_tensor("US", [128, NICH], F32, kind="ExternalInput").ap(),
        DS=nc.dram_tensor("DS", [128, NCH], F32, kind="ExternalInput").ap(),
    )


def build_B():
    nc = bass.Bass("TRN2", target_bir_lowering=False, debug=False)
    gdr = declare_globals(nc)
    dr = declare_B(nc)
    dr["X"] = nc.dram_tensor("X", [128, NCH * TOK], BF16, kind="ExternalInput").ap()
    dr["XOUT"] = nc.dram_tensor("XOUT", [128, NCH * TOK], BF16, kind="ExternalOutput").ap()
    with ExitStack() as ctx:
        p = Prog(nc, ctx)
        block = ctx.enter_context(nc.Block())
        g = emit_globals(p, nc, ctx, gdr, with_masks=False)
        emit_B(p, nc, ctx, g, dr)
        p.finalize(block)
    return nc


def stage_B(inp, x1_own_fm):
    cst = stage_globals()
    gw = stage_lhsT_tiles(np.asarray(inp["L0_ffn_gate_w"]).T)
    uw = stage_lhsT_tiles(np.asarray(inp["L0_ffn_up_w"]).T)
    dwt = stage_lhsT_tiles(np.asarray(inp["L0_ffn_down_w"]).T)
    dw = np.ascontiguousarray(
        dwt.reshape(NCH, 128, 2, 22 * 128).transpose(0, 2, 1, 3).reshape(NCH * 2, 128, 22 * 128))
    gs = chunk_col(inp["L0_ffn_gate_s"]).astype(np.float32)
    us = chunk_col(inp["L0_ffn_up_s"]).astype(np.float32)
    ds = chunk_col(inp["L0_ffn_down_s"]).astype(np.float32)
    maps = []
    for c in range(NCORE):
        maps.append(dict(X=x1_own_fm[c], GW=gw, UW=uw, DW=dw, GS=gs, US=us, DS=ds,
                         **cst))
    return maps


def x_fm_to_own(x_fm):
    xr = x_fm.reshape(NCH, 128, T)
    return [np.ascontiguousarray(
        xr[:, :, c * TOK:(c + 1) * TOK].transpose(1, 0, 2).reshape(128, NCH * TOK))
        for c in range(NCORE)]


def fm_own_to_full(outs):
    full = np.empty((NCH, 128, T), dtype=outs[0].dtype)
    for c in range(NCORE):
        full[:, :, c * TOK:(c + 1) * TOK] = outs[c].reshape(128, NCH, TOK).transpose(1, 0, 2)
    return full.reshape(D, T)






SCALE_L0 = 128.0 ** -0.5

def emit_A1(p, nc, ctx, g, dr, o_store=None, tag="a1"):
    XF, WQ, WK, WV = dr["XF"], dr["WQ"], dr["WK"], dr["WV"]
    BQ, BK, BV, COS, SIN, PIT, O = dr["BQ"], dr["BK"], dr["BV"], dr["COS"], dr["SIN"], dr["PIT"], dr["O"]
    if True:
        sb = lambda n, s, d=BF16: ctx.enter_context(nc.sbuf_tensor(tag + n, s, d))

        xs = [sb(f"xs{i}", [128, NCH * 512]) for i in range(2)]
        coss = [sb(f"coss{i}", [128, 512]) for i in range(2)]
        sins = [sb(f"sins{i}", [128, 512]) for i in range(2)]
        wq = sb("wq", [128, 2 * NCH * 128])
        wk = sb("wk", [128, NCH * 128])
        wv = sb("wv", [128, NCH * 128])
        bq = sb("bq", [128, 2])
        bk = sb("bk", [128, 1])
        bv = sb("bv", [128, 1])
        pit = sb("pit", [128, 128])
        idt, masks, one1, onec, csts = g["idt"], g["masks"], g["one1"], g["onec"], g["csts"]
        kbuf = sb("kbuf", [128, T])
        vtb = sb("vtb", [128, T])
        qbuf = sb("qbuf", [128, 2 * T])
        obuf = qbuf
        praw = [sb(f"praw{i}", [128, 512]) for i in range(2)]
        rp1 = [sb(f"rp1{i}", [128, 512]) for i in range(2)]
        rp2 = [sb(f"rp2{i}", [128, 512]) for i in range(2)]
        pt = [sb(f"pt{i}", [128, 1024]) for i in range(2)]
        srow = sb("srow", [1, 512], F32)
        rrow = sb("rrow", [1, 512], F32)
        rhl = sb("rhl", [1, 2 * 512])
        rb = sb("rb", [128, 512], F32)
        acc = sb("acc", [128, 1024], F32)
        accbf = sb("accbf", [128, 1024])

        S2 = [ctx.enter_context(nc.psum_tensor(f"{tag}S{i}", [128, 1024], F32)) for i in range(2)]
        POB = [ctx.enter_context(nc.psum_tensor(f"{tag}P{i}", [128, 512], F32)) for i in range(2)]
        BCB = ctx.enter_context(nc.psum_tensor(tag + "C", [128, 512], F32))
        prt = ctx.enter_context(nc.psum_tensor(tag + "prt", [128, 512], BF16))
        BNK = [S2[0][:, 0:512], S2[0][:, 512:1024], S2[1][:, 0:512], S2[1][:, 512:1024],
               POB[0][:], POB[1][:], BCB[:]]

        s_ld = p.sem("ld")
        s_xs = [p.sem("xs0"), p.sem("xs1")]
        s_cs = [p.sem("cs0"), p.sem("cs1")]
        s_mm = p.sem("mm")
        s_ev = p.sem("ev")
        s_exp = p.sem("exp")
        s_msk = p.sem("msk")
        s_nrm = p.sem("nrm")
        s_st = p.sem("st")

        for dst, srcap in ((wq[:].rearrange("p (a c) -> p a c", a=2), WQ.rearrange("a p c -> p a c")),
                           (wk[:], WK[:, :]),
                           (wv[:], WV[:, :]), (bq[:], BQ[:, :]), (bk[:], BK[:, :]), (bv[:], BV[:, :]),
                           (pit[:], PIT[:, :])):
            p.dma("sync", dst, srcap, s_ld)
        ld0 = s_ld.at()
        gld = g["ld"]

        consumed_xs = {}
        consumed_cs = {}
        bank_free = {}

        def load_slab(js):
            if js >= 16:
                return
            b = js % 2
            wait = [(s_mm, consumed_xs[js - 2])] if js >= 2 else []
            p.dma("sync", xs[b][:],
                  XF[:].rearrange("p (c t) -> p c t", t=T)[:, :, js * 512:(js + 1) * 512],
                  s_xs[b], wait=wait)
            waitc = [(s_ev, consumed_cs[js - 2])] if js >= 2 else []
            p.dma("gpsimd", coss[b][:], COS[:, js * 512:(js + 1) * 512], s_cs[b], wait=waitc)
            p.dma("gpsimd", sins[b][:], SIN[:, js * 512:(js + 1) * 512], s_cs[b])

        load_slab(0)
        load_slab(1)

        def rope_emit(bankid, pb, bias_ap, b, cv, dst_ap):
            mmv = s_mm.at()
            p.emit("scalar", lambda e: e.add(praw[pb][:], BNK[bankid], bias_ap),
                   wait=[mmv, s_ev.at()], inc=[(s_ev, 1)])
            bank_free[bankid] = s_ev.n
            rhb = 4 + b
            w = [s_ev.at()]
            if rhb in bank_free:
                w.append((s_ev, bank_free[rhb]))
            p.emit("tensor", lambda e: e.matmul(BNK[rhb], pit[:], praw[pb][:],
                                                start=True, stop=True, skip_group_check=True),
                   wait=w, inc=[(s_mm, 1)])
            rhv = s_mm.at()
            p.emit("gpsimd", lambda e: e.tensor_mul(rp1[pb][:], praw[pb][:], coss[b][:]),
                   wait=[s_ev.at(), cv], inc=[(s_ev, 1)])
            p.emit("vector", lambda e: e.tensor_mul(rp2[pb][:], BNK[rhb], sins[b][:]),
                   wait=[rhv, s_ev.at(), cv], inc=[(s_ev, 1)])
            bank_free[rhb] = s_ev.n
            p.emit("vector", lambda e: e.tensor_add(dst_ap, rp1[pb][:], rp2[pb][:]),
                   wait=[s_ev.at()], inc=[(s_ev, 1)])

        for js in range(16):
            b = js % 2
            xv = (s_xs[b], (js // 2 + 1) * 16)
            cv = (s_cs[b], (js // 2 + 1) * 32)
            t0 = js * 512

            def proj_group(widx, lh_sb, lh_base, gi):
                bankid = (4 * js + gi) % 4
                wait = [xv, ld0, gld]
                if bankid in bank_free:
                    wait.append((s_ev, bank_free[bankid]))
                pairs = [(lh_sb[:, lh_base + hc * 128: lh_base + (hc + 1) * 128],
                          xs[b][:, hc * 512:(hc + 1) * 512]) for hc in range(NCH)]
                emit_mm_acc(p, BNK[bankid], pairs, wait=wait, inc=[(s_mm, 1)])
                return bankid

            bk_id = proj_group(0, wk, 0, 0)
            rope_emit(bk_id, b, bk[:, 0:1], b, cv, kbuf[:, t0:t0 + 512])
            bv_id = proj_group(1, wv, 0, 1)
            mmv = s_mm.at()
            p.emit("scalar", lambda e, b=b, bv_id=bv_id: e.add(praw[b][:], BNK[bv_id], bv[:, 0:1]),
                   wait=[mmv, s_ev.at()], inc=[(s_ev, 1)])
            bank_free[bv_id] = s_ev.n
            for jt in range(4):
                w = [s_ev.at()]
                if "prt" in bank_free:
                    w.append((s_ev, bank_free["prt"]))
                p.emit("tensor",
                       lambda e, jt=jt, b=b: e.transpose(prt[:, 0:128], praw[b][:, jt * 128:(jt + 1) * 128], idt[:]),
                       wait=w, inc=[(s_mm, 1)])
                mmt = s_mm.at()
                p.emit("vector",
                       lambda e, jt=jt, t0=t0: e.tensor_copy(vtb[:, t0 + jt * 128: t0 + (jt + 1) * 128], prt[:, 0:128]),
                       wait=[mmt, s_ev.at()], inc=[(s_ev, 1)])
                bank_free["prt"] = s_ev.n
            for h in range(2):
                bq_id = proj_group(2 + h, wq, h * NCH * 128, 2 + h)
                if h == 1:
                    consumed_xs[js] = s_mm.n
                rope_emit(bq_id, b, bq[:, h:h + 1], b, cv, qbuf[:, h * T + t0: h * T + t0 + 512])
            consumed_cs[js] = s_ev.n
            load_slab(js + 2)

        barrier = s_ev.at()
        barrier_g = gld
        s_acc = p.sem("acc")
        sfree = {0: 0, 1: 0}
        ptcons = {0: 0, 1: 0}
        acccons = {0: 0, 1: 0}
        pofree = {0: None, 1: None}
        bc_free = None
        psm = BCB[0:1, 0:512]
        for h in range(2):
            for gq in range(16):
                pi = h * 16 + gq
                jn = 4 * (gq + 1)
                nu = jn // 2
                po = POB[pi % 2][:]
                qap = qbuf[:, h * T + gq * 512: h * T + (gq + 1) * 512]

                def emit_S(u, qap=qap):
                    w = [barrier, barrier_g, (s_exp, sfree[u % 2])]
                    for hf in range(2):
                        jt = 2 * u + hf
                        p.emit("tensor",
                               lambda e, u=u, jt=jt, hf=hf, qap=qap: e.matmul(
                                   S2[u % 2][:, hf * 512:(hf + 1) * 512],
                                   kbuf[:, jt * 128:(jt + 1) * 128],
                                   qap, start=True, stop=True, skip_group_check=True),
                               wait=(w if hf == 0 else ()), inc=[(s_mm, 1)])
                    return s_mm.n

                sv = emit_S(0)
                for u in range(nu):
                    i0 = 2 * u - 4 * gq
                    if i0 >= 0:
                        p.emit("vector",
                               lambda e, u=u, i0=i0: e.tensor_add(S2[u % 2][:], S2[u % 2][:],
                                                                  masks[:, i0 * 512:(i0 + 2) * 512]),
                               wait=[(s_mm, sv), s_msk.at()], inc=[(s_msk, 1)])
                        ewait = [(s_msk, s_msk.n)]
                    else:
                        ewait = [(s_mm, sv)]
                    p.emit("scalar",
                           lambda e, u=u: e.activation(pt[u % 2][:], S2[u % 2][:], AF.Exp,
                                                       bias=float(EXP_SHIFT), scale=float(SCALE_L0)),
                           wait=ewait + [(s_mm, ptcons[u % 2]), (s_acc, acccons[u % 2])],
                           inc=[(s_exp, 1)])
                    sfree[u % 2] = s_exp.n
                    ev = s_exp.n
                    if u + 1 < nu:
                        sv = emit_S(u + 1)
                    w = [(s_exp, ev)]
                    if u == 0 and pofree[pi % 2] is not None:
                        w.append((s_nrm, pofree[pi % 2]))
                    for hf in range(2):
                        jt = 2 * u + hf
                        p.emit("tensor",
                               lambda e, jt=jt, u=u, hf=hf, po=po: e.matmul(
                                   po, vtb[:, jt * 128:(jt + 1) * 128],
                                   pt[u % 2][:, hf * 512:(hf + 1) * 512],
                                   start=(jt == 0), stop=(jt == jn - 1), skip_group_check=True),
                               wait=(w if hf == 0 else ()), inc=[(s_mm, 1)])
                    ptcons[u % 2] = s_mm.n
                    if u == 0:
                        p.emit("vector", lambda e, u=u: e.tensor_copy(acc[:], pt[u % 2][:]),
                               wait=[(s_exp, ev), s_acc.at()], inc=[(s_acc, 1)])
                    else:
                        p.emit("vector", lambda e, u=u: e.tensor_add(acc[:], acc[:], pt[u % 2][:]),
                               wait=[(s_exp, ev), s_acc.at()], inc=[(s_acc, 1)])
                    acccons[u % 2] = s_acc.n

                p.emit("vector", lambda e: e.tensor_copy(accbf[:], acc[:]),
                       wait=[s_acc.at()], inc=[(s_acc, 1)])
                w = [s_acc.at()]
                if bc_free is not None:
                    w.append((s_nrm, bc_free))
                emit_mm_acc(p, psm, [(onec[:], accbf[:, 0:512]), (onec[:], accbf[:, 512:1024])],
                            wait=w, inc=[(s_mm, 1)])
                nv = s_mm.at()
                p.emit("vector", lambda e, psm=psm: e.tensor_copy(srow[:], psm),
                       wait=[nv, s_nrm.at()], inc=[(s_nrm, 1)])
                p.emit("vector", lambda e: e.reciprocal(rrow[:], srow[:]),
                       wait=[s_nrm.at()], inc=[(s_nrm, 1)])
                p.emit("vector", lambda e: e.tensor_copy(rhl[:, 0:512], rrow[:]),
                       wait=[s_nrm.at()], inc=[(s_nrm, 1)])
                p.emit("vector", lambda e: e.tensor_sub(srow[:], rrow[:], rhl[:, 0:512]),
                       wait=[s_nrm.at()], inc=[(s_nrm, 1)])
                p.emit("vector", lambda e: e.tensor_copy(rhl[:, 512:1024], srow[:]),
                       wait=[s_nrm.at()], inc=[(s_nrm, 1)])
                emit_mm_acc(p, BCB[:], [(one1[:], rhl[:, 0:512]), (one1[:], rhl[:, 512:1024])],
                            wait=[s_nrm.at()], inc=[(s_mm, 1)])
                bcv = s_mm.at()
                p.emit("vector", lambda e: e.tensor_copy(rb[:], BCB[:]),
                       wait=[bcv, s_nrm.at()], inc=[(s_nrm, 1)])
                bc_free = s_nrm.n
                dst = obuf[:, h * T + gq * 512: h * T + (gq + 1) * 512]
                p.emit("vector",
                       lambda e, dst=dst, po=po: e.scalar_tensor_tensor(dst, po, 1.0, rb[:],
                                                                        ALU.mult, ALU.mult),
                       wait=[s_nrm.at()], inc=[(s_nrm, 1)])
                pofree[pi % 2] = s_nrm.n
                if o_store is not None and h == 1 and gq % 2 == 1:
                    k = (gq - 1) // 2
                    o_store(h, obuf[:, h * T + k * TOK: h * T + (k + 1) * TOK],
                            [s_nrm.at()], chunk=k)

            if o_store is not None:
                if h == 0:
                    o_store(h, obuf[:, h * T:(h + 1) * T], [s_nrm.at()])
            else:
                p.dma("gpsimd", O[:, h * T:(h + 1) * T], obuf[:, h * T:(h + 1) * T],
                      s_st, wait=[s_nrm.at()])

        p.emit("gpsimd", lambda e: e.nop(), wait=[s_st.at()])
    return s_st.at()


def declare_A1(nc, o_ap):
    return dict(
        XF=nc.dram_tensor("XF", [128, NCH * T], BF16, kind="ExternalInput").ap(),
        WQ=nc.dram_tensor("WQ", [2, 128, NCH * 128], BF16, kind="ExternalInput").ap(),
        WK=nc.dram_tensor("WK", [128, NCH * 128], BF16, kind="ExternalInput").ap(),
        WV=nc.dram_tensor("WV", [128, NCH * 128], BF16, kind="ExternalInput").ap(),
        BQ=nc.dram_tensor("BQ", [128, 2], BF16, kind="ExternalInput").ap(),
        BK=nc.dram_tensor("BK", [128, 1], BF16, kind="ExternalInput").ap(),
        BV=nc.dram_tensor("BV", [128, 1], BF16, kind="ExternalInput").ap(),
        COS=nc.dram_tensor("COS", [128, T], BF16, kind="ExternalInput").ap(),
        SIN=nc.dram_tensor("SIN", [128, T], BF16, kind="ExternalInput").ap(),
        PIT=nc.dram_tensor("PIT", [128, 128], BF16, kind="ExternalInput").ap(),
        O=o_ap,
    )


def build_A1():
    nc = bass.Bass("TRN2", target_bir_lowering=False, debug=False)
    gdr = declare_globals(nc)
    dr = declare_A1(nc, nc.dram_tensor("O", [128, 2 * T], BF16, kind="ExternalOutput").ap())
    with ExitStack() as ctx:
        p = Prog(nc, ctx)
        block = ctx.enter_context(nc.Block())
        g = emit_globals(p, nc, ctx, gdr)
        emit_A1(p, nc, ctx, g, dr)
        p.finalize(block)
    return nc


def stage_A1(inp):
    cst = stage_globals()
    cos, sin = rope_tables_l0()
    pit = perm_rh_T(128)
    Wq = np.asarray(inp["L0_attn_Wq"])
    Wk = np.asarray(inp["L0_attn_Wk"])
    Wv = np.asarray(inp["L0_attn_Wv"])
    bqv = np.asarray(inp["L0_attn_bq"])
    bkv = np.asarray(inp["L0_attn_bk"])
    bvv = np.asarray(inp["L0_attn_bv"])
    x = np.asarray(inp["L0_attn_x"])[0]
    xf = fm(x)
    XF = np.ascontiguousarray(xf.reshape(NCH, 128, T).transpose(1, 0, 2).reshape(128, NCH * T))
    wq_t = stage_lhsT_tiles(Wq)
    wk_t = stage_lhsT_tiles(Wk)
    wv_t = stage_lhsT_tiles(Wv)
    maps = []
    for c in range(NCORE):
        h0 = 2 * c
        kv = h0 // 4
        maps.append(dict(
            XF=XF,
            WQ=np.ascontiguousarray(wq_t[h0:h0 + 2]),
            WK=np.ascontiguousarray(wk_t[kv]),
            WV=np.ascontiguousarray(wv_t[kv]),
            BQ=np.ascontiguousarray(bqv.reshape(16, 128)[h0:h0 + 2].T),
            BK=np.ascontiguousarray(bkv.reshape(4, 128)[kv:kv + 1].T),
            BV=np.ascontiguousarray(bvv.reshape(4, 128)[kv:kv + 1].T),
            COS=cos, SIN=sin, PIT=pit, **cst,
        ))
    return maps


def emit_A2(p, nc, ctx, g, dr, wo_f32=False, of_loader=None, xo_ap=None,
            out_writer=None, tag="a2"):
    WO = dr["WO"]
    if True:
        sb = lambda n, s, d=BF16: ctx.enter_context(nc.sbuf_tensor(tag + n, s, d))
        of = sb("of", [128, NCH * TOK])
        xo = sb("xo", [128, NCH * TOK]) if xo_ap is None else xo_ap
        wot = [sb(f"wot{i}", [128, NCH * 128]) for i in range(2)]
        wof = [sb(f"wof{i}", [128, NCH * 128], F32) for i in range(2)] if wo_f32 else None
        one1, onec, csts = g["one1"], g["onec"], g["csts"]
        xbf = [sb(f"xbf{i}", [128, NCH * 512]) for i in range(2)]
        sq = [sb(f"sq{i}", [128, 512]) for i in range(2)]
        rows = sb("rows", [1, 3 * 512], F32)
        rhl = sb("rhl", [1, 4 * 512])
        tmp32 = sb("tmp32", [128, 512], F32)
        outsl = [of[:, 0:NCH * 512], of[:, NCH * 512:2 * NCH * 512]]

        BNK = [ctx.enter_context(nc.psum_tensor(f"{tag}B{i}", [128, 512], F32)) for i in range(8)]
        pd = [BNK[0], BNK[1]]
        rowx = [BNK[2][0:1, 0:512], BNK[6][0:1, 0:512]]
        rowsq = [BNK[3][0:1, 0:512], BNK[7][0:1, 0:512]]
        bcr, bcb = BNK[4], BNK[5]

        s_ld = p.sem("ld"); s_mm = p.sem("mm"); s_si = p.sem("si")
        s_ep = p.sem("ep"); s_row = p.sem("row"); s_bc = p.sem("bc")
        s_ap = p.sem("ap"); s_st = p.sem("st"); s_cvw = p.sem("cvw")

        if of_loader is None:
            p.dma("sync", of[:], dr["OF"][:, :], s_ld)
        else:
            of_loader(p, of, s_ld)
        if xo_ap is None:
            p.dma("sync", xo[:], dr["XO"][:, :], s_ld)
        ld0 = (s_ld, s_ld.n)
        gld = g["ld"]

        s_wl = [p.sem("wl0"), p.sem("wl1")]
        dg_mm = {}
        wo_ready = {}
        def load_wo(ec):
            if ec >= NCH:
                return
            b = ec % 2
            wait = []
            if ec >= 2:
                wait.append((s_mm, dg_mm[(ec - 2, 1)]))
            if wo_f32:
                if ec >= 2:
                    wait.append((s_cvw, ec - 1))
                p.dma("sync", wof[b][:], WO[ec, :, :], s_wl[b], wait=wait)
                cw = [(s_wl[b], (ec // 2 + 1) * 16)]
                if ec >= 2:
                    cw.append((s_mm, dg_mm[(ec - 2, 1)]))
                p.emit("scalar", lambda e, b=b: e.copy(wot[b][:], wof[b][:]),
                       wait=cw, inc=[(s_cvw, 1)])
                wo_ready[ec] = (s_cvw, s_cvw.n)
            else:
                p.dma("sync", wot[b][:], WO[ec, :, :], s_wl[b], wait=wait)
                wo_ready[ec] = (s_wl[b], (ec // 2 + 1) * 16)
        load_wo(0)
        load_wo(1)

        ep_after_stt = {}
        si_after_sq = {}
        def emit_rows_a2(ec):
            for sl in range(2):
                emit_mm_acc(p, rowx[sl], [(onec[:], xbf[sl][:, ec * 512:(ec + 1) * 512])],
                            wait=[ep_after_stt[(ec, sl)]], inc=[(s_mm, 1)],
                            start=(ec == 0), stop=(ec == NCH - 1))
                emit_mm_acc(p, rowsq[sl], [(onec[:], sq[sl][:])],
                            wait=[si_after_sq[(ec, sl)]], inc=[(s_mm, 1)],
                            start=(ec == 0), stop=(ec == NCH - 1))
                dg_mm[("rsq", ec, sl)] = s_mm.n

        for ec in range(NCH):
            for sl in range(2):
                t0 = sl * 512
                wait = [ld0, gld, wo_ready[ec]]
                if ec >= 1:
                    wait.append((s_ep, 2 * (ec - 1) + sl + 1))
                pairs = [(wot[ec % 2][:, fc * 128:(fc + 1) * 128],
                          of[:, fc * TOK + t0: fc * TOK + t0 + 512]) for fc in range(NCH)]
                emit_mm_acc(p, pd[sl][:], pairs, wait=wait, inc=[(s_mm, 1)])
                dg_mm[(ec, sl)] = s_mm.n
            load_wo(ec + 2)
            if ec >= 1:
                emit_rows_a2(ec - 1)
            for sl in range(2):
                mmv = (s_mm, dg_mm[(ec, sl)])
                dst = xbf[sl][:, ec * 512:(ec + 1) * 512]
                xres = xo[:, ec * TOK + sl * 512: ec * TOK + sl * 512 + 512]
                p.emit("vector",
                       lambda e, sl=sl, dst=dst, xres=xres: e.tensor_add(dst, pd[sl][:], xres),
                       wait=[mmv], inc=[(s_ep, 1)])
                ep_after_stt[(ec, sl)] = s_ep.at()
                sqw = [ep_after_stt[(ec, sl)]]
                if ec >= 1:
                    sqw.append((s_mm, dg_mm[("rsq", ec - 1, sl)]))
                p.emit("scalar",
                       lambda e, sl=sl, ec=ec: e.square(sq[sl][:], xbf[sl][:, ec * 512:(ec + 1) * 512]),
                       wait=sqw, inc=[(s_si, 1)])
                si_after_sq[(ec, sl)] = s_si.at()
        emit_rows_a2(NCH - 1)

        for sl in range(2):
            wait_rows = [s_mm.at()]
            wait_bc_extra = [s_ap.at()] if sl == 1 else []
            apv = emit_ln_apply(p,
                          xbf_tile=lambda ec, sl=sl: xbf[sl][:, ec * 512:(ec + 1) * 512],
                          put_out=lambda ec, sl=sl: outsl[sl][:, ec * 512:(ec + 1) * 512],
                          rowx=rowx[sl], rowsq=rowsq[sl], bcr=bcr[:], bcb=bcb[:],
                          rows=rows, rhl=rhl, one1=one1[:], tmp32=tmp32,
                          s_row=s_row, s_bc=s_bc, s_ap=s_ap,
                          wait_rows=wait_rows, wait_bc_extra=wait_bc_extra)
            if out_writer is not None:
                for g4 in range(4):
                    out_writer(sl, outsl[sl], [(s_ap, apv[4 * g4 + 3])],
                               ec0=4 * g4, ec1=4 * g4 + 4)
            else:
                dst = dr["X1"][:].rearrange("p (c t) -> p c t", t=TOK)[:, :, sl * 512:(sl + 1) * 512]
                srcap = outsl[sl][:].rearrange("p (c t) -> p c t", t=512)
                p.dma("gpsimd", dst, srcap, s_st, wait=[s_ap.at()])

        p.emit("gpsimd", lambda e: e.nop(), wait=[s_st.at()])
    return s_ap.at()


def build_A2(wo_f32=False):
    nc = bass.Bass("TRN2", target_bir_lowering=False, debug=False)
    gdr = declare_globals(nc)
    dr = dict(
        OF=nc.dram_tensor("OF", [128, NCH * TOK], BF16, kind="ExternalInput").ap(),
        XO=nc.dram_tensor("XO", [128, NCH * TOK], BF16, kind="ExternalInput").ap(),
        WO=nc.dram_tensor("WO", [NCH, 128, NCH * 128], F32 if wo_f32 else BF16,
                          kind="ExternalInput").ap(),
        X1=nc.dram_tensor("X1", [128, NCH * TOK], BF16, kind="ExternalOutput").ap(),
    )
    with ExitStack() as ctx:
        p = Prog(nc, ctx)
        block = ctx.enter_context(nc.Block())
        g = emit_globals(p, nc, ctx, gdr, with_masks=False)
        emit_A2(p, nc, ctx, g, dr, wo_f32=wo_f32)
        p.finalize(block)
    return nc


def o_heads_to_full_fm(o_outs):
    full = np.empty((128, NCH, T), dtype=o_outs[0].dtype)
    for c in range(NCORE):
        o = np.asarray(o_outs[c]).reshape(128, 2, T)
        full[:, 2 * c, :] = o[:, 0, :]
        full[:, 2 * c + 1, :] = o[:, 1, :]
    return full.reshape(128, NCH * T)


def fm_full_slice_own(XFfm, c):
    return np.ascontiguousarray(
        XFfm.reshape(128, NCH, T)[:, :, c * TOK:(c + 1) * TOK].reshape(128, NCH * TOK))


def stage_A2(inp, o_full_fm, x_own_list):
    cst = stage_globals()
    wo_t = stage_lhsT_tiles(np.asarray(inp["L0_attn_Wo"]))
    maps = []
    for c in range(NCORE):
        maps.append(dict(
            OF=fm_full_slice_own(o_full_fm, c),
            XO=x_own_list[c],
            WO=wo_t,
            **cst,
        ))
    return maps


def emit_C0(p, nc, ctx, g, dr, x_ap=None, kr_dup_ap=None, slab_store=None, tag="c0"):
    WCKV, WCQ, WKR = dr["WCKV"], dr["WCQ"], dr["WKR"]
    CKV, CQ, KR = dr["CKV"], dr["CQ"], dr["KR"]
    if True:
        sb = lambda n, s, d=BF16: ctx.enter_context(nc.sbuf_tensor(tag + n, s, d))
        x = sb("x", [128, NCH * TOK]) if x_ap is None else x_ap
        wf = sb("wf", [128, 9 * NCH * 128], F32)
        wb = sb("wb", [128, 9 * NCH * 128])
        ckv = sb("ckv", [128, 4 * TOK])
        cq = sb("cq", [128, 4 * TOK])
        kr = sb("kr", [64, TOK])
        BNK = [ctx.enter_context(nc.psum_tensor(f"{tag}B{i}", [128, 512], F32)) for i in range(4)]

        s_ld = p.sem("ld"); s_cv = p.sem("cv"); s_mm = p.sem("mm")
        s_ev = p.sem("ev"); s_st = p.sem("st")

        if x_ap is None:
            p.dma("sync", x[:], dr["X2"][:, :], s_ld)
        p.dma("sync", wf[:, 0: 4 * NCH * 128].rearrange("p (a c) -> p a c", a=4),
              WCKV.rearrange("a p c -> p a c"), s_ld)
        p.dma("sync", wf[:, 4 * NCH * 128: 8 * NCH * 128].rearrange("p (a c) -> p a c", a=4),
              WCQ.rearrange("a p c -> p a c"), s_ld)
        p.dma("sync", wf[:, 8 * NCH * 128: 8 * NCH * 128 + NCH * 64], WKR[:, :], s_ld)
        ld0 = s_ld.at()
        used = 8 * NCH * 128 + NCH * 64
        half = used // 2
        p.emit("scalar", lambda e, half=half: e.copy(wb[:, 0:half], wf[:, 0:half]),
               wait=[ld0], inc=[(s_cv, 1)])
        p.emit("gpsimd", lambda e, half=half, used=used: e.tensor_copy(wb[:, half:used], wf[:, half:used]),
               wait=[ld0], inc=[(s_cv, 1)])
        cv = s_cv.at()

        bank_free = {}
        gi = 0
        for sl in range(2):
            t0 = sl * 512
            for ck in range(4):
                b = gi % 4
                w = [cv]
                if b in bank_free:
                    w.append((s_ev, bank_free[b]))
                pairs = [(wb[:, (ck * NCH + hc) * 128:(ck * NCH + hc + 1) * 128],
                          x[:, hc * TOK + t0: hc * TOK + t0 + 512]) for hc in range(NCH)]
                emit_mm_acc(p, BNK[b][:], pairs, wait=w, inc=[(s_mm, 1)])
                mmv = s_mm.at()
                p.emit("vector",
                       lambda e, b=b, ck=ck, t0=t0: e.tensor_copy(ckv[:, ck * TOK + t0: ck * TOK + t0 + 512], BNK[b][:]),
                       wait=[mmv, s_ev.at()], inc=[(s_ev, 1)])
                bank_free[b] = s_ev.n
                gi += 1
            for ck in range(4):
                b = gi % 4
                w = [cv]
                if b in bank_free:
                    w.append((s_ev, bank_free[b]))
                pairs = [(wb[:, ((4 + ck) * NCH + hc) * 128:((4 + ck) * NCH + hc + 1) * 128],
                          x[:, hc * TOK + t0: hc * TOK + t0 + 512]) for hc in range(NCH)]
                emit_mm_acc(p, BNK[b][:], pairs, wait=w, inc=[(s_mm, 1)])
                mmv = s_mm.at()
                p.emit("vector",
                       lambda e, b=b, ck=ck, t0=t0: e.tensor_copy(cq[:, ck * TOK + t0: ck * TOK + t0 + 512], BNK[b][:]),
                       wait=[mmv, s_ev.at()], inc=[(s_ev, 1)])
                bank_free[b] = s_ev.n
                gi += 1
            b = gi % 4
            w = [cv]
            if b in bank_free:
                w.append((s_ev, bank_free[b]))
            pairs = [(wb[:, 8 * NCH * 128 + hc * 64: 8 * NCH * 128 + (hc + 1) * 64],
                      x[:, hc * TOK + t0: hc * TOK + t0 + 512]) for hc in range(NCH)]
            emit_mm_acc(p, BNK[b][0:64, 0:512], pairs, wait=w, inc=[(s_mm, 1)])
            mmv = s_mm.at()
            p.emit("vector",
                   lambda e, b=b, t0=t0: e.tensor_copy(kr[0:64, t0:t0 + 512], BNK[b][0:64, 0:512]),
                   wait=[mmv, s_ev.at()], inc=[(s_ev, 1)])
            bank_free[b] = s_ev.n
            gi += 1
            if slab_store is not None:
                slab_store(sl, ckv, cq, kr, [s_ev.at()])

        if slab_store is None:
            p.dma("gpsimd", CKV[:, :], ckv[:], s_st, wait=[s_ev.at()])
            p.dma("gpsimd", CQ[:, :], cq[:], s_st, wait=[s_ev.at()])
            p.dma("gpsimd", KR[:, :], kr[:], s_st, wait=[s_ev.at()])
            if kr_dup_ap is not None:
                p.dma("gpsimd", kr_dup_ap, kr[:], s_st)
        p.emit("gpsimd", lambda e: e.nop(), wait=[s_st.at()])
    return s_st.at()


def declare_C0(nc, ckv_ap, cq_ap, kr_ap):
    return dict(
        WCKV=nc.dram_tensor("WCKV", [4, 128, NCH * 128], F32, kind="ExternalInput").ap(),
        WCQ=nc.dram_tensor("WCQ", [4, 128, NCH * 128], F32, kind="ExternalInput").ap(),
        WKR=nc.dram_tensor("WKR", [128, NCH * 64], F32, kind="ExternalInput").ap(),
        CKV=ckv_ap, CQ=cq_ap, KR=kr_ap,
    )


def build_C0():
    nc = bass.Bass("TRN2", target_bir_lowering=False, debug=False)
    gdr = declare_globals(nc)
    dr = declare_C0(nc,
                    nc.dram_tensor("CKV", [128, 4 * TOK], BF16, kind="ExternalOutput").ap(),
                    nc.dram_tensor("CQ", [128, 4 * TOK], BF16, kind="ExternalOutput").ap(),
                    nc.dram_tensor("KR", [64, TOK], BF16, kind="ExternalOutput").ap())
    dr["X2"] = nc.dram_tensor("X2", [128, NCH * TOK], BF16, kind="ExternalInput").ap()
    with ExitStack() as ctx:
        p = Prog(nc, ctx)
        block = ctx.enter_context(nc.Block())
        g = emit_globals(p, nc, ctx, gdr, with_masks=False)
        emit_C0(p, nc, ctx, g, dr)
        p.finalize(block)
    return nc


def permute_qr_cols(W):
    W = np.asarray(W)
    n = W.shape[-1] // 64
    Wr = W.reshape(W.shape[0], n, 32, 2)
    out = np.concatenate([Wr[:, :, :, 0], Wr[:, :, :, 1]], axis=2)
    return np.ascontiguousarray(out.reshape(W.shape[0], n * 64))


def stage_C0(inp, x2_own_list):
    wckv = stage_lhsT_tiles(np.asarray(inp["L1_attn_w_ckv"]))
    wcq = stage_lhsT_tiles(np.asarray(inp["L1_attn_w_cq"]))
    wkr_p = permute_qr_cols(np.asarray(inp["L1_attn_w_kr"]))
    wkr = np.ascontiguousarray(
        wkr_p.reshape(NCH, 128, 64).transpose(1, 0, 2).reshape(128, NCH * 64)).astype(np.float32)
    cst = stage_globals()
    maps = []
    for c in range(NCORE):
        maps.append(dict(X2=x2_own_list[c], WCKV=wckv, WCQ=wcq, WKR=wkr, **cst))
    return maps


SCALE_L1 = 192.0 ** -0.5

def emit_C1(p, nc, ctx, g, dr, ckv_src=None, cq_src=None, kr_src=None,
             o_store=None, lat_gate=None, tag="c1"):
    WKH, WVH, WQH, WQRH = dr["WKH"], dr["WVH"], dr["WQH"], dr["WQRH"]
    COS64, SIN64, PIT64, O = dr["COS64"], dr["SIN64"], dr["PIT64"], dr["O"]
    if ckv_src is None:
        ckv_src = lambda js: dr["CKVF"][:].rearrange("p (a t) -> p a t", a=4)[:, :, js * 512:(js + 1) * 512]
    if cq_src is None:
        cq_src = lambda js: dr["CQF"][:].rearrange("p (a t) -> p a t", a=4)[:, :, js * 512:(js + 1) * 512]
    if kr_src is None:
        kr_src = lambda js: dr["KRF"][0:64, js * 512:(js + 1) * 512]
    if True:
        sb = lambda n, s, d=BF16: ctx.enter_context(nc.sbuf_tensor(tag + n, s, d))

        ckvs = [sb(f"ckvs{i}", [128, 4 * 512]) for i in range(2)]
        cqs = [sb(f"cqs{i}", [128, 4 * 512]) for i in range(2)]
        wfh = sb("wfh", [128, 4 * (256 * 3 + 128)], F32)
        wbh = sb("wbh", [128, 4 * (256 * 3 + 128)])
        krr = sb("krr", [64, T])
        krs = [sb(f"krs{i}", [64, 512]) for i in range(2)]
        cos64 = [sb(f"cos64_{i}", [64, 512]) for i in range(2)]
        sin64 = [sb(f"sin64_{i}", [64, 512]) for i in range(2)]
        pit64 = sb("pit64", [64, 64])
        idt, masks, one1, onec, csts = g["idt"], g["masks"], g["one1"], g["onec"], g["csts"]
        kc = sb("kc", [128, T])
        vt = sb("vt", [128, T])
        qc = sb("qc", [128, T])
        qr = sb("qr", [64, T])
        obuf = qc
        praw = [sb(f"praw{i}", [128, 512]) for i in range(2)]
        rp1 = [sb(f"rp1{i}", [128, 512]) for i in range(2)]
        rp2 = [sb(f"rp2{i}", [128, 512]) for i in range(2)]
        pt = [sb(f"pt{i}", [128, 1024]) for i in range(2)]
        srow = sb("srow", [1, 512], F32)
        rrow = sb("rrow", [1, 512], F32)
        rhl = sb("rhl", [1, 2 * 512])
        rb = sb("rb", [128, 512], F32)
        acc = sb("acc", [128, 1024], F32)
        accbf = sb("accbf", [128, 1024])
        S2 = [ctx.enter_context(nc.psum_tensor(f"{tag}S{i}", [128, 1024], F32)) for i in range(2)]
        POB = [ctx.enter_context(nc.psum_tensor(f"{tag}P{i}", [128, 512], F32)) for i in range(2)]
        BCB = ctx.enter_context(nc.psum_tensor(tag + "C", [128, 512], F32))
        prt = ctx.enter_context(nc.psum_tensor(tag + "prt", [128, 512], BF16))
        BNK = [S2[0][:, 0:512], S2[0][:, 512:1024], S2[1][:, 0:512], S2[1][:, 512:1024],
               POB[0][:], POB[1][:], BCB[:]]

        s_ld = p.sem("ld")
        s_x1 = [p.sem("x10"), p.sem("x11")]
        s_x2 = [p.sem("x20"), p.sem("x21")]
        s_kr = [p.sem("kr0"), p.sem("kr1")]
        s_cv = p.sem("cv")
        s_mm = p.sem("mm")
        s_ev = p.sem("ev")
        s_exp = p.sem("exp")
        s_msk = p.sem("msk")
        s_nrm = p.sem("nrm")
        s_st = p.sem("st")

        WCOLS = 256 * 3 + 128
        for dst, srcap in ((wfh[:].rearrange("p (a c) -> p a c", a=4, c=WCOLS)[:, :, 0:256],
                            WKH.rearrange("a p c -> p a c")),
                           (wfh[:].rearrange("p (a c) -> p a c", a=4, c=WCOLS)[:, :, 256:512],
                            WVH.rearrange("a p c -> p a c")),
                           (wfh[:].rearrange("p (a c) -> p a c", a=4, c=WCOLS)[:, :, 512:768],
                            WQH.rearrange("a p c -> p a c")),
                           (wfh[:].rearrange("p (a c) -> p a c", a=4, c=WCOLS)[:, :, 768:896],
                            WQRH.rearrange("a p c -> p a c")),
                           (pit64[:], PIT64[:, :])):
            p.dma("sync", dst, srcap, s_ld)
        ld0 = s_ld.at()
        gld = g["ld"]
        p.emit("scalar", lambda e: e.copy(wbh[:, 0:1792], wfh[:, 0:1792]),
               wait=[ld0], inc=[(s_cv, 1)])
        p.emit("gpsimd", lambda e: e.tensor_copy(wbh[:, 1792:3584], wfh[:, 1792:3584]),
               wait=[ld0], inc=[(s_cv, 1)])
        cv = s_cv.at()

        def wslice(ck, base, lo, hi):
            return wbh[:, ck * WCOLS + base + lo: ck * WCOLS + base + hi]

        consumed_ckv = {}
        consumed_cq = {}
        consumed_kr = {}
        bank_free = {}
        s_acc = p.sem("acc")
        sfree = {0: 0, 1: 0}
        ptcons = {0: 0, 1: 0}
        acccons = {0: 0, 1: 0}
        pofree = {0: None, 1: None}
        bc_free = None
        psm = BCB[0:1, 0:512]
        nslab = [0]

        for h in range(2):
            def load_slab(js, h=h):
                if js >= 16:
                    return
                k = h * 16 + js
                b = k % 2
                gate = lat_gate(js) if lat_gate is not None else []
                wait = ([(s_mm, consumed_ckv[k - 2])] if k >= 2 else []) + gate
                p.dma("sync", ckvs[b][:].rearrange("p (a c) -> p a c", a=4),
                      ckv_src(js), s_x1[b], wait=wait)
                wait = ([(s_mm, consumed_cq[k - 2])] if k >= 2 else []) + gate
                p.dma("sync", cqs[b][:].rearrange("p (a c) -> p a c", a=4),
                      cq_src(js), s_x2[b], wait=wait)
                wait = ([(s_ev, consumed_kr[k - 2])] if k >= 2 else []) + gate
                p.dma("gpsimd", krs[b][0:64, :], kr_src(js), s_kr[b], wait=wait)
                p.dma("gpsimd", cos64[b][0:64, :], COS64[0:64, js * 512:(js + 1) * 512], s_kr[b], wait=wait)
                p.dma("gpsimd", sin64[b][0:64, :], SIN64[0:64, js * 512:(js + 1) * 512], s_kr[b], wait=wait)

            load_slab(0)
            load_slab(1)

            for js in range(16):
                k = h * 16 + js
                b = k % 2
                x1v = (s_x1[b], (k // 2 + 1) * 16)
                x2v = (s_x2[b], (k // 2 + 1) * 16)
                krv = (s_kr[b], (k // 2 + 1) * 48)
                t0 = js * 512
                if h == 0:
                    w = [cv, krv, s_ev.at()]
                    if "rh" in bank_free:
                        w.append((s_ev, bank_free["rh"]))
                    p.emit("tensor", lambda e, b=b: e.matmul(BNK[4][0:64, 0:512], pit64[:], krs[b][0:64, :],
                                                             start=True, stop=True, skip_group_check=True),
                           wait=w, inc=[(s_mm, 1)])
                    rhv = s_mm.at()
                    p.emit("gpsimd", lambda e, b=b: e.tensor_mul(rp1[0][0:64, :], krs[b][0:64, :], cos64[b][0:64, :]),
                           wait=[cv, krv, s_ev.at()], inc=[(s_ev, 1)])
                    p.emit("vector", lambda e, b=b: e.tensor_mul(rp2[0][0:64, :], BNK[4][0:64, 0:512], sin64[b][0:64, :]),
                           wait=[rhv, s_ev.at()], inc=[(s_ev, 1)])
                    bank_free["rh"] = s_ev.n
                    p.emit("vector", lambda e, t0=t0: e.tensor_add(krr[0:64, t0:t0 + 512], rp1[0][0:64, :], rp2[0][0:64, :]),
                           wait=[s_ev.at()], inc=[(s_ev, 1)])

                bk_ = (2 * js) % 4
                w = [cv, x1v, ld0, gld]
                if ("p", bk_) in bank_free:
                    w.append((s_ev, bank_free[("p", bk_)]))
                pairs = [(wslice(ck, 0, h * 128, (h + 1) * 128), ckvs[b][:, ck * 512:(ck + 1) * 512])
                         for ck in range(4)]
                emit_mm_acc(p, BNK[bk_], pairs, wait=w, inc=[(s_mm, 1)])
                mmv = s_mm.at()
                p.emit("vector", lambda e, t0=t0, bk_=bk_:
                       e.tensor_copy(kc[:, t0:t0 + 512], BNK[bk_]),
                       wait=[mmv, s_ev.at()], inc=[(s_ev, 1)])
                bank_free[("p", bk_)] = s_ev.n
                bv_ = (2 * js + 1) % 4
                w = [cv, x1v]
                if ("p", bv_) in bank_free:
                    w.append((s_ev, bank_free[("p", bv_)]))
                pairs = [(wslice(ck, 256, h * 128, (h + 1) * 128), ckvs[b][:, ck * 512:(ck + 1) * 512])
                         for ck in range(4)]
                emit_mm_acc(p, BNK[bv_], pairs, wait=w, inc=[(s_mm, 1)])
                mmv = s_mm.at()
                p.emit("scalar", lambda e, b=b, bv_=bv_: e.copy(praw[b][:], BNK[bv_]),
                       wait=[mmv, s_ev.at()], inc=[(s_ev, 1)])
                bank_free[("p", bv_)] = s_ev.n
                for jt in range(4):
                    w = [s_ev.at()]
                    if "prt" in bank_free:
                        w.append((s_ev, bank_free["prt"]))
                    p.emit("tensor",
                           lambda e, jt=jt, b=b: e.transpose(prt[:, 0:128], praw[b][:, jt * 128:(jt + 1) * 128], idt[:]),
                           wait=w, inc=[(s_mm, 1)])
                    mmt = s_mm.at()
                    p.emit("vector",
                           lambda e, jt=jt, t0=t0: e.tensor_copy(vt[:, t0 + jt * 128: t0 + (jt + 1) * 128], prt[:, 0:128]),
                           wait=[mmt, s_ev.at()], inc=[(s_ev, 1)])
                    bank_free["prt"] = s_ev.n
                bq_ = (2 * js) % 4
                w = [cv, x2v]
                if ("p", bq_) in bank_free:
                    w.append((s_ev, bank_free[("p", bq_)]))
                pairs = [(wslice(ck, 512, h * 128, (h + 1) * 128), cqs[b][:, ck * 512:(ck + 1) * 512])
                         for ck in range(4)]
                emit_mm_acc(p, BNK[bq_], pairs, wait=w, inc=[(s_mm, 1)])
                mmv = s_mm.at()
                p.emit("vector", lambda e, t0=t0, bq_=bq_:
                       e.tensor_copy(qc[:, t0:t0 + 512], BNK[bq_]),
                       wait=[mmv, s_ev.at()], inc=[(s_ev, 1)])
                bank_free[("p", bq_)] = s_ev.n
                consumed_ckv[k] = s_mm.n
                bqr = (2 * js + 1) % 4
                w = [cv, x2v]
                if ("p", bqr) in bank_free:
                    w.append((s_ev, bank_free[("p", bqr)]))
                pairs = [(wslice(ck, 768, h * 64, (h + 1) * 64), cqs[b][:, ck * 512:(ck + 1) * 512])
                         for ck in range(4)]
                emit_mm_acc(p, BNK[bqr][0:64, 0:512], pairs, wait=w, inc=[(s_mm, 1)])
                mmv = s_mm.at()
                consumed_cq[k] = s_mm.n
                p.emit("scalar", lambda e, b=b, bqr=bqr: e.copy(praw[b][0:64, :], BNK[bqr][0:64, 0:512]),
                       wait=[mmv, s_ev.at()], inc=[(s_ev, 1)])
                bank_free[("p", bqr)] = s_ev.n
                w = [s_ev.at()]
                if "rh" in bank_free:
                    w.append((s_ev, bank_free["rh"]))
                p.emit("tensor", lambda e, b=b: e.matmul(BNK[4][0:64, 0:512], pit64[:], praw[b][0:64, :],
                                                         start=True, stop=True, skip_group_check=True),
                       wait=w, inc=[(s_mm, 1)])
                rhv = s_mm.at()
                p.emit("gpsimd", lambda e, b=b: e.tensor_mul(rp1[1][0:64, :], praw[b][0:64, :], cos64[b][0:64, :]),
                       wait=[krv, s_ev.at()], inc=[(s_ev, 1)])
                p.emit("vector", lambda e, b=b: e.tensor_mul(rp2[1][0:64, :], BNK[4][0:64, 0:512], sin64[b][0:64, :]),
                       wait=[rhv, s_ev.at()], inc=[(s_ev, 1)])
                bank_free["rh"] = s_ev.n
                p.emit("vector", lambda e, t0=t0: e.tensor_add(qr[0:64, t0:t0 + 512], rp1[1][0:64, :], rp2[1][0:64, :]),
                       wait=[s_ev.at()], inc=[(s_ev, 1)])
                consumed_kr[k] = s_ev.n
                load_slab(js + 2)

            barrier = s_ev.at()
            barrier_g = gld
            for gq in range(16):
                pi = h * 16 + gq
                jn = 4 * (gq + 1)
                nu = jn // 2
                po = POB[pi % 2][:]
                qcap = qc[:, gq * 512:(gq + 1) * 512]
                qrap = qr[0:64, gq * 512:(gq + 1) * 512]

                def emit_S(u, qcap=qcap, qrap=qrap, barrier=barrier):
                    w = [barrier, barrier_g, (s_exp, sfree[u % 2])]
                    for hf in range(2):
                        jt = 2 * u + hf
                        dst = S2[u % 2][:, hf * 512:(hf + 1) * 512]
                        p.emit("tensor",
                               lambda e, jt=jt, dst=dst, qcap=qcap:
                                   e.matmul(dst, kc[:, jt * 128:(jt + 1) * 128],
                                            qcap, start=True, stop=False, skip_group_check=True),
                               wait=(w if hf == 0 else ()), inc=[(s_mm, 1)])
                        p.emit("tensor",
                               lambda e, jt=jt, dst=dst, qrap=qrap:
                                   e.matmul(dst, krr[0:64, jt * 128:(jt + 1) * 128],
                                            qrap, start=False, stop=True, skip_group_check=True),
                               inc=[(s_mm, 1)])
                    return s_mm.n

                sv = emit_S(0)
                for u in range(nu):
                    i0 = 2 * u - 4 * gq
                    if i0 >= 0:
                        p.emit("vector",
                               lambda e, u=u, i0=i0: e.tensor_add(S2[u % 2][:], S2[u % 2][:],
                                                                  masks[:, i0 * 512:(i0 + 2) * 512]),
                               wait=[(s_mm, sv), s_msk.at()], inc=[(s_msk, 1)])
                        ewait = [(s_msk, s_msk.n)]
                    else:
                        ewait = [(s_mm, sv)]
                    p.emit("scalar",
                           lambda e, u=u: e.activation(pt[u % 2][:], S2[u % 2][:], AF.Exp,
                                                       bias=float(EXP_SHIFT), scale=float(SCALE_L1)),
                           wait=ewait + [(s_mm, ptcons[u % 2]), (s_acc, acccons[u % 2])],
                           inc=[(s_exp, 1)])
                    sfree[u % 2] = s_exp.n
                    ev = s_exp.n
                    if u + 1 < nu:
                        sv = emit_S(u + 1)
                    w = [(s_exp, ev)]
                    if u == 0 and pofree[pi % 2] is not None:
                        w.append((s_nrm, pofree[pi % 2]))
                    for hf in range(2):
                        jt = 2 * u + hf
                        p.emit("tensor",
                               lambda e, jt=jt, u=u, hf=hf, po=po: e.matmul(
                                   po, vt[:, jt * 128:(jt + 1) * 128],
                                   pt[u % 2][:, hf * 512:(hf + 1) * 512],
                                   start=(jt == 0), stop=(jt == jn - 1), skip_group_check=True),
                               wait=(w if hf == 0 else ()), inc=[(s_mm, 1)])
                    ptcons[u % 2] = s_mm.n
                    if u == 0:
                        p.emit("vector", lambda e, u=u: e.tensor_copy(acc[:], pt[u % 2][:]),
                               wait=[(s_exp, ev), s_acc.at()], inc=[(s_acc, 1)])
                    else:
                        p.emit("vector", lambda e, u=u: e.tensor_add(acc[:], acc[:], pt[u % 2][:]),
                               wait=[(s_exp, ev), s_acc.at()], inc=[(s_acc, 1)])
                    acccons[u % 2] = s_acc.n

                p.emit("vector", lambda e: e.tensor_copy(accbf[:], acc[:]),
                       wait=[s_acc.at()], inc=[(s_acc, 1)])
                w = [s_acc.at()]
                if bc_free is not None:
                    w.append((s_nrm, bc_free))
                emit_mm_acc(p, psm, [(onec[:], accbf[:, 0:512]), (onec[:], accbf[:, 512:1024])],
                            wait=w, inc=[(s_mm, 1)])
                nv = s_mm.at()
                p.emit("vector", lambda e, psm=psm: e.tensor_copy(srow[:], psm),
                       wait=[nv, s_nrm.at()], inc=[(s_nrm, 1)])
                p.emit("vector", lambda e: e.reciprocal(rrow[:], srow[:]),
                       wait=[s_nrm.at()], inc=[(s_nrm, 1)])
                p.emit("vector", lambda e: e.tensor_copy(rhl[:, 0:512], rrow[:]),
                       wait=[s_nrm.at()], inc=[(s_nrm, 1)])
                p.emit("vector", lambda e: e.tensor_sub(srow[:], rrow[:], rhl[:, 0:512]),
                       wait=[s_nrm.at()], inc=[(s_nrm, 1)])
                p.emit("vector", lambda e: e.tensor_copy(rhl[:, 512:1024], srow[:]),
                       wait=[s_nrm.at()], inc=[(s_nrm, 1)])
                emit_mm_acc(p, BCB[:], [(one1[:], rhl[:, 0:512]), (one1[:], rhl[:, 512:1024])],
                            wait=[s_nrm.at()], inc=[(s_mm, 1)])
                bcv = s_mm.at()
                p.emit("vector", lambda e: e.tensor_copy(rb[:], BCB[:]),
                       wait=[bcv, s_nrm.at()], inc=[(s_nrm, 1)])
                bc_free = s_nrm.n
                dst = obuf[:, gq * 512:(gq + 1) * 512]
                p.emit("vector",
                       lambda e, dst=dst, po=po: e.scalar_tensor_tensor(dst, po, 1.0, rb[:],
                                                                        ALU.mult, ALU.mult),
                       wait=[s_nrm.at()], inc=[(s_nrm, 1)])
                pofree[pi % 2] = s_nrm.n
                if o_store is not None and h == 1 and gq % 2 == 1:
                    k = (gq - 1) // 2
                    o_store(h, obuf[:, k * TOK:(k + 1) * TOK], [s_nrm.at()], chunk=k)

            if o_store is not None:
                if h == 0:
                    stw = o_store(h, obuf[:], [s_nrm.at()])
            else:
                p.dma("gpsimd", O[:, h * T:(h + 1) * T], obuf[:], s_st, wait=[s_nrm.at()])
                stw = (s_st, s_st.n)
            if h == 0:
                stv0 = stw
                consumed_ckv[15] = s_mm.n
                consumed_cq[15] = s_mm.n
                consumed_kr[15] = s_ev.n
                p.emit("sync", lambda e: e.nop(), wait=[stv0])
                p.emit("tensor", lambda e: e.nop(), wait=[stv0])
                p.emit("vector", lambda e: e.nop(), wait=[stv0])
                p.emit("scalar", lambda e: e.nop(), wait=[stv0])
                p.emit("gpsimd", lambda e: e.nop(), wait=[stv0])

        p.emit("gpsimd", lambda e: e.nop(), wait=[s_st.at()])
    return s_st.at()


def declare_C1(nc, o_ap):
    return dict(
        WKH=nc.dram_tensor("WKH", [4, 128, 256], F32, kind="ExternalInput").ap(),
        WVH=nc.dram_tensor("WVH", [4, 128, 256], F32, kind="ExternalInput").ap(),
        WQH=nc.dram_tensor("WQH", [4, 128, 256], F32, kind="ExternalInput").ap(),
        WQRH=nc.dram_tensor("WQRH", [4, 128, 128], F32, kind="ExternalInput").ap(),
        COS64=nc.dram_tensor("COS64", [64, T], BF16, kind="ExternalInput").ap(),
        SIN64=nc.dram_tensor("SIN64", [64, T], BF16, kind="ExternalInput").ap(),
        PIT64=nc.dram_tensor("PIT64", [64, 64], BF16, kind="ExternalInput").ap(),
        O=o_ap,
    )


def build_C1():
    nc = bass.Bass("TRN2", target_bir_lowering=False, debug=False)
    gdr = declare_globals(nc)
    dr = declare_C1(nc, nc.dram_tensor("O", [128, 2 * T], BF16, kind="ExternalOutput").ap())
    dr["CKVF"] = nc.dram_tensor("CKVF", [128, 4 * T], BF16, kind="ExternalInput").ap()
    dr["CQF"] = nc.dram_tensor("CQF", [128, 4 * T], BF16, kind="ExternalInput").ap()
    dr["KRF"] = nc.dram_tensor("KRF", [64, T], BF16, kind="ExternalInput").ap()
    with ExitStack() as ctx:
        p = Prog(nc, ctx)
        block = ctx.enter_context(nc.Block())
        g = emit_globals(p, nc, ctx, gdr)
        emit_C1(p, nc, ctx, g, dr)
        p.finalize(block)
    return nc


def stage_C1(inp, ckv_full, cq_full, kr_full):
    cst = stage_globals()
    cos, sin = rope_tables_l1()
    pit = perm_rh_T(64)
    wk_t = stage_lhsT_tiles(np.asarray(inp["L1_attn_w_k"]))
    wv_t = stage_lhsT_tiles(np.asarray(inp["L1_attn_w_v"]))
    wq_t = stage_lhsT_tiles(np.asarray(inp["L1_attn_w_q"]))
    wqr_p = permute_qr_cols(np.asarray(inp["L1_attn_w_qr"]))
    wqr_t = stage_lhsT_tiles(wqr_p)
    maps = []
    for c in range(NCORE):
        h0 = 2 * c
        def two_heads(wt):
            a = wt[h0].reshape(128, 4, 128)
            bq = wt[h0 + 1].reshape(128, 4, 128)
            out = np.stack([a, bq], axis=3)
            return np.ascontiguousarray(out.transpose(1, 0, 3, 2).reshape(4, 128, 256))
        def two_heads2(wt):
            a = wt[h0].reshape(128, 4, 128).transpose(1, 0, 2)
            bq = wt[h0 + 1].reshape(128, 4, 128).transpose(1, 0, 2)
            return np.ascontiguousarray(np.concatenate([a, bq], axis=2))
        wqr2 = np.concatenate([
            wqr_t[h0 // 2].reshape(128, 4, 128).transpose(1, 0, 2)[:, :, (h0 % 2) * 64:(h0 % 2) * 64 + 64]
            if False else None for _ in range(0)], axis=0) if False else None
        def qr_heads():
            out = np.zeros((4, 128, 128), dtype=np.float32)
            for i, h in enumerate((h0, h0 + 1)):
                oc = (h * 64) // 128
                off = (h * 64) % 128
                tile = wqr_t[oc].reshape(128, 4, 128).transpose(1, 0, 2)
                out[:, :, i * 64:(i + 1) * 64] = tile[:, :, off:off + 64]
            return np.ascontiguousarray(out)
        maps.append(dict(
            CKVF=ckv_full, CQF=cq_full, KRF=kr_full,
            WKH=two_heads2(wk_t), WVH=two_heads2(wv_t), WQH=two_heads2(wq_t),
            WQRH=qr_heads(),
            COS64=cos, SIN64=sin, PIT64=pit, **cst,
        ))
    return maps


def ckv_own_to_full(outs, nch4=4):
    full = np.empty((128, nch4, T), dtype=outs[0].dtype)
    for c in range(NCORE):
        full[:, :, c * TOK:(c + 1) * TOK] = np.asarray(outs[c]).reshape(128, nch4, TOK)
    return np.ascontiguousarray(full.reshape(128, nch4 * T))


def kr_own_to_full(outs):
    full = np.empty((64, T), dtype=outs[0].dtype)
    for c in range(NCORE):
        full[:, c * TOK:(c + 1) * TOK] = outs[c]
    return np.ascontiguousarray(full)


def emit_D(p, nc, ctx, g, dr, x_ap=None, tag="d"):
    WB, ROUT, ACAT, BSTK, EMAT, X4 = dr["WB"], dr["ROUT"], dr["ACAT"], dr["BSTK"], dr["EMAT"], dr["X4"]
    if True:
        sb = lambda n, s, d=BF16: ctx.enter_context(nc.sbuf_tensor(tag + n, s, d))
        x = sb("x", [128, NCH * TOK]) if x_ap is None else x_ap
        wf = [sb(f"wf{i}", [128, NCH * 128], F32) for i in range(2)]
        wb = [sb(f"wb{i}", [128, NCH * 128]) for i in range(3)]
        acat = sb("acat", [128, NCH * 128])
        bstk = sb("bstk", [128, NCH * 128])
        rout = sb("rout", [128, NCH * 8])
        acatf = sb("acatf", [128, NCH * 128], F32)
        bstkf = sb("bstkf", [128, NCH * 128], F32)
        routf = sb("routf", [128, NCH * 8], F32)
        emat = sb("emat", [8, 128])
        idt, one1, onec, csts = g["idt"], g["one1"], g["onec"], g["csts"]
        wfm8 = sb("wfm8", [8, TOK])
        wexp = sb("wexp", [128, TOK])
        zbuf = sb("zbuf", [128, TOK])
        wdns = sb("wdns", [128, 8], F32)
        wdb = sb("wdb", [128, 8])
        t8a = sb("t8a", [128, 8], F32)
        t8b = sb("t8b", [128, 8], F32)
        m1 = sb("m1", [128, 1], F32)
        m2 = sb("m2", [128, 1], F32)
        w1 = sb("w1", [128, 1], F32)
        w2 = sb("w2", [128, 1], F32)
        xbf = [sb(f"xbf{i}", [128, NCH * 512]) for i in range(2)]
        sq = [sb(f"sq{i}", [128, 512]) for i in range(2)]
        rows = sb("rows", [1, 3 * 512], F32)
        rhl = sb("rhl", [1, 4 * 512])
        tmp32 = sb("tmp32", [128, 512], F32)
        outsl = [sb(f"outsl{i}", [128, NCH * 512], F32) for i in range(2)]

        BNK = [ctx.enter_context(nc.psum_tensor(f"{tag}B{i}", [128, 512], F32)) for i in range(8)]
        pd = [BNK[0], BNK[1]]
        rowx = [BNK[2][0:1, 0:512], BNK[6][0:1, 0:512]]
        rowsq = [BNK[3][0:1, 0:512], BNK[7][0:1, 0:512]]
        bcr, bcb = BNK[4], BNK[5]
        prt8 = BNK[4][:].bitcast(BF16)

        s_ld = p.sem("ld"); s_wl = [p.sem("w0"), p.sem("w1")]
        s_cva = p.sem("cva"); s_cvb = p.sem("cvb"); s_cvs = p.sem("cvs")
        s_mm = p.sem("mm"); s_si = p.sem("si"); s_ep = p.sem("ep")
        s_rt = p.sem("rt"); s_row = p.sem("row"); s_bc = p.sem("bc")
        s_ap = p.sem("ap"); s_st = p.sem("st")

        if x_ap is None:
            p.dma("sync", x[:], dr["X3"][:, :], s_ld)
        p.dma("sync", emat[:], EMAT[:, :], s_ld)
        ld0 = s_ld.at()
        gld = g["ld"]
        s_lsm = p.sem("lsm")
        p.dma("sync", acatf[:], ACAT[:, :], s_lsm)
        p.dma("sync", bstkf[:].rearrange("p (a c) -> p a c", a=NCH),
              BSTK.rearrange("a p c -> p a c"), s_lsm)
        p.dma("sync", routf[:].rearrange("p (a c) -> p a c", a=NCH),
              ROUT.rearrange("a p c -> p a c"), s_lsm)
        lsm = (s_lsm, 48)
        p.emit("scalar", lambda e: e.copy(acat[:], acatf[:]), wait=[lsm], inc=[(s_cvs, 1)])
        p.emit("gpsimd", lambda e: e.tensor_copy(bstk[:], bstkf[:]), wait=[lsm], inc=[(s_cvs, 1)])
        p.emit("vector", lambda e: e.tensor_copy(rout[:], routf[:]), wait=[lsm], inc=[(s_cvs, 1)])
        cvs = s_cvs.at()

        consumed_mm = {}
        emitted_w = set()
        def record_wload(wi):
            if wi >= NCH or wi in emitted_w:
                return
            emitted_w.add(wi)
            fb = wi % 2
            buf = wi % 3
            wait = []
            if wi >= 2:
                wait.append((s_cva, wi - 1))
                wait.append((s_cvb, wi - 1))
            if wi >= 3:
                wait.append((s_mm, consumed_mm[wi - 3]))
            p.dma("sync", wf[fb][:], WB[wi, :, :], s_wl[fb], wait=wait)
            wv = (s_wl[fb], (wi // 2 + 1) * 16)
            cw = [wv] + ([(s_mm, consumed_mm[wi - 3])] if wi >= 3 else [])
            p.emit("scalar", lambda e, b=buf, f=fb: e.copy(wb[b][:, 0:1024], wf[f][:, 0:1024]),
                   wait=cw, inc=[(s_cva, 1)])
            p.emit("gpsimd", lambda e, b=buf, f=fb: e.tensor_copy(wb[b][:, 1024:2048], wf[f][:, 1024:2048]),
                   wait=cw, inc=[(s_cvb, 1)])

        for wi in range(3):
            record_wload(wi)

        bank_rt = {}
        for qt in range(8):
            t0 = qt * 128
            bnkid = 2 + qt % 2
            bnk = BNK[bnkid]
            w = [cvs, ld0, gld]
            if bnkid in bank_rt:
                w.append((s_rt, bank_rt[bnkid]))
            pairs = [(x[:, hc * TOK + t0: hc * TOK + t0 + 128], rout[:, hc * 8:(hc + 1) * 8])
                     for hc in range(NCH)]
            emit_mm_acc(p, bnk[0:128, 0:8], pairs, wait=w, inc=[(s_mm, 1)])
            mmv = s_mm.at()
            def rstep(fn, extra=()):
                p.emit("vector", fn, wait=[s_rt.at()] + list(extra), inc=[(s_rt, 1)])
            rstep(lambda e, bnk=bnk: e.reduce_max(m1[:], bnk[0:128, 0:8], axis=AX.X), [mmv])
            rstep(lambda e, bnk=bnk: e.tensor_scalar(t8a[:], bnk[0:128, 0:8], m1[:], None, ALU.is_equal))
            rstep(lambda e, bnk=bnk: e.scalar_tensor_tensor(t8b[:], t8a[:], -30000.0, bnk[0:128, 0:8], ALU.mult, ALU.add))
            bank_rt[bnkid] = s_rt.n
            rstep(lambda e: e.reduce_max(m2[:], t8b[:], axis=AX.X))
            rstep(lambda e: e.tensor_scalar(t8b[:], t8b[:], m2[:], None, ALU.is_equal))
            rstep(lambda e: e.tensor_sub(w1[:], m2[:], m1[:]))
            p.emit("scalar", lambda e: e.activation(w2[:], w1[:], AF.Exp),
                   wait=[s_rt.at()], inc=[(s_rt, 1)])
            rstep(lambda e: e.tensor_scalar_add(w2[:], w2[:], 1.0))
            rstep(lambda e: e.reciprocal(w1[:], w2[:]))
            rstep(lambda e: e.tensor_scalar(w2[:], w1[:], -1.0, 1.0, ALU.mult, ALU.add))
            rstep(lambda e: e.tensor_scalar_mul(t8a[:], t8a[:], w1[:]))
            rstep(lambda e: e.scalar_tensor_tensor(wdns[:], t8b[:], w2[:], t8a[:], ALU.mult, ALU.add))
            rstep(lambda e: e.tensor_copy(wdb[:], wdns[:]))
            w = [s_rt.at()]
            if "prt8" in bank_rt:
                w.append((s_rt, bank_rt["prt8"]))
            p.emit("tensor", lambda e: e.transpose(prt8[0:8, 0:128], wdb[:], idt[:]),
                   wait=w, inc=[(s_mm, 1)])
            mmt = s_mm.at()
            p.emit("vector", lambda e, t0=t0: e.tensor_copy(wfm8[0:8, t0:t0 + 128], prt8[0:8, 0:128]),
                   wait=[mmt, s_rt.at()], inc=[(s_rt, 1)])
            bank_rt["prt8"] = s_rt.n
        rt_done = s_rt.at()

        for sl in range(2):
            t0 = sl * 512
            bnkid = 2 + sl
            w = [rt_done, (s_rt, bank_rt[bnkid])] if bnkid in bank_rt else [rt_done]
            p.emit("tensor", lambda e, sl=sl, t0=t0, bnkid=bnkid:
                   e.matmul(BNK[bnkid][:], emat[:], wfm8[0:8, t0:t0 + 512],
                            start=True, stop=True, skip_group_check=True),
                   wait=w, inc=[(s_mm, 1)])
            mmv = s_mm.at()
            p.emit("vector", lambda e, sl=sl, t0=t0, bnkid=bnkid:
                   e.tensor_copy(wexp[:, t0:t0 + 512], BNK[bnkid][:]),
                   wait=[mmv, s_rt.at()], inc=[(s_rt, 1)])
        for sl in range(2):
            t0 = sl * 512
            bnkid = 6 + sl
            pairs = [(acat[:, hc * 128:(hc + 1) * 128], x[:, hc * TOK + t0: hc * TOK + t0 + 512])
                     for hc in range(NCH)]
            emit_mm_acc(p, BNK[bnkid][:], pairs, wait=[cvs, s_rt.at()], inc=[(s_mm, 1)])
            mmv = s_mm.at()
            p.emit("vector", lambda e, sl=sl, t0=t0, bnkid=bnkid:
                   e.scalar_tensor_tensor(zbuf[:, t0:t0 + 512], BNK[bnkid][:], 2.0,
                                          wexp[:, t0:t0 + 512], ALU.mult, ALU.mult),
                   wait=[mmv, s_rt.at()], inc=[(s_rt, 1)])
        z_done = s_rt.at()

        dg_mm = {}
        ep_after_stt = {}
        si_after_sq = {}
        def emit_rows_d(ec):
            for sl in range(2):
                emit_mm_acc(p, rowx[sl], [(onec[:], xbf[sl][:, ec * 512:(ec + 1) * 512])],
                            wait=[ep_after_stt[(ec, sl)]], inc=[(s_mm, 1)],
                            start=(ec == 0), stop=(ec == NCH - 1))
                emit_mm_acc(p, rowsq[sl], [(onec[:], sq[sl][:])],
                            wait=[si_after_sq[(ec, sl)]], inc=[(s_mm, 1)],
                            start=(ec == 0), stop=(ec == NCH - 1))
                dg_mm[("rsq", ec, sl)] = s_mm.n
        for ec in range(NCH):
            for sl in range(2):
                t0 = sl * 512
                wait = [(s_cva, ec + 1), (s_cvb, ec + 1), z_done]
                if ec >= 1:
                    wait.append((s_ep, 2 * (ec - 1) + sl + 1))
                buf = ec % 3
                pairs = [(wb[buf][:, hc * 128:(hc + 1) * 128],
                          x[:, hc * TOK + t0: hc * TOK + t0 + 512]) for hc in range(NCH)] + \
                        [(bstk[:, ec * 128:(ec + 1) * 128], zbuf[:, t0:t0 + 512])]
                emit_mm_acc(p, pd[sl][:], pairs, wait=wait, inc=[(s_mm, 1)])
                dg_mm[(ec, sl)] = s_mm.n
                if sl == 1:
                    consumed_mm[ec] = s_mm.n
            record_wload(ec + 3)
            if ec >= 1:
                emit_rows_d(ec - 1)
            for sl in range(2):
                mmv = (s_mm, dg_mm[(ec, sl)])
                dst = xbf[sl][:, ec * 512:(ec + 1) * 512]
                xres = x[:, ec * TOK + sl * 512: ec * TOK + sl * 512 + 512]
                p.emit("vector",
                       lambda e, sl=sl, dst=dst, xres=xres: e.tensor_add(dst, pd[sl][:], xres),
                       wait=[mmv], inc=[(s_ep, 1)])
                ep_after_stt[(ec, sl)] = s_ep.at()
                sqw = [ep_after_stt[(ec, sl)]]
                if ec >= 1:
                    sqw.append((s_mm, dg_mm[("rsq", ec - 1, sl)]))
                p.emit("scalar",
                       lambda e, sl=sl, ec=ec: e.square(sq[sl][:], xbf[sl][:, ec * 512:(ec + 1) * 512]),
                       wait=sqw, inc=[(s_si, 1)])
                si_after_sq[(ec, sl)] = s_si.at()

        emit_rows_d(NCH - 1)

        for sl in range(2):
            wait_rows = [s_mm.at()]
            wait_bc_extra = [s_ap.at()] if sl == 1 else []
            apv = emit_ln_apply(p,
                          xbf_tile=lambda ec, sl=sl: xbf[sl][:, ec * 512:(ec + 1) * 512],
                          put_out=lambda ec, sl=sl: outsl[sl][:, ec * 512:(ec + 1) * 512],
                          rowx=rowx[sl], rowsq=rowsq[sl], bcr=bcr[:], bcb=bcb[:],
                          rows=rows, rhl=rhl, one1=one1[:], tmp32=tmp32,
                          s_row=s_row, s_bc=s_bc, s_ap=s_ap,
                          wait_rows=wait_rows, wait_bc_extra=wait_bc_extra)
            dst = X4[:].rearrange("p (c t) -> p c t", t=TOK)[:, :, sl * 512:(sl + 1) * 512]
            src = outsl[sl][:].rearrange("p (c t) -> p c t", t=512)
            p.dma("gpsimd", dst, src, s_st, wait=[s_ap.at()])

        p.emit("gpsimd", lambda e: e.nop(), wait=[s_st.at()])
    return s_st.at()


def declare_D(nc, x4_ap):
    return dict(
        WB=nc.dram_tensor("WB", [NCH, 128, NCH * 128], F32, kind="ExternalInput").ap(),
        ROUT=nc.dram_tensor("ROUT", [NCH, 128, 8], F32, kind="ExternalInput").ap(),
        ACAT=nc.dram_tensor("ACAT", [128, NCH * 128], F32, kind="ExternalInput").ap(),
        BSTK=nc.dram_tensor("BSTK", [NCH, 128, 128], F32, kind="ExternalInput").ap(),
        EMAT=nc.dram_tensor("EMAT", [8, 128], BF16, kind="ExternalInput").ap(),
        X4=x4_ap,
    )


def build_D():
    nc = bass.Bass("TRN2", target_bir_lowering=False, debug=False)
    gdr = declare_globals(nc)
    dr = declare_D(nc, nc.dram_tensor("X4", [128, NCH * TOK], F32, kind="ExternalOutput").ap())
    dr["X3"] = nc.dram_tensor("X3", [128, NCH * TOK], BF16, kind="ExternalInput").ap()
    with ExitStack() as ctx:
        p = Prog(nc, ctx)
        block = ctx.enter_context(nc.Block())
        g = emit_globals(p, nc, ctx, gdr)
        emit_D(p, nc, ctx, g, dr)
        p.finalize(block)
    return nc


def stage_D(inp, x3_own_list):
    cst = stage_globals()
    wbt = stage_lhsT_tiles(np.asarray(inp["L1_ffn_weight"]).T)
    routT = np.ascontiguousarray(np.asarray(inp["L1_ffn_router_w"]).T)
    rout = np.ascontiguousarray(routT.reshape(NCH, 128, 8))
    A = np.asarray(inp["L1_ffn_lora_A"])
    acatT = np.ascontiguousarray(A.reshape(128, 2048).T)
    acat = stage_lhsT_tiles(acatT)[0]
    Bm = np.asarray(inp["L1_ffn_lora_B"])
    bstack = np.ascontiguousarray(Bm.transpose(0, 2, 1).reshape(128, 2048))
    bstk = np.ascontiguousarray(
        bstack.reshape(128, NCH, 128).transpose(1, 0, 2))
    ematv = np.zeros((8, 128), dtype=BF)
    for e_ in range(8):
        ematv[e_, e_ * 16:(e_ + 1) * 16] = 1
    maps = []
    for c in range(NCORE):
        maps.append(dict(X3=x3_own_list[c], WB=wbt, ROUT=rout, ACAT=acat, BSTK=bstk,
                         EMAT=ematv, **cst))
    return maps


def stage_C2(inp, o_full_fm, x_own_list):
    cst = stage_globals()
    wo_t = np.ascontiguousarray(stage_lhsT_tiles(np.asarray(inp["L1_attn_w_o"])).astype(np.float32))
    maps = []
    for c in range(NCORE):
        maps.append(dict(
            OF=fm_full_slice_own(o_full_fm, c),
            XO=x_own_list[c],
            WO=wo_t,
            **cst,
        ))
    return maps


def kernel_unfused(**inputs):
    inp = {k: np.asarray(v) for k, v in inputs.items()}
    cores = list(range(NCORE))

    nc = build_A1()
    res = run_bass_kernel_spmd(nc, stage_A1(inp), core_ids=cores)
    o_full = o_heads_to_full_fm([r["O"] for r in res.results])

    x = inp["L0_attn_x"][0]
    XF = np.ascontiguousarray(fm(x).reshape(NCH, 128, T).transpose(1, 0, 2).reshape(128, NCH * T))
    x_own = [fm_full_slice_own(XF, c) for c in cores]
    nc = build_A2()
    res = run_bass_kernel_spmd(nc, stage_A2(inp, o_full, x_own), core_ids=cores)
    x1_own = [np.asarray(r["X1"]) for r in res.results]

    nc = build_B()
    res = run_bass_kernel_spmd(nc, stage_B(inp, x1_own), core_ids=cores)
    x2_own = [np.asarray(r["XOUT"]) for r in res.results]

    nc = build_C0()
    res = run_bass_kernel_spmd(nc, stage_C0(inp, x2_own), core_ids=cores)
    ckv_full = ckv_own_to_full([np.asarray(r["CKV"]) for r in res.results])
    cq_full = ckv_own_to_full([np.asarray(r["CQ"]) for r in res.results])
    kr_full = kr_own_to_full([np.asarray(r["KR"]) for r in res.results])

    nc = build_C1()
    res = run_bass_kernel_spmd(nc, stage_C1(inp, ckv_full, cq_full, kr_full), core_ids=cores)
    o1_full = o_heads_to_full_fm([r["O"] for r in res.results])

    nc = build_A2(wo_f32=True)
    res = run_bass_kernel_spmd(nc, stage_C2(inp, o1_full, x2_own), core_ids=cores)
    x3_own = [np.asarray(r["X1"]) for r in res.results]

    nc = build_D()
    res = run_bass_kernel_spmd(nc, stage_D(inp, x3_own), core_ids=cores)
    x4 = np.empty((NCH, 128, T), dtype=np.float32)
    for c in cores:
        x4[:, :, c * TOK:(c + 1) * TOK] = np.asarray(
            res.results[c]["X4"]).reshape(128, NCH, TOK).transpose(1, 0, 2)
    out = x4.reshape(D, T).T.reshape(1, T, D)
    return np.ascontiguousarray(out)

import dataclasses as _dc

def build_fused():
    nc = bass.Bass("TRN2", target_bir_lowering=False, debug=False)
    gdr = declare_globals(nc)
    OL0H = [nc.dram_tensor(f"OL0H{i}", [128, T], BF16, kind="Internal").ap() for i in range(2)]
    OL0GH = [nc.dram_tensor(f"OL0GH{i}", [1024, T], BF16, kind="Internal", addr_space="Shared").ap()
             for i in range(2)]
    OH1F0 = nc.dram_tensor("OH1F0", [8, 128, TOK], BF16, kind="Internal").ap()
    OG1F0 = nc.dram_tensor("OG1F0", [8, 1024, TOK], BF16, kind="Internal", addr_space="Shared").ap()
    OH1F1 = nc.dram_tensor("OH1F1", [8, 128, TOK], BF16, kind="Internal").ap()
    OG1F1 = nc.dram_tensor("OG1F1", [8, 1024, TOK], BF16, kind="Internal", addr_space="Shared").ap()
    LAT0 = nc.dram_tensor("LAT0", [2, 128, 9 * 512], BF16, kind="Internal").ap()
    LATG = nc.dram_tensor("LATG", [2, 1024, 9 * 512], BF16, kind="Internal", addr_space="Shared").ap()
    OL1H = [nc.dram_tensor(f"OL1H{i}", [128, T], BF16, kind="Internal").ap() for i in range(2)]
    OL1GH = [nc.dram_tensor(f"OL1GH{i}", [1024, T], BF16, kind="Internal", addr_space="Shared").ap()
             for i in range(2)]

    drA1 = declare_A1(nc, None)
    WO0 = nc.dram_tensor("WO0", [NCH, 128, NCH * 128], BF16, kind="ExternalInput").ap()
    XOWN = nc.dram_tensor("XOWN", [128, NCH * TOK], BF16, kind="ExternalInput").ap()
    drB = declare_B(nc)
    drC0 = declare_C0(nc, None, None, None)
    drC1 = declare_C1(nc, None)
    WO1 = nc.dram_tensor("WO1", [NCH, 128, NCH * 128], F32, kind="ExternalInput").ap()
    X4 = nc.dram_tensor("X4", [128, NCH * TOK], F32, kind="ExternalOutput").ap()
    drD = declare_D(nc, X4)

    RG = [list(range(NCORE))]

    with ExitStack() as ctx:
        p = Prog(nc, ctx)
        block = ctx.enter_context(nc.Block())
        g = emit_globals(p, nc, ctx, gdr)
        xres = ctx.enter_context(nc.sbuf_tensor("xres", [128, NCH * TOK], BF16))

        def allgather(name, pairs, wait):
            s_cc = p.sem(name)
            for a, bb in pairs:
                p.emit("gpsimd",
                       lambda e, a=a, bb=bb: e.collective_compute(
                           "AllGather", ALU.bypass, replica_groups=RG,
                           ins=[a.opt()], outs=[bb.opt()]),
                       wait=wait, inc=[(s_cc, 1)])
            return s_cc

        rtok_cache = {}
        def o_store_gather(OH, OGH, H1F, G1F, names):
            s_os = p.sem(names + "s")
            s_cc = p.sem(names + "g")
            st = {"n": 0, "done": {}, "sem": s_cc}
            def cb(h, src_ap, wait, chunk=None):
                st["n"] += 1
                sv = 16 * st["n"]
                if chunk is None:
                    p.dma("gpsimd", OH[h][:, :], src_ap, s_os, wait=wait)
                    ins, outs = OH[h], OGH[h]
                else:
                    p.dma("gpsimd", H1F[chunk], src_ap, s_os, wait=wait)
                    ins, outs = H1F[chunk], G1F[chunk]
                p.emit("gpsimd",
                       lambda e, ins=ins, outs=outs: e.collective_compute(
                           "AllGather", ALU.bypass, replica_groups=RG,
                           ins=[ins.opt()], outs=[outs.opt()]),
                       wait=[(s_os, sv)], inc=[(s_cc, 1)])
                st["done"][h] = s_cc.n
                return (s_os, sv)
            return cb, st

        def dyn_of_loader(OGH, G1F, st):
            def loader(p, of, s_ld):
                for fc in range(NCH):
                    r0 = (fc // 2) * 128
                    hl = fc % 2
                    if hl == 0:
                        base = OGH[0][r0:r0 + 128, 0:TOK]
                        mult = TOK
                    else:
                        base = G1F[0][r0:r0 + 128, 0:TOK]
                        mult = 1024 * TOK
                    def fn(e, base=base, fc=fc, of=of, mult=mult):
                        key = (id(e), mult)
                        if key not in rtok_cache:
                            rtok_cache[key] = e.snap(e.partition_id() * mult)
                        dyn = _dc.replace(base, offset=base.offset + rtok_cache[key])
                        return e.dma_start(out=of[:, fc * TOK:(fc + 1) * TOK], in_=dyn)
                    p.emit("sync", fn, wait=[(st["sem"], st["done"][hl])], inc=[(s_ld, 16)])
            return loader

        s_xw = p.sem("xw")
        def xres_writer(sl, slab, wait, ec0=0, ec1=NCH):
            dst = xres[:].rearrange("p (c t) -> p c t", t=TOK)[:, ec0:ec1, sl * 512:(sl + 1) * 512]
            srcap = slab[:].rearrange("p (c t) -> p c t", t=512)[:, ec0:ec1, :]
            p.dma("gpsimd", dst, srcap, s_xw, wait=wait)

        with ExitStack() as pc:
            cb0, ost0 = o_store_gather(OL0H, OL0GH, OH1F0, OG1F0, "c0o")
            emit_A1(p, nc, pc, g, drA1, o_store=cb0, tag="a1")
            p.full_barrier("b1")

        with ExitStack() as pc:
            emit_A2(p, nc, pc, g, dict(WO=WO0, XO=XOWN), wo_f32=False,
                    of_loader=dyn_of_loader(OL0GH, OG1F0, ost0), out_writer=xres_writer, tag="a2")
            p.full_barrier("b2")

        with ExitStack() as pc:
            emit_B(p, nc, pc, g, drB, x_ap=xres[:], out_writer=xres_writer, tag="b")
            p.full_barrier("b3")

        s_l0 = p.sem("l0s")
        cc1 = p.sem("ccL")
        def lat_slab_store(sl, ckv, cq, kr, wait):
            p.dma("gpsimd", LAT0[sl][:, 0:4 * 512].rearrange("p (a t) -> p a t", t=512),
                  ckv[:].rearrange("p (a t) -> p a t", t=TOK)[:, :, sl * 512:(sl + 1) * 512],
                  s_l0, wait=wait)
            p.dma("gpsimd", LAT0[sl][:, 4 * 512:8 * 512].rearrange("p (a t) -> p a t", t=512),
                  cq[:].rearrange("p (a t) -> p a t", t=TOK)[:, :, sl * 512:(sl + 1) * 512],
                  s_l0, wait=wait)
            p.dma("gpsimd", LAT0[sl][0:64, 8 * 512:9 * 512], kr[0:64, sl * 512:(sl + 1) * 512],
                  s_l0, wait=wait)
            p.dma("gpsimd", LAT0[sl][64:128, 8 * 512:9 * 512], kr[0:64, sl * 512:(sl + 1) * 512],
                  s_l0, wait=wait)
            p.emit("gpsimd",
                   lambda e, sl=sl: e.collective_compute(
                       "AllGather", ALU.bypass, replica_groups=RG,
                       ins=[LAT0[sl].opt()], outs=[LATG[sl].opt()]),
                   wait=[(s_l0, 64 * (sl + 1))], inc=[(cc1, 1)])
        with ExitStack() as pc:
            emit_C0(p, nc, pc, g, drC0, x_ap=xres[:], slab_store=lat_slab_store, tag="c0")
            p.full_barrier("b4")

        def ckv_src(js):
            r, ts = js // 2, js % 2
            return LATG[ts][r * 128:(r + 1) * 128, 0:4 * 512].rearrange(
                "p (a t) -> p a t", a=4)
        def cq_src(js):
            r, ts = js // 2, js % 2
            return LATG[ts][r * 128:(r + 1) * 128, 4 * 512:8 * 512].rearrange(
                "p (a t) -> p a t", a=4)
        def kr_src(js):
            r, ts = js // 2, js % 2
            return LATG[ts][r * 128:r * 128 + 64, 8 * 512:9 * 512]
        with ExitStack() as pc:
            cb1, ost1 = o_store_gather(OL1H, OL1GH, OH1F1, OG1F1, "c1o")
            emit_C1(p, nc, pc, g, drC1, ckv_src=ckv_src, cq_src=cq_src,
                    kr_src=kr_src, o_store=cb1,
                    lat_gate=lambda js: [(cc1, (js % 2) + 1)], tag="c1")
            p.full_barrier("b5")

        with ExitStack() as pc:
            emit_A2(p, nc, pc, g, dict(WO=WO1), wo_f32=True,
                    of_loader=dyn_of_loader(OL1GH, OG1F1, ost1), xo_ap=xres[:],
                    out_writer=xres_writer, tag="c2")
            p.full_barrier("b6")

        with ExitStack() as pc:
            emit_D(p, nc, pc, g, drD, x_ap=xres[:], tag="d")

        p.finalize(block)
    return nc


def stage_fused(inp):
    mA1 = stage_A1(inp)
    wo0 = stage_lhsT_tiles(np.asarray(inp["L0_attn_Wo"]))
    wo1 = np.ascontiguousarray(stage_lhsT_tiles(np.asarray(inp["L1_attn_w_o"])).astype(np.float32))
    gw = stage_lhsT_tiles(np.asarray(inp["L0_ffn_gate_w"]).T)
    uw = stage_lhsT_tiles(np.asarray(inp["L0_ffn_up_w"]).T)
    dwt = stage_lhsT_tiles(np.asarray(inp["L0_ffn_down_w"]).T)
    dw = np.ascontiguousarray(
        dwt.reshape(NCH, 128, 2, 22 * 128).transpose(0, 2, 1, 3).reshape(NCH * 2, 128, 22 * 128))
    gs = chunk_col(inp["L0_ffn_gate_s"]).astype(np.float32)
    us = chunk_col(inp["L0_ffn_up_s"]).astype(np.float32)
    ds = chunk_col(inp["L0_ffn_down_s"]).astype(np.float32)
    mC0 = stage_C0(inp, [None] * NCORE)
    mC1 = stage_C1(inp, None, None, None)
    mD = stage_D(inp, [None] * NCORE)
    x = np.asarray(inp["L0_attn_x"])[0]
    XF = np.ascontiguousarray(fm(x).reshape(NCH, 128, T).transpose(1, 0, 2).reshape(128, NCH * T))
    maps = []
    for c in range(NCORE):
        m = dict(mA1[c])
        m.update(WO0=wo0, WO1=wo1,
                 XOWN=fm_full_slice_own(XF, c),
                 GW=gw, UW=uw, DW=dw, GS=gs, US=us, DS=ds,
                 WCKV=mC0[c]["WCKV"], WCQ=mC0[c]["WCQ"], WKR=mC0[c]["WKR"],
                 WKH=mC1[c]["WKH"], WVH=mC1[c]["WVH"], WQH=mC1[c]["WQH"],
                 WQRH=mC1[c]["WQRH"], COS64=mC1[c]["COS64"], SIN64=mC1[c]["SIN64"],
                 PIT64=mC1[c]["PIT64"],
                 WB=mD[c]["WB"], ROUT=mD[c]["ROUT"], ACAT=mD[c]["ACAT"],
                 BSTK=mD[c]["BSTK"], EMAT=mD[c]["EMAT"])
        maps.append(m)
    return maps


def kernel(**inputs):
    inp = {k: np.asarray(v) for k, v in inputs.items()}
    cores = list(range(NCORE))
    nc = build_fused()
    res = run_bass_kernel_spmd(nc, stage_fused(inp), core_ids=cores)
    x4 = np.empty((NCH, 128, T), dtype=np.float32)
    for c in cores:
        x4[:, :, c * TOK:(c + 1) * TOK] = np.asarray(
            res.results[c]["X4"]).reshape(128, NCH, TOK).transpose(1, 0, 2)
    out = x4.reshape(D, T).T.reshape(1, T, D)
    return np.ascontiguousarray(out)


# version 12 (final)
# speedup vs baseline: 1.0508x; 1.0002x over previous
import numpy as np
import ml_dtypes
import concourse.bass as bass
import concourse.mybir as mybir
from concourse.bass_utils import run_bass_kernel_spmd
from contextlib import ExitStack

BF = ml_dtypes.bfloat16
F32 = mybir.dt.float32
BF16 = mybir.dt.bfloat16
AF = mybir.ActivationFunctionType
ALU = mybir.AluOpType
AX = mybir.AxisListType

NCORE = 8
T = 8192
D = 2048
NCH = D // 128
TOK = T // NCORE
INTER = 5632
NICH = INTER // 128
LN_EPS = 1e-5
EXP_SHIFT = -20.0
MASK_NEG = -30000.0

ENGINES = ("sync", "tensor", "vector", "scalar", "gpsimd")


class Sem:
    __slots__ = ("h", "n", "name")
    def __init__(self, h, name):
        self.h = h; self.n = 0; self.name = name
    def at(self):
        return (self, self.n)


class Prog:
    def __init__(self, nc, ctx):
        self.nc = nc
        self.ctx = ctx
        self.ops = {e: [] for e in ENGINES}
        self.nsem = 0
        self.all_sems = []
        self.floor = None

    def sem(self, name=None):
        name = f"{name or 's'}_{self.nsem}"
        self.nsem += 1
        s = Sem(self.ctx.enter_context(self.nc.semaphore(name)), name)
        self.all_sems.append(s)
        return s

    def emit(self, eng, fn, wait=(), inc=()):
        wait = tuple(wait) + ((self.floor,) if self.floor else ())
        for s, v in wait:
            assert v <= s.n, f"wait on future value of {s.name}: {v} > {s.n}"
        self.ops[eng].append((fn, wait, tuple(inc)))
        for s, step in inc:
            s.n += step

    def full_barrier(self, name="bar"):
        bar = self.sem(name)
        waits = [(s, s.n) for s in self.all_sems if s.n > 0 and s is not bar]
        self.floor = None
        for eng in ENGINES:
            self.emit(eng, lambda e: e.nop(), wait=waits, inc=[(bar, 1)])
        self.floor = (bar, len(ENGINES))
        return bar

    def dma(self, eng, out, in_, sem, wait=()):
        self.emit(eng, lambda e, o=out, i=in_: e.dma_start(out=o, in_=i),
                  wait=wait, inc=((sem, 16),))

    def finalize(self, block):
        hw = {}
        for eng in ENGINES:
            eng_ops = self.ops[eng]
            def body(e, eng=eng, eng_ops=eng_ops):
                for fn, waits, incs in eng_ops:
                    for s, v in waits:
                        k = (eng, s.name)
                        if hw.get(k, -1) < v:
                            e.wait_ge(s.h, v)
                            hw[k] = v
                    inst = fn(e)
                    for i, (s, step) in enumerate(incs):
                        assert inst is not None
                        if i == 0:
                            inst.then_inc(s.h, step)
                        else:
                            e.nop().then_inc(s.h, step)
            getattr(block, eng)(body)


def emit_mm_acc(p, psum_ap, pairs, wait=(), inc=(), start=True, stop=True):
    n = len(pairs)
    for i, (lh, rh) in enumerate(pairs):
        st = start and (i == 0)
        sp = stop and (i == n - 1)
        p.emit("tensor",
               lambda e, o=psum_ap, l=lh, r=rh, a=st, b=sp:
                   e.matmul(o, l, r, start=a, stop=b, skip_group_check=True),
               wait=(wait if i == 0 else ()),
               inc=(inc if i == n - 1 else ()))



def stage_lhsT_tiles(W):
    In, Out = W.shape
    nin, nout = In // 128, Out // 128
    return np.ascontiguousarray(
        np.asarray(W).reshape(nin, 128, nout, 128).transpose(2, 1, 0, 3)
        .reshape(nout, 128, nin * 128))

def rope_tables_l0():
    half = 64
    inv = 1.0 / (1000000.0 ** (np.arange(half, dtype=np.float64) / half))
    ang = np.outer(np.arange(T, dtype=np.float64), inv)
    c, s = np.cos(ang).T, np.sin(ang).T
    return (np.ascontiguousarray(np.concatenate([c, c], 0)).astype(BF),
            np.ascontiguousarray(np.concatenate([s, s], 0)).astype(BF))

def rope_tables_l1():
    inv = 1.0 / (10000.0 ** (np.arange(0, 64, 2, dtype=np.float64) / 64.0))
    ang = np.outer(np.arange(T, dtype=np.float64), inv)
    c, s = np.cos(ang).T, np.sin(ang).T
    return (np.ascontiguousarray(np.concatenate([c, c], 0)).astype(BF),
            np.ascontiguousarray(np.concatenate([s, s], 0)).astype(BF))

def perm_rh_T(n):
    h = n // 2
    P = np.zeros((n, n), dtype=np.float32)
    P[:h, h:] = -np.eye(h)
    P[h:, :h] = np.eye(h)
    return np.ascontiguousarray(P.T).astype(BF)

def tri_masks():
    tri = np.where(np.arange(128)[:, None] > np.arange(128)[None, :],
                   np.float32(MASK_NEG), np.float32(0.0))
    dead = np.full((128, 128), np.float32(MASK_NEG), dtype=np.float32)
    live = np.zeros((128, 128), dtype=np.float32)
    out = []
    for i in range(4):
        out.append(np.concatenate([dead] * i + [tri] + [live] * (3 - i), axis=1))
    return np.stack(out).astype(BF)

def fm(x):
    return np.ascontiguousarray(np.asarray(x).T)

def chunk_col(vec):
    return np.ascontiguousarray(np.asarray(vec).reshape(-1, 128).T)

def consts_common():
    consts = np.zeros((128, 2), dtype=np.float32)
    consts[:, 0] = LN_EPS
    consts[:, 1] = EXP_SHIFT
    return {
        "ONE1": np.ones((1, 128), dtype=BF),
        "ONEC": np.ones((128, 1), dtype=BF),
        "CONSTS": consts,
    }



def emit_ln_apply(p, xbf_tile, put_out, rowx, rowsq, bcr, bcb, rows, rhl, one1,
                  tmp32, s_row, s_bc, s_ap, wait_rows, wait_bc_extra=(),
                  nfeat=D, out_chunks=NCH):
    inv_n = 1.0 / nfeat
    S0, S1, S2 = rows[:, 0:512], rows[:, 512:1024], rows[:, 1024:1536]
    def step(eng, fn, extra=()):
        p.emit(eng, fn, wait=[s_row.at()] + list(extra), inc=[(s_row, 1)])
    step("vector", lambda e: e.tensor_scalar_mul(S0, rowx, inv_n), wait_rows)
    step("vector", lambda e: e.tensor_scalar_mul(S1, rowsq, inv_n))
    step("vector", lambda e: e.tensor_mul(S2, S0, S0))
    step("vector", lambda e: e.tensor_sub(S1, S1, S2))
    step("scalar", lambda e: e.activation(S2, S1, AF.Ln, bias=float(LN_EPS)))
    step("scalar", lambda e: e.activation(S2, S2, AF.Exp, scale=-0.5))
    step("vector", lambda e: e.tensor_mul(S1, S0, S2))
    step("vector", lambda e: e.tensor_scalar_mul(S1, S1, -1.0))
    step("vector", lambda e: e.tensor_copy(rhl[:, 0:512], S2))
    step("vector", lambda e: e.tensor_sub(S0, S2, rhl[:, 0:512]))
    step("vector", lambda e: e.tensor_copy(rhl[:, 512:1024], S0))
    step("vector", lambda e: e.tensor_copy(rhl[:, 1024:1536], S1))
    step("vector", lambda e: e.tensor_sub(S0, S1, rhl[:, 1024:1536]))
    step("vector", lambda e: e.tensor_copy(rhl[:, 1536:2048], S0))
    rv = s_row.at()
    emit_mm_acc(p, bcr, [(one1, rhl[:, 0:512]), (one1, rhl[:, 512:1024])],
                wait=[rv] + list(wait_bc_extra), inc=[(s_bc, 1)])
    emit_mm_acc(p, bcb, [(one1, rhl[:, 1024:1536]), (one1, rhl[:, 1536:2048])],
                inc=[(s_bc, 1)])
    bv = s_bc.at()
    apvals = []
    for ec in range(out_chunks):
        xt = xbf_tile(ec)
        ot = put_out(ec)
        p.emit("vector", lambda e, xt=xt: e.tensor_mul(tmp32[:], xt, bcr),
               wait=[bv, s_ap.at()], inc=[(s_ap, 1)])
        p.emit("vector", lambda e, ot=ot: e.tensor_add(ot, tmp32[:], bcb),
               wait=[s_ap.at()], inc=[(s_ap, 1)])
        apvals.append(s_ap.n)
    return apvals




def declare_globals(nc):
    return {
        "ONE1": nc.dram_tensor("ONE1", [1, 128], BF16, kind="ExternalInput").ap(),
        "ONEC": nc.dram_tensor("ONEC", [128, 1], BF16, kind="ExternalInput").ap(),
        "CONSTS": nc.dram_tensor("CONSTS", [128, 2], F32, kind="ExternalInput").ap(),
        "MASKS": nc.dram_tensor("MASKS", [4, 128, 512], BF16, kind="ExternalInput").ap(),
        "IDT": nc.dram_tensor("IDT", [128, 128], BF16, kind="ExternalInput").ap(),
    }

def stage_globals():
    cst = consts_common()
    return dict(ONE1=cst["ONE1"], ONEC=cst["ONEC"], CONSTS=cst["CONSTS"],
                MASKS=tri_masks(), IDT=np.eye(128, dtype=BF))

def emit_globals(p, nc, ctx, dr, with_masks=True):
    sb = lambda n, s, d=BF16: ctx.enter_context(nc.sbuf_tensor(n, s, d))
    g = {
        "one1": sb("g_one1", [1, 128]),
        "onec": sb("g_onec", [128, 1]),
        "csts": sb("g_csts", [128, 2], F32),
    }
    nc.const_aps.register(F32, float(LN_EPS), g["csts"][:, 0:1])
    nc.const_aps.register(F32, float(EXP_SHIFT), g["csts"][:, 1:2])
    s_gld = p.sem("gld")
    p.dma("sync", g["one1"][:], dr["ONE1"][:, :], s_gld)
    p.dma("sync", g["onec"][:], dr["ONEC"][:, :], s_gld)
    p.dma("sync", g["csts"][:], dr["CONSTS"][:, :], s_gld)
    if with_masks:
        g["masks"] = sb("g_masks", [128, 4 * 512])
        g["idt"] = sb("g_idt", [128, 128])
        p.dma("sync", g["masks"][:].rearrange("p (a c) -> p a c", a=4),
              dr["MASKS"].rearrange("a p c -> p a c"), s_gld)
        p.dma("sync", g["idt"][:], dr["IDT"][:, :], s_gld)
    g["ld"] = s_gld.at()
    return g



def emit_B(p, nc, ctx, g, dr, x_ap=None, out_writer=None, tag="b"):
    GW, UW, DW, GS, US, DS = dr["GW"], dr["UW"], dr["DW"], dr["GS"], dr["US"], dr["DS"]
    if True:
        sb = lambda n, s, d=BF16: ctx.enter_context(nc.sbuf_tensor(tag + n, s, d))
        psb = lambda n: ctx.enter_context(nc.psum_tensor(tag + n, [128, 512], F32))

        x = sb("x", [128, NCH * TOK]) if x_ap is None else x_ap
        h = sb("h", [128, NICH * TOK])
        wf = [sb(f"wf{i}", [128, 22 * 128], F32) for i in range(2)]
        wb = [sb(f"wb{i}", [128, 22 * 128]) for i in range(3)]
        gs = sb("gs", [128, NICH], F32)
        us = sb("us", [128, NICH], F32)
        ds = sb("ds", [128, NCH], F32)
        one1, onec = g["one1"], g["onec"]
        sg = [sb(f"sg{i}", [128, 512]) for i in range(2)]
        xbf = [sb(f"xbf{i}", [128, NCH * 512]) for i in range(2)]
        sq = sg
        rows = wf[0][0:1, 0:1536]
        rhl = wf[0][0:1, 1536:2560].bitcast(BF16)
        tmp32 = wf[1][:, 0:512]
        outsl = [h[:, 0:NCH * 512], h[:, NCH * 512:2 * NCH * 512]]

        BNK = [psb(f"B{i}") for i in range(8)]
        pg = [BNK[0], BNK[1]]
        pu = [BNK[2], BNK[3]]
        pd = [BNK[0], BNK[1]]
        rowx = [BNK[2][0:1, 0:512], BNK[6][0:1, 0:512]]
        rowsq = [BNK[3][0:1, 0:512], BNK[7][0:1, 0:512]]
        bcr, bcb = BNK[4], BNK[5]

        s_ld = p.sem("ld")
        s_w2 = [p.sem("w0"), p.sem("w1")]
        s_cva = p.sem("cva"); s_cvb = p.sem("cvb")
        s_mm = p.sem("mm"); s_si = p.sem("si")
        consumed_mm = {}
        s_ep = p.sem("ep"); s_row = p.sem("row"); s_bc = p.sem("bc")
        s_ap = p.sem("ap"); s_st = p.sem("st")

        if x_ap is None:
            p.dma("sync", x[:], dr["X"][:, :], s_ld)
        p.dma("sync", gs[:], GS[:, :], s_ld)
        p.dma("sync", us[:], US[:, :], s_ld)
        p.dma("sync", ds[:], DS[:, :], s_ld)
        ld0 = s_ld.at()
        gld = g["ld"]

        wlist = []
        for ic in range(NICH):
            wlist.append((GW[ic, :, :], NCH * 128))
            wlist.append((UW[ic, :, :], NCH * 128))
        for ec in range(NCH):
            wlist.append((DW[2 * ec, :, :], 22 * 128))
            wlist.append((DW[2 * ec + 1, :, :], 22 * 128))
        nW = len(wlist)

        emitted_w = set()
        def record_wload(wi):
            if wi >= nW or wi in emitted_w:
                return
            emitted_w.add(wi)
            src, ncols = wlist[wi]
            fb = wi % 2
            buf = wi % 3
            wait = []
            if wi >= 2:
                wait.append((s_cva, wi - 1))
                wait.append((s_cvb, wi - 1))
            if wi >= 3:
                wait.append((s_mm, consumed_mm[wi - 3]))
            p.dma("sync", wf[fb][:, 0:ncols], src, s_w2[fb], wait=wait)
            half = ncols // 2
            wv = (s_w2[fb], (wi // 2 + 1) * 16)
            cw = [wv] + ([(s_mm, consumed_mm[wi - 3])] if wi >= 3 else [])
            p.emit("scalar", lambda e, b=buf, f=fb, h2=half: e.copy(wb[b][:, 0:h2], wf[f][:, 0:h2]),
                   wait=cw, inc=[(s_cva, 1)])
            p.emit("gpsimd", lambda e, b=buf, f=fb, h2=half, n=ncols: e.tensor_copy(wb[b][:, h2:n], wf[f][:, h2:n]),
                   wait=cw, inc=[(s_cvb, 1)])

        for wi in range(3):
            record_wload(wi)

        for ic in range(NICH):
            gbuf = (2 * ic) % 3
            ubuf = (2 * ic + 1) % 3
            for sl in range(2):
                t0 = sl * 512
                wait = [(s_cva, 2 * ic + 1), (s_cvb, 2 * ic + 1), ld0, gld]
                if ic >= 1:
                    wait.append((s_si, 2 * (ic - 1) + sl + 1))
                pairs = [(wb[gbuf][:, hc * 128:(hc + 1) * 128],
                          x[:, hc * TOK + t0: hc * TOK + t0 + 512]) for hc in range(NCH)]
                emit_mm_acc(p, pg[sl][:], pairs, wait=wait, inc=[(s_mm, 1)])
                if sl == 1:
                    consumed_mm[2 * ic] = s_mm.n
            for sl in range(2):
                t0 = sl * 512
                wait = [(s_cva, 2 * ic + 2), (s_cvb, 2 * ic + 2)]
                if ic >= 1:
                    wait.append((s_ep, 2 * (ic - 1) + sl + 1))
                pairs = [(wb[ubuf][:, hc * 128:(hc + 1) * 128],
                          x[:, hc * TOK + t0: hc * TOK + t0 + 512]) for hc in range(NCH)]
                emit_mm_acc(p, pu[sl][:], pairs, wait=wait, inc=[(s_mm, 1)])
                if sl == 1:
                    consumed_mm[2 * ic + 1] = s_mm.n
            for sl in range(2):
                wait = [(s_mm, 4 * ic + sl + 1)]
                if ic >= 1:
                    wait.append((s_ep, 2 * (ic - 1) + sl + 1))
                p.emit("scalar",
                       lambda e, sl=sl, ic=ic: e.activation(sg[sl][:], pg[sl][:], AF.Silu,
                                                            scale=gs[:, ic:ic + 1]),
                       wait=wait, inc=[(s_si, 1)])
            for sl in range(2):
                dst = h[:, ic * TOK + sl * 512: ic * TOK + sl * 512 + 512]
                p.emit("vector",
                       lambda e, sl=sl, ic=ic, dst=dst:
                           e.scalar_tensor_tensor(dst, pu[sl][:], us[:, ic:ic + 1],
                                                  sg[sl][:], ALU.mult, ALU.mult),
                       wait=[(s_si, 2 * ic + sl + 1), (s_mm, 4 * ic + 2 + sl + 1)],
                       inc=[(s_ep, 1)])
            record_wload(2 * ic + 3)
            record_wload(2 * ic + 4)

        ep_all = s_ep.at()
        si_all = s_si.at()
        mm_d0 = s_mm.n
        dg_mm = {}
        ep_after_stt = {}
        si_after_sq = {}
        rsq_mm = {}
        def emit_rows_b(ec):
            for sl in range(2):
                emit_mm_acc(p, rowx[sl], [(onec[:], xbf[sl][:, ec * 512:(ec + 1) * 512])],
                            wait=[ep_after_stt[(ec, sl)]], inc=[(s_mm, 1)],
                            start=(ec == 0), stop=(ec == NCH - 1))
                emit_mm_acc(p, rowsq[sl], [(onec[:], sq[sl][:])],
                            wait=[si_after_sq[(ec, sl)]], inc=[(s_mm, 1)],
                            start=(ec == 0), stop=(ec == NCH - 1))
                rsq_mm[(ec, sl)] = s_mm.n
        for ec in range(NCH):
            w0 = 88 + 2 * ec
            b0 = w0 % 3
            b1 = (w0 + 1) % 3
            for sl in range(2):
                t0 = sl * 512
                wait = [(s_cva, w0 + 2), (s_cvb, w0 + 2), ep_all, si_all]
                if ec >= 1:
                    wait.append((s_ep, 88 + 2 * (ec - 1) + sl + 1))
                pairs = ([(wb[b0][:, k * 128:(k + 1) * 128],
                           h[:, k * TOK + t0: k * TOK + t0 + 512]) for k in range(22)] +
                         [(wb[b1][:, (k - 22) * 128:(k - 21) * 128],
                           h[:, k * TOK + t0: k * TOK + t0 + 512]) for k in range(22, 44)])
                emit_mm_acc(p, pd[sl][:], pairs, wait=wait, inc=[(s_mm, 1)])
                dg_mm[(ec, sl)] = s_mm.n
                if sl == 1:
                    consumed_mm[w0] = consumed_mm[w0 + 1] = s_mm.n
            record_wload(w0 + 2)
            record_wload(w0 + 3)
            if ec >= 1:
                emit_rows_b(ec - 1)
            for sl in range(2):
                mmv = (s_mm, dg_mm[(ec, sl)])
                dst = xbf[sl][:, ec * 512:(ec + 1) * 512]
                xres = x[:, ec * TOK + sl * 512: ec * TOK + sl * 512 + 512]
                p.emit("vector",
                       lambda e, sl=sl, ec=ec, dst=dst, xres=xres:
                           e.scalar_tensor_tensor(dst, pd[sl][:], ds[:, ec:ec + 1],
                                                  xres, ALU.mult, ALU.add),
                       wait=[mmv], inc=[(s_ep, 1)])
                ep_after_stt[(ec, sl)] = s_ep.at()
                sqw = [ep_after_stt[(ec, sl)]]
                if ec >= 1:
                    sqw.append((s_mm, rsq_mm[(ec - 1, sl)]))
                p.emit("scalar",
                       lambda e, sl=sl, ec=ec: e.square(sq[sl][:], xbf[sl][:, ec * 512:(ec + 1) * 512]),
                       wait=sqw, inc=[(s_si, 1)])
                si_after_sq[(ec, sl)] = s_si.at()
        emit_rows_b(NCH - 1)

        for sl in range(2):
            wait_rows = [s_mm.at()]
            wait_bc_extra = [s_ap.at()] if sl == 1 else []
            apv = emit_ln_apply(p,
                          xbf_tile=lambda ec, sl=sl: xbf[sl][:, ec * 512:(ec + 1) * 512],
                          put_out=lambda ec, sl=sl: outsl[sl][:, ec * 512:(ec + 1) * 512],
                          rowx=rowx[sl], rowsq=rowsq[sl], bcr=bcr[:], bcb=bcb[:],
                          rows=rows, rhl=rhl, one1=one1[:], tmp32=tmp32,
                          s_row=s_row, s_bc=s_bc, s_ap=s_ap,
                          wait_rows=wait_rows, wait_bc_extra=wait_bc_extra)
            if out_writer is not None:
                for g4 in range(4):
                    out_writer(sl, outsl[sl], [(s_ap, apv[4 * g4 + 3])],
                               ec0=4 * g4, ec1=4 * g4 + 4)
            else:
                dst = dr["XOUT"][:].rearrange("p (c t) -> p c t", t=TOK)[:, :, sl * 512:(sl + 1) * 512]
                srcap = outsl[sl][:].rearrange("p (c t) -> p c t", t=512)
                p.dma("gpsimd", dst, srcap, s_st, wait=[s_ap.at()])

        p.emit("gpsimd", lambda e: e.nop(), wait=[s_st.at()])
    return s_ap.at()


def declare_B(nc):
    return dict(
        GW=nc.dram_tensor("GW", [NICH, 128, NCH * 128], F32, kind="ExternalInput").ap(),
        UW=nc.dram_tensor("UW", [NICH, 128, NCH * 128], F32, kind="ExternalInput").ap(),
        DW=nc.dram_tensor("DW", [NCH * 2, 128, 22 * 128], F32, kind="ExternalInput").ap(),
        GS=nc.dram_tensor("GS", [128, NICH], F32, kind="ExternalInput").ap(),
        US=nc.dram_tensor("US", [128, NICH], F32, kind="ExternalInput").ap(),
        DS=nc.dram_tensor("DS", [128, NCH], F32, kind="ExternalInput").ap(),
    )


def build_B():
    nc = bass.Bass("TRN2", target_bir_lowering=False, debug=False)
    gdr = declare_globals(nc)
    dr = declare_B(nc)
    dr["X"] = nc.dram_tensor("X", [128, NCH * TOK], BF16, kind="ExternalInput").ap()
    dr["XOUT"] = nc.dram_tensor("XOUT", [128, NCH * TOK], BF16, kind="ExternalOutput").ap()
    with ExitStack() as ctx:
        p = Prog(nc, ctx)
        block = ctx.enter_context(nc.Block())
        g = emit_globals(p, nc, ctx, gdr, with_masks=False)
        emit_B(p, nc, ctx, g, dr)
        p.finalize(block)
    return nc


def stage_B(inp, x1_own_fm):
    cst = stage_globals()
    gw = stage_lhsT_tiles(np.asarray(inp["L0_ffn_gate_w"]).T)
    uw = stage_lhsT_tiles(np.asarray(inp["L0_ffn_up_w"]).T)
    dwt = stage_lhsT_tiles(np.asarray(inp["L0_ffn_down_w"]).T)
    dw = np.ascontiguousarray(
        dwt.reshape(NCH, 128, 2, 22 * 128).transpose(0, 2, 1, 3).reshape(NCH * 2, 128, 22 * 128))
    gs = chunk_col(inp["L0_ffn_gate_s"]).astype(np.float32)
    us = chunk_col(inp["L0_ffn_up_s"]).astype(np.float32)
    ds = chunk_col(inp["L0_ffn_down_s"]).astype(np.float32)
    maps = []
    for c in range(NCORE):
        maps.append(dict(X=x1_own_fm[c], GW=gw, UW=uw, DW=dw, GS=gs, US=us, DS=ds,
                         **cst))
    return maps


def x_fm_to_own(x_fm):
    xr = x_fm.reshape(NCH, 128, T)
    return [np.ascontiguousarray(
        xr[:, :, c * TOK:(c + 1) * TOK].transpose(1, 0, 2).reshape(128, NCH * TOK))
        for c in range(NCORE)]


def fm_own_to_full(outs):
    full = np.empty((NCH, 128, T), dtype=outs[0].dtype)
    for c in range(NCORE):
        full[:, :, c * TOK:(c + 1) * TOK] = outs[c].reshape(128, NCH, TOK).transpose(1, 0, 2)
    return full.reshape(D, T)






SCALE_L0 = 128.0 ** -0.5

def emit_A1(p, nc, ctx, g, dr, o_store=None, tag="a1"):
    XF, WQ, WK, WV = dr["XF"], dr["WQ"], dr["WK"], dr["WV"]
    BQ, BK, BV, COS, SIN, PIT, O = dr["BQ"], dr["BK"], dr["BV"], dr["COS"], dr["SIN"], dr["PIT"], dr["O"]
    if True:
        sb = lambda n, s, d=BF16: ctx.enter_context(nc.sbuf_tensor(tag + n, s, d))

        xs = [sb(f"xs{i}", [128, NCH * 512]) for i in range(2)]
        coss = [sb(f"coss{i}", [128, 512]) for i in range(2)]
        sins = [sb(f"sins{i}", [128, 512]) for i in range(2)]
        wq = sb("wq", [128, 2 * NCH * 128])
        wk = sb("wk", [128, NCH * 128])
        wv = sb("wv", [128, NCH * 128])
        bq = sb("bq", [128, 2])
        bk = sb("bk", [128, 1])
        bv = sb("bv", [128, 1])
        pit = sb("pit", [128, 128])
        idt, masks, one1, onec, csts = g["idt"], g["masks"], g["one1"], g["onec"], g["csts"]
        kbuf = sb("kbuf", [128, T])
        vtb = sb("vtb", [128, T])
        qbuf = sb("qbuf", [128, 2 * T])
        obuf = qbuf
        praw = [sb(f"praw{i}", [128, 512]) for i in range(2)]
        rp1 = [sb(f"rp1{i}", [128, 512]) for i in range(2)]
        rp2 = [sb(f"rp2{i}", [128, 512]) for i in range(2)]
        pt = [sb(f"pt{i}", [128, 1024]) for i in range(2)]
        srow = sb("srow", [1, 512], F32)
        rrow = sb("rrow", [1, 512], F32)
        rhl = sb("rhl", [1, 2 * 512])
        rb = sb("rb", [128, 512], F32)
        acc = sb("acc", [128, 1024], F32)
        accbf = sb("accbf", [128, 1024])

        S2 = [ctx.enter_context(nc.psum_tensor(f"{tag}S{i}", [128, 1024], F32)) for i in range(2)]
        POB = [ctx.enter_context(nc.psum_tensor(f"{tag}P{i}", [128, 512], F32)) for i in range(2)]
        BCB = ctx.enter_context(nc.psum_tensor(tag + "C", [128, 512], F32))
        prt = ctx.enter_context(nc.psum_tensor(tag + "prt", [128, 512], BF16))
        BNK = [S2[0][:, 0:512], S2[0][:, 512:1024], S2[1][:, 0:512], S2[1][:, 512:1024],
               POB[0][:], POB[1][:], BCB[:]]

        s_ld = p.sem("ld")
        s_xs = [p.sem("xs0"), p.sem("xs1")]
        s_cs = [p.sem("cs0"), p.sem("cs1")]
        s_mm = p.sem("mm")
        s_ev = p.sem("ev")
        s_exp = p.sem("exp")
        s_msk = p.sem("msk")
        s_nrm = p.sem("nrm")
        s_st = p.sem("st")

        for dst, srcap in ((wq[:].rearrange("p (a c) -> p a c", a=2), WQ.rearrange("a p c -> p a c")),
                           (wk[:], WK[:, :]),
                           (wv[:], WV[:, :]), (bq[:], BQ[:, :]), (bk[:], BK[:, :]), (bv[:], BV[:, :]),
                           (pit[:], PIT[:, :])):
            p.dma("sync", dst, srcap, s_ld)
        ld0 = s_ld.at()
        gld = g["ld"]

        consumed_xs = {}
        consumed_cs = {}
        bank_free = {}

        def load_slab(js):
            if js >= 16:
                return
            b = js % 2
            wait = [(s_mm, consumed_xs[js - 2])] if js >= 2 else []
            p.dma("sync", xs[b][:],
                  XF[:].rearrange("p (c t) -> p c t", t=T)[:, :, js * 512:(js + 1) * 512],
                  s_xs[b], wait=wait)
            waitc = [(s_ev, consumed_cs[js - 2])] if js >= 2 else []
            p.dma("gpsimd", coss[b][:], COS[:, js * 512:(js + 1) * 512], s_cs[b], wait=waitc)
            p.dma("gpsimd", sins[b][:], SIN[:, js * 512:(js + 1) * 512], s_cs[b])

        load_slab(0)
        load_slab(1)

        def rope_emit(bankid, pb, bias_ap, b, cv, dst_ap):
            mmv = s_mm.at()
            p.emit("scalar", lambda e: e.add(praw[pb][:], BNK[bankid], bias_ap),
                   wait=[mmv, s_ev.at()], inc=[(s_ev, 1)])
            bank_free[bankid] = s_ev.n
            rhb = 4 + b
            w = [s_ev.at()]
            if rhb in bank_free:
                w.append((s_ev, bank_free[rhb]))
            p.emit("tensor", lambda e: e.matmul(BNK[rhb], pit[:], praw[pb][:],
                                                start=True, stop=True, skip_group_check=True),
                   wait=w, inc=[(s_mm, 1)])
            rhv = s_mm.at()
            p.emit("gpsimd", lambda e: e.tensor_mul(rp1[pb][:], praw[pb][:], coss[b][:]),
                   wait=[s_ev.at(), cv], inc=[(s_ev, 1)])
            p.emit("vector", lambda e: e.tensor_mul(rp2[pb][:], BNK[rhb], sins[b][:]),
                   wait=[rhv, s_ev.at(), cv], inc=[(s_ev, 1)])
            bank_free[rhb] = s_ev.n
            p.emit("vector", lambda e: e.tensor_add(dst_ap, rp1[pb][:], rp2[pb][:]),
                   wait=[s_ev.at()], inc=[(s_ev, 1)])

        for js in range(16):
            b = js % 2
            xv = (s_xs[b], (js // 2 + 1) * 16)
            cv = (s_cs[b], (js // 2 + 1) * 32)
            t0 = js * 512

            def proj_group(widx, lh_sb, lh_base, gi):
                bankid = (4 * js + gi) % 4
                wait = [xv, ld0, gld]
                if bankid in bank_free:
                    wait.append((s_ev, bank_free[bankid]))
                pairs = [(lh_sb[:, lh_base + hc * 128: lh_base + (hc + 1) * 128],
                          xs[b][:, hc * 512:(hc + 1) * 512]) for hc in range(NCH)]
                emit_mm_acc(p, BNK[bankid], pairs, wait=wait, inc=[(s_mm, 1)])
                return bankid

            bk_id = proj_group(0, wk, 0, 0)
            rope_emit(bk_id, b, bk[:, 0:1], b, cv, kbuf[:, t0:t0 + 512])
            bv_id = proj_group(1, wv, 0, 1)
            mmv = s_mm.at()
            p.emit("scalar", lambda e, b=b, bv_id=bv_id: e.add(praw[b][:], BNK[bv_id], bv[:, 0:1]),
                   wait=[mmv, s_ev.at()], inc=[(s_ev, 1)])
            bank_free[bv_id] = s_ev.n
            for jt in range(4):
                w = [s_ev.at()]
                if "prt" in bank_free:
                    w.append((s_ev, bank_free["prt"]))
                p.emit("tensor",
                       lambda e, jt=jt, b=b: e.transpose(prt[:, 0:128], praw[b][:, jt * 128:(jt + 1) * 128], idt[:]),
                       wait=w, inc=[(s_mm, 1)])
                mmt = s_mm.at()
                p.emit("vector",
                       lambda e, jt=jt, t0=t0: e.tensor_copy(vtb[:, t0 + jt * 128: t0 + (jt + 1) * 128], prt[:, 0:128]),
                       wait=[mmt, s_ev.at()], inc=[(s_ev, 1)])
                bank_free["prt"] = s_ev.n
            for h in range(2):
                bq_id = proj_group(2 + h, wq, h * NCH * 128, 2 + h)
                if h == 1:
                    consumed_xs[js] = s_mm.n
                rope_emit(bq_id, b, bq[:, h:h + 1], b, cv, qbuf[:, h * T + t0: h * T + t0 + 512])
            consumed_cs[js] = s_ev.n
            load_slab(js + 2)

        barrier = s_ev.at()
        barrier_g = gld
        s_acc = p.sem("acc")
        sfree = {0: 0, 1: 0}
        ptcons = {0: 0, 1: 0}
        acccons = {0: 0, 1: 0}
        pofree = {0: None, 1: None}
        bc_free = None
        psm = BCB[0:1, 0:512]
        for h in range(2):
            for gq in range(16):
                pi = h * 16 + gq
                jn = 4 * (gq + 1)
                nu = jn // 2
                po = POB[pi % 2][:]
                qap = qbuf[:, h * T + gq * 512: h * T + (gq + 1) * 512]

                def emit_S(u, qap=qap):
                    w = [barrier, barrier_g, (s_exp, sfree[u % 2])]
                    for hf in range(2):
                        jt = 2 * u + hf
                        p.emit("tensor",
                               lambda e, u=u, jt=jt, hf=hf, qap=qap: e.matmul(
                                   S2[u % 2][:, hf * 512:(hf + 1) * 512],
                                   kbuf[:, jt * 128:(jt + 1) * 128],
                                   qap, start=True, stop=True, skip_group_check=True),
                               wait=(w if hf == 0 else ()), inc=[(s_mm, 1)])
                    return s_mm.n

                sv = emit_S(0)
                for u in range(nu):
                    i0 = 2 * u - 4 * gq
                    if i0 >= 0:
                        p.emit("vector",
                               lambda e, u=u, i0=i0: e.tensor_add(S2[u % 2][:], S2[u % 2][:],
                                                                  masks[:, i0 * 512:(i0 + 2) * 512]),
                               wait=[(s_mm, sv), s_msk.at()], inc=[(s_msk, 1)])
                        ewait = [(s_msk, s_msk.n)]
                    else:
                        ewait = [(s_mm, sv)]
                    p.emit("scalar",
                           lambda e, u=u: e.activation(pt[u % 2][:], S2[u % 2][:], AF.Exp,
                                                       bias=float(EXP_SHIFT), scale=float(SCALE_L0)),
                           wait=ewait + [(s_mm, ptcons[u % 2]), (s_acc, acccons[u % 2])],
                           inc=[(s_exp, 1)])
                    sfree[u % 2] = s_exp.n
                    ev = s_exp.n
                    if u + 1 < nu:
                        sv = emit_S(u + 1)
                    w = [(s_exp, ev)]
                    if u == 0 and pofree[pi % 2] is not None:
                        w.append((s_nrm, pofree[pi % 2]))
                    for hf in range(2):
                        jt = 2 * u + hf
                        p.emit("tensor",
                               lambda e, jt=jt, u=u, hf=hf, po=po: e.matmul(
                                   po, vtb[:, jt * 128:(jt + 1) * 128],
                                   pt[u % 2][:, hf * 512:(hf + 1) * 512],
                                   start=(jt == 0), stop=(jt == jn - 1), skip_group_check=True),
                               wait=(w if hf == 0 else ()), inc=[(s_mm, 1)])
                    ptcons[u % 2] = s_mm.n
                    if u == 0:
                        p.emit("vector", lambda e, u=u: e.tensor_copy(acc[:], pt[u % 2][:]),
                               wait=[(s_exp, ev), s_acc.at()], inc=[(s_acc, 1)])
                    else:
                        p.emit("vector", lambda e, u=u: e.tensor_add(acc[:], acc[:], pt[u % 2][:]),
                               wait=[(s_exp, ev), s_acc.at()], inc=[(s_acc, 1)])
                    acccons[u % 2] = s_acc.n

                p.emit("vector", lambda e: e.tensor_copy(accbf[:], acc[:]),
                       wait=[s_acc.at()], inc=[(s_acc, 1)])
                w = [s_acc.at()]
                if bc_free is not None:
                    w.append((s_nrm, bc_free))
                emit_mm_acc(p, psm, [(onec[:], accbf[:, 0:512]), (onec[:], accbf[:, 512:1024])],
                            wait=w, inc=[(s_mm, 1)])
                nv = s_mm.at()
                p.emit("vector", lambda e, psm=psm: e.tensor_copy(srow[:], psm),
                       wait=[nv, s_nrm.at()], inc=[(s_nrm, 1)])
                p.emit("vector", lambda e: e.reciprocal(rrow[:], srow[:]),
                       wait=[s_nrm.at()], inc=[(s_nrm, 1)])
                p.emit("vector", lambda e: e.tensor_copy(rhl[:, 0:512], rrow[:]),
                       wait=[s_nrm.at()], inc=[(s_nrm, 1)])
                p.emit("vector", lambda e: e.tensor_sub(srow[:], rrow[:], rhl[:, 0:512]),
                       wait=[s_nrm.at()], inc=[(s_nrm, 1)])
                p.emit("vector", lambda e: e.tensor_copy(rhl[:, 512:1024], srow[:]),
                       wait=[s_nrm.at()], inc=[(s_nrm, 1)])
                emit_mm_acc(p, BCB[:], [(one1[:], rhl[:, 0:512]), (one1[:], rhl[:, 512:1024])],
                            wait=[s_nrm.at()], inc=[(s_mm, 1)])
                bcv = s_mm.at()
                p.emit("vector", lambda e: e.tensor_copy(rb[:], BCB[:]),
                       wait=[bcv, s_nrm.at()], inc=[(s_nrm, 1)])
                bc_free = s_nrm.n
                dst = obuf[:, h * T + gq * 512: h * T + (gq + 1) * 512]
                p.emit("vector",
                       lambda e, dst=dst, po=po: e.scalar_tensor_tensor(dst, po, 1.0, rb[:],
                                                                        ALU.mult, ALU.mult),
                       wait=[s_nrm.at()], inc=[(s_nrm, 1)])
                pofree[pi % 2] = s_nrm.n
                if o_store is not None and h == 1 and gq % 2 == 1:
                    k = (gq - 1) // 2
                    o_store(h, obuf[:, h * T + k * TOK: h * T + (k + 1) * TOK],
                            [s_nrm.at()], chunk=k)

            if o_store is not None:
                if h == 0:
                    o_store(h, obuf[:, h * T:(h + 1) * T], [s_nrm.at()])
            else:
                p.dma("gpsimd", O[:, h * T:(h + 1) * T], obuf[:, h * T:(h + 1) * T],
                      s_st, wait=[s_nrm.at()])

        p.emit("gpsimd", lambda e: e.nop(), wait=[s_st.at()])
    return s_st.at()


def declare_A1(nc, o_ap):
    return dict(
        XF=nc.dram_tensor("XF", [128, NCH * T], BF16, kind="ExternalInput").ap(),
        WQ=nc.dram_tensor("WQ", [2, 128, NCH * 128], BF16, kind="ExternalInput").ap(),
        WK=nc.dram_tensor("WK", [128, NCH * 128], BF16, kind="ExternalInput").ap(),
        WV=nc.dram_tensor("WV", [128, NCH * 128], BF16, kind="ExternalInput").ap(),
        BQ=nc.dram_tensor("BQ", [128, 2], BF16, kind="ExternalInput").ap(),
        BK=nc.dram_tensor("BK", [128, 1], BF16, kind="ExternalInput").ap(),
        BV=nc.dram_tensor("BV", [128, 1], BF16, kind="ExternalInput").ap(),
        COS=nc.dram_tensor("COS", [128, T], BF16, kind="ExternalInput").ap(),
        SIN=nc.dram_tensor("SIN", [128, T], BF16, kind="ExternalInput").ap(),
        PIT=nc.dram_tensor("PIT", [128, 128], BF16, kind="ExternalInput").ap(),
        O=o_ap,
    )


def build_A1():
    nc = bass.Bass("TRN2", target_bir_lowering=False, debug=False)
    gdr = declare_globals(nc)
    dr = declare_A1(nc, nc.dram_tensor("O", [128, 2 * T], BF16, kind="ExternalOutput").ap())
    with ExitStack() as ctx:
        p = Prog(nc, ctx)
        block = ctx.enter_context(nc.Block())
        g = emit_globals(p, nc, ctx, gdr)
        emit_A1(p, nc, ctx, g, dr)
        p.finalize(block)
    return nc


def stage_A1(inp):
    cst = stage_globals()
    cos, sin = rope_tables_l0()
    pit = perm_rh_T(128)
    Wq = np.asarray(inp["L0_attn_Wq"])
    Wk = np.asarray(inp["L0_attn_Wk"])
    Wv = np.asarray(inp["L0_attn_Wv"])
    bqv = np.asarray(inp["L0_attn_bq"])
    bkv = np.asarray(inp["L0_attn_bk"])
    bvv = np.asarray(inp["L0_attn_bv"])
    x = np.asarray(inp["L0_attn_x"])[0]
    xf = fm(x)
    XF = np.ascontiguousarray(xf.reshape(NCH, 128, T).transpose(1, 0, 2).reshape(128, NCH * T))
    wq_t = stage_lhsT_tiles(Wq)
    wk_t = stage_lhsT_tiles(Wk)
    wv_t = stage_lhsT_tiles(Wv)
    maps = []
    for c in range(NCORE):
        h0 = 2 * c
        kv = h0 // 4
        maps.append(dict(
            XF=XF,
            WQ=np.ascontiguousarray(wq_t[h0:h0 + 2]),
            WK=np.ascontiguousarray(wk_t[kv]),
            WV=np.ascontiguousarray(wv_t[kv]),
            BQ=np.ascontiguousarray(bqv.reshape(16, 128)[h0:h0 + 2].T),
            BK=np.ascontiguousarray(bkv.reshape(4, 128)[kv:kv + 1].T),
            BV=np.ascontiguousarray(bvv.reshape(4, 128)[kv:kv + 1].T),
            COS=cos, SIN=sin, PIT=pit, **cst,
        ))
    return maps


def emit_A2(p, nc, ctx, g, dr, wo_f32=False, of_loader=None, xo_ap=None,
            out_writer=None, tag="a2"):
    WO = dr["WO"]
    if True:
        sb = lambda n, s, d=BF16: ctx.enter_context(nc.sbuf_tensor(tag + n, s, d))
        of = sb("of", [128, NCH * TOK])
        xo = sb("xo", [128, NCH * TOK]) if xo_ap is None else xo_ap
        wot = [sb(f"wot{i}", [128, NCH * 128]) for i in range(2)]
        wof = [sb(f"wof{i}", [128, NCH * 128], F32) for i in range(2)] if wo_f32 else None
        one1, onec, csts = g["one1"], g["onec"], g["csts"]
        xbf = [sb(f"xbf{i}", [128, NCH * 512]) for i in range(2)]
        sq = [sb(f"sq{i}", [128, 512]) for i in range(2)]
        rows = sb("rows", [1, 3 * 512], F32)
        rhl = sb("rhl", [1, 4 * 512])
        tmp32 = sb("tmp32", [128, 512], F32)
        outsl = [of[:, 0:NCH * 512], of[:, NCH * 512:2 * NCH * 512]]

        BNK = [ctx.enter_context(nc.psum_tensor(f"{tag}B{i}", [128, 512], F32)) for i in range(8)]
        pd = [BNK[0], BNK[1]]
        rowx = [BNK[2][0:1, 0:512], BNK[6][0:1, 0:512]]
        rowsq = [BNK[3][0:1, 0:512], BNK[7][0:1, 0:512]]
        bcr, bcb = BNK[4], BNK[5]

        s_ld = p.sem("ld"); s_mm = p.sem("mm"); s_si = p.sem("si")
        s_ep = p.sem("ep"); s_row = p.sem("row"); s_bc = p.sem("bc")
        s_ap = p.sem("ap"); s_st = p.sem("st"); s_cvw = p.sem("cvw")

        if of_loader is None:
            p.dma("sync", of[:], dr["OF"][:, :], s_ld)
        else:
            of_loader(p, of, s_ld)
        if xo_ap is None:
            p.dma("sync", xo[:], dr["XO"][:, :], s_ld)
        ld0 = (s_ld, s_ld.n)
        gld = g["ld"]

        s_wl = [p.sem("wl0"), p.sem("wl1")]
        dg_mm = {}
        wo_ready = {}
        def load_wo(ec):
            if ec >= NCH:
                return
            b = ec % 2
            wait = []
            if ec >= 2:
                wait.append((s_mm, dg_mm[(ec - 2, 1)]))
            if wo_f32:
                if ec >= 2:
                    wait.append((s_cvw, ec - 1))
                p.dma("sync", wof[b][:], WO[ec, :, :], s_wl[b], wait=wait)
                cw = [(s_wl[b], (ec // 2 + 1) * 16)]
                if ec >= 2:
                    cw.append((s_mm, dg_mm[(ec - 2, 1)]))
                p.emit("scalar", lambda e, b=b: e.copy(wot[b][:], wof[b][:]),
                       wait=cw, inc=[(s_cvw, 1)])
                wo_ready[ec] = (s_cvw, s_cvw.n)
            else:
                p.dma("sync", wot[b][:], WO[ec, :, :], s_wl[b], wait=wait)
                wo_ready[ec] = (s_wl[b], (ec // 2 + 1) * 16)
        load_wo(0)
        load_wo(1)

        ep_after_stt = {}
        si_after_sq = {}
        def emit_rows_a2(ec):
            for sl in range(2):
                emit_mm_acc(p, rowx[sl], [(onec[:], xbf[sl][:, ec * 512:(ec + 1) * 512])],
                            wait=[ep_after_stt[(ec, sl)]], inc=[(s_mm, 1)],
                            start=(ec == 0), stop=(ec == NCH - 1))
                emit_mm_acc(p, rowsq[sl], [(onec[:], sq[sl][:])],
                            wait=[si_after_sq[(ec, sl)]], inc=[(s_mm, 1)],
                            start=(ec == 0), stop=(ec == NCH - 1))
                dg_mm[("rsq", ec, sl)] = s_mm.n

        for ec in range(NCH):
            for sl in range(2):
                t0 = sl * 512
                wait = [ld0, gld, wo_ready[ec]]
                if ec >= 1:
                    wait.append((s_ep, 2 * (ec - 1) + sl + 1))
                pairs = [(wot[ec % 2][:, fc * 128:(fc + 1) * 128],
                          of[:, fc * TOK + t0: fc * TOK + t0 + 512]) for fc in range(NCH)]
                emit_mm_acc(p, pd[sl][:], pairs, wait=wait, inc=[(s_mm, 1)])
                dg_mm[(ec, sl)] = s_mm.n
            load_wo(ec + 2)
            if ec >= 1:
                emit_rows_a2(ec - 1)
            for sl in range(2):
                mmv = (s_mm, dg_mm[(ec, sl)])
                dst = xbf[sl][:, ec * 512:(ec + 1) * 512]
                xres = xo[:, ec * TOK + sl * 512: ec * TOK + sl * 512 + 512]
                p.emit("vector",
                       lambda e, sl=sl, dst=dst, xres=xres: e.tensor_add(dst, pd[sl][:], xres),
                       wait=[mmv], inc=[(s_ep, 1)])
                ep_after_stt[(ec, sl)] = s_ep.at()
                sqw = [ep_after_stt[(ec, sl)]]
                if ec >= 1:
                    sqw.append((s_mm, dg_mm[("rsq", ec - 1, sl)]))
                p.emit("scalar",
                       lambda e, sl=sl, ec=ec: e.square(sq[sl][:], xbf[sl][:, ec * 512:(ec + 1) * 512]),
                       wait=sqw, inc=[(s_si, 1)])
                si_after_sq[(ec, sl)] = s_si.at()
        emit_rows_a2(NCH - 1)

        for sl in range(2):
            wait_rows = [s_mm.at()]
            wait_bc_extra = [s_ap.at()] if sl == 1 else []
            apv = emit_ln_apply(p,
                          xbf_tile=lambda ec, sl=sl: xbf[sl][:, ec * 512:(ec + 1) * 512],
                          put_out=lambda ec, sl=sl: outsl[sl][:, ec * 512:(ec + 1) * 512],
                          rowx=rowx[sl], rowsq=rowsq[sl], bcr=bcr[:], bcb=bcb[:],
                          rows=rows, rhl=rhl, one1=one1[:], tmp32=tmp32,
                          s_row=s_row, s_bc=s_bc, s_ap=s_ap,
                          wait_rows=wait_rows, wait_bc_extra=wait_bc_extra)
            if out_writer is not None:
                for g4 in range(4):
                    out_writer(sl, outsl[sl], [(s_ap, apv[4 * g4 + 3])],
                               ec0=4 * g4, ec1=4 * g4 + 4)
            else:
                dst = dr["X1"][:].rearrange("p (c t) -> p c t", t=TOK)[:, :, sl * 512:(sl + 1) * 512]
                srcap = outsl[sl][:].rearrange("p (c t) -> p c t", t=512)
                p.dma("gpsimd", dst, srcap, s_st, wait=[s_ap.at()])

        p.emit("gpsimd", lambda e: e.nop(), wait=[s_st.at()])
    return s_ap.at()


def build_A2(wo_f32=False):
    nc = bass.Bass("TRN2", target_bir_lowering=False, debug=False)
    gdr = declare_globals(nc)
    dr = dict(
        OF=nc.dram_tensor("OF", [128, NCH * TOK], BF16, kind="ExternalInput").ap(),
        XO=nc.dram_tensor("XO", [128, NCH * TOK], BF16, kind="ExternalInput").ap(),
        WO=nc.dram_tensor("WO", [NCH, 128, NCH * 128], F32 if wo_f32 else BF16,
                          kind="ExternalInput").ap(),
        X1=nc.dram_tensor("X1", [128, NCH * TOK], BF16, kind="ExternalOutput").ap(),
    )
    with ExitStack() as ctx:
        p = Prog(nc, ctx)
        block = ctx.enter_context(nc.Block())
        g = emit_globals(p, nc, ctx, gdr, with_masks=False)
        emit_A2(p, nc, ctx, g, dr, wo_f32=wo_f32)
        p.finalize(block)
    return nc


def o_heads_to_full_fm(o_outs):
    full = np.empty((128, NCH, T), dtype=o_outs[0].dtype)
    for c in range(NCORE):
        o = np.asarray(o_outs[c]).reshape(128, 2, T)
        full[:, 2 * c, :] = o[:, 0, :]
        full[:, 2 * c + 1, :] = o[:, 1, :]
    return full.reshape(128, NCH * T)


def fm_full_slice_own(XFfm, c):
    return np.ascontiguousarray(
        XFfm.reshape(128, NCH, T)[:, :, c * TOK:(c + 1) * TOK].reshape(128, NCH * TOK))


def stage_A2(inp, o_full_fm, x_own_list):
    cst = stage_globals()
    wo_t = stage_lhsT_tiles(np.asarray(inp["L0_attn_Wo"]))
    maps = []
    for c in range(NCORE):
        maps.append(dict(
            OF=fm_full_slice_own(o_full_fm, c),
            XO=x_own_list[c],
            WO=wo_t,
            **cst,
        ))
    return maps


def emit_C0(p, nc, ctx, g, dr, x_ap=None, kr_dup_ap=None, slab_store=None, tag="c0"):
    WCKV, WCQ, WKR = dr["WCKV"], dr["WCQ"], dr["WKR"]
    CKV, CQ, KR = dr["CKV"], dr["CQ"], dr["KR"]
    if True:
        sb = lambda n, s, d=BF16: ctx.enter_context(nc.sbuf_tensor(tag + n, s, d))
        x = sb("x", [128, NCH * TOK]) if x_ap is None else x_ap
        wf = sb("wf", [128, 9 * NCH * 128], F32)
        wb = sb("wb", [128, 9 * NCH * 128])
        ckv = sb("ckv", [128, 4 * TOK])
        cq = sb("cq", [128, 4 * TOK])
        kr = sb("kr", [64, TOK])
        BNK = [ctx.enter_context(nc.psum_tensor(f"{tag}B{i}", [128, 512], F32)) for i in range(4)]

        s_ld = p.sem("ld"); s_cv = p.sem("cv"); s_mm = p.sem("mm")
        s_ev = p.sem("ev"); s_st = p.sem("st")

        if x_ap is None:
            p.dma("sync", x[:], dr["X2"][:, :], s_ld)
        p.dma("sync", wf[:, 0: 4 * NCH * 128].rearrange("p (a c) -> p a c", a=4),
              WCKV.rearrange("a p c -> p a c"), s_ld)
        p.dma("sync", wf[:, 4 * NCH * 128: 8 * NCH * 128].rearrange("p (a c) -> p a c", a=4),
              WCQ.rearrange("a p c -> p a c"), s_ld)
        p.dma("sync", wf[:, 8 * NCH * 128: 8 * NCH * 128 + NCH * 64], WKR[:, :], s_ld)
        ld0 = s_ld.at()
        used = 8 * NCH * 128 + NCH * 64
        half = used // 2
        p.emit("scalar", lambda e, half=half: e.copy(wb[:, 0:half], wf[:, 0:half]),
               wait=[ld0], inc=[(s_cv, 1)])
        p.emit("gpsimd", lambda e, half=half, used=used: e.tensor_copy(wb[:, half:used], wf[:, half:used]),
               wait=[ld0], inc=[(s_cv, 1)])
        cv = s_cv.at()

        bank_free = {}
        gi = 0
        for sl in range(2):
            t0 = sl * 512
            for ck in range(4):
                b = gi % 4
                w = [cv]
                if b in bank_free:
                    w.append((s_ev, bank_free[b]))
                pairs = [(wb[:, (ck * NCH + hc) * 128:(ck * NCH + hc + 1) * 128],
                          x[:, hc * TOK + t0: hc * TOK + t0 + 512]) for hc in range(NCH)]
                emit_mm_acc(p, BNK[b][:], pairs, wait=w, inc=[(s_mm, 1)])
                mmv = s_mm.at()
                p.emit("vector",
                       lambda e, b=b, ck=ck, t0=t0: e.tensor_copy(ckv[:, ck * TOK + t0: ck * TOK + t0 + 512], BNK[b][:]),
                       wait=[mmv, s_ev.at()], inc=[(s_ev, 1)])
                bank_free[b] = s_ev.n
                gi += 1
            for ck in range(4):
                b = gi % 4
                w = [cv]
                if b in bank_free:
                    w.append((s_ev, bank_free[b]))
                pairs = [(wb[:, ((4 + ck) * NCH + hc) * 128:((4 + ck) * NCH + hc + 1) * 128],
                          x[:, hc * TOK + t0: hc * TOK + t0 + 512]) for hc in range(NCH)]
                emit_mm_acc(p, BNK[b][:], pairs, wait=w, inc=[(s_mm, 1)])
                mmv = s_mm.at()
                p.emit("vector",
                       lambda e, b=b, ck=ck, t0=t0: e.tensor_copy(cq[:, ck * TOK + t0: ck * TOK + t0 + 512], BNK[b][:]),
                       wait=[mmv, s_ev.at()], inc=[(s_ev, 1)])
                bank_free[b] = s_ev.n
                gi += 1
            b = gi % 4
            w = [cv]
            if b in bank_free:
                w.append((s_ev, bank_free[b]))
            pairs = [(wb[:, 8 * NCH * 128 + hc * 64: 8 * NCH * 128 + (hc + 1) * 64],
                      x[:, hc * TOK + t0: hc * TOK + t0 + 512]) for hc in range(NCH)]
            emit_mm_acc(p, BNK[b][0:64, 0:512], pairs, wait=w, inc=[(s_mm, 1)])
            mmv = s_mm.at()
            p.emit("vector",
                   lambda e, b=b, t0=t0: e.tensor_copy(kr[0:64, t0:t0 + 512], BNK[b][0:64, 0:512]),
                   wait=[mmv, s_ev.at()], inc=[(s_ev, 1)])
            bank_free[b] = s_ev.n
            gi += 1
            if slab_store is not None:
                slab_store(sl, ckv, cq, kr, [s_ev.at()])

        if slab_store is None:
            p.dma("gpsimd", CKV[:, :], ckv[:], s_st, wait=[s_ev.at()])
            p.dma("gpsimd", CQ[:, :], cq[:], s_st, wait=[s_ev.at()])
            p.dma("gpsimd", KR[:, :], kr[:], s_st, wait=[s_ev.at()])
            if kr_dup_ap is not None:
                p.dma("gpsimd", kr_dup_ap, kr[:], s_st)
        p.emit("gpsimd", lambda e: e.nop(), wait=[s_st.at()])
    return s_st.at()


def declare_C0(nc, ckv_ap, cq_ap, kr_ap):
    return dict(
        WCKV=nc.dram_tensor("WCKV", [4, 128, NCH * 128], F32, kind="ExternalInput").ap(),
        WCQ=nc.dram_tensor("WCQ", [4, 128, NCH * 128], F32, kind="ExternalInput").ap(),
        WKR=nc.dram_tensor("WKR", [128, NCH * 64], F32, kind="ExternalInput").ap(),
        CKV=ckv_ap, CQ=cq_ap, KR=kr_ap,
    )


def build_C0():
    nc = bass.Bass("TRN2", target_bir_lowering=False, debug=False)
    gdr = declare_globals(nc)
    dr = declare_C0(nc,
                    nc.dram_tensor("CKV", [128, 4 * TOK], BF16, kind="ExternalOutput").ap(),
                    nc.dram_tensor("CQ", [128, 4 * TOK], BF16, kind="ExternalOutput").ap(),
                    nc.dram_tensor("KR", [64, TOK], BF16, kind="ExternalOutput").ap())
    dr["X2"] = nc.dram_tensor("X2", [128, NCH * TOK], BF16, kind="ExternalInput").ap()
    with ExitStack() as ctx:
        p = Prog(nc, ctx)
        block = ctx.enter_context(nc.Block())
        g = emit_globals(p, nc, ctx, gdr, with_masks=False)
        emit_C0(p, nc, ctx, g, dr)
        p.finalize(block)
    return nc


def permute_qr_cols(W):
    W = np.asarray(W)
    n = W.shape[-1] // 64
    Wr = W.reshape(W.shape[0], n, 32, 2)
    out = np.concatenate([Wr[:, :, :, 0], Wr[:, :, :, 1]], axis=2)
    return np.ascontiguousarray(out.reshape(W.shape[0], n * 64))


def stage_C0(inp, x2_own_list):
    wckv = stage_lhsT_tiles(np.asarray(inp["L1_attn_w_ckv"]))
    wcq = stage_lhsT_tiles(np.asarray(inp["L1_attn_w_cq"]))
    wkr_p = permute_qr_cols(np.asarray(inp["L1_attn_w_kr"]))
    wkr = np.ascontiguousarray(
        wkr_p.reshape(NCH, 128, 64).transpose(1, 0, 2).reshape(128, NCH * 64)).astype(np.float32)
    cst = stage_globals()
    maps = []
    for c in range(NCORE):
        maps.append(dict(X2=x2_own_list[c], WCKV=wckv, WCQ=wcq, WKR=wkr, **cst))
    return maps


SCALE_L1 = 192.0 ** -0.5

def emit_C1(p, nc, ctx, g, dr, ckv_src=None, cq_src=None, kr_src=None,
             o_store=None, lat_gate=None, tag="c1"):
    WKH, WVH, WQH, WQRH = dr["WKH"], dr["WVH"], dr["WQH"], dr["WQRH"]
    COS64, SIN64, PIT64, O = dr["COS64"], dr["SIN64"], dr["PIT64"], dr["O"]
    if ckv_src is None:
        ckv_src = lambda js: dr["CKVF"][:].rearrange("p (a t) -> p a t", a=4)[:, :, js * 512:(js + 1) * 512]
    if cq_src is None:
        cq_src = lambda js: dr["CQF"][:].rearrange("p (a t) -> p a t", a=4)[:, :, js * 512:(js + 1) * 512]
    if kr_src is None:
        kr_src = lambda js: dr["KRF"][0:64, js * 512:(js + 1) * 512]
    if True:
        sb = lambda n, s, d=BF16: ctx.enter_context(nc.sbuf_tensor(tag + n, s, d))

        ckvs = [sb(f"ckvs{i}", [128, 4 * 512]) for i in range(2)]
        cqs = [sb(f"cqs{i}", [128, 4 * 512]) for i in range(2)]
        wfh = sb("wfh", [128, 4 * (256 * 3 + 128)], F32)
        wbh = sb("wbh", [128, 4 * (256 * 3 + 128)])
        krr = sb("krr", [64, T])
        krs = [sb(f"krs{i}", [64, 512]) for i in range(2)]
        cos64 = [sb(f"cos64_{i}", [64, 512]) for i in range(2)]
        sin64 = [sb(f"sin64_{i}", [64, 512]) for i in range(2)]
        pit64 = sb("pit64", [64, 64])
        idt, masks, one1, onec, csts = g["idt"], g["masks"], g["one1"], g["onec"], g["csts"]
        kc = sb("kc", [128, T])
        vt = sb("vt", [128, T])
        qc = sb("qc", [128, T])
        qr = sb("qr", [64, T])
        obuf = qc
        praw = [sb(f"praw{i}", [128, 512]) for i in range(2)]
        rp1 = [sb(f"rp1{i}", [128, 512]) for i in range(2)]
        rp2 = [sb(f"rp2{i}", [128, 512]) for i in range(2)]
        pt = [sb(f"pt{i}", [128, 1024]) for i in range(2)]
        srow = sb("srow", [1, 512], F32)
        rrow = sb("rrow", [1, 512], F32)
        rhl = sb("rhl", [1, 2 * 512])
        rb = sb("rb", [128, 512], F32)
        acc = sb("acc", [128, 1024], F32)
        accbf = sb("accbf", [128, 1024])
        S2 = [ctx.enter_context(nc.psum_tensor(f"{tag}S{i}", [128, 1024], F32)) for i in range(2)]
        POB = [ctx.enter_context(nc.psum_tensor(f"{tag}P{i}", [128, 512], F32)) for i in range(2)]
        BCB = ctx.enter_context(nc.psum_tensor(tag + "C", [128, 512], F32))
        prt = ctx.enter_context(nc.psum_tensor(tag + "prt", [128, 512], BF16))
        BNK = [S2[0][:, 0:512], S2[0][:, 512:1024], S2[1][:, 0:512], S2[1][:, 512:1024],
               POB[0][:], POB[1][:], BCB[:]]

        s_ld = p.sem("ld")
        s_x1 = [p.sem("x10"), p.sem("x11")]
        s_x2 = [p.sem("x20"), p.sem("x21")]
        s_kr = [p.sem("kr0"), p.sem("kr1")]
        s_cv = p.sem("cv")
        s_mm = p.sem("mm")
        s_ev = p.sem("ev")
        s_exp = p.sem("exp")
        s_msk = p.sem("msk")
        s_nrm = p.sem("nrm")
        s_st = p.sem("st")

        WCOLS = 256 * 3 + 128
        for dst, srcap in ((wfh[:].rearrange("p (a c) -> p a c", a=4, c=WCOLS)[:, :, 0:256],
                            WKH.rearrange("a p c -> p a c")),
                           (wfh[:].rearrange("p (a c) -> p a c", a=4, c=WCOLS)[:, :, 256:512],
                            WVH.rearrange("a p c -> p a c")),
                           (wfh[:].rearrange("p (a c) -> p a c", a=4, c=WCOLS)[:, :, 512:768],
                            WQH.rearrange("a p c -> p a c")),
                           (wfh[:].rearrange("p (a c) -> p a c", a=4, c=WCOLS)[:, :, 768:896],
                            WQRH.rearrange("a p c -> p a c")),
                           (pit64[:], PIT64[:, :])):
            p.dma("sync", dst, srcap, s_ld)
        ld0 = s_ld.at()
        gld = g["ld"]
        p.emit("scalar", lambda e: e.copy(wbh[:, 0:1792], wfh[:, 0:1792]),
               wait=[ld0], inc=[(s_cv, 1)])
        p.emit("gpsimd", lambda e: e.tensor_copy(wbh[:, 1792:3584], wfh[:, 1792:3584]),
               wait=[ld0], inc=[(s_cv, 1)])
        cv = s_cv.at()

        def wslice(ck, base, lo, hi):
            return wbh[:, ck * WCOLS + base + lo: ck * WCOLS + base + hi]

        consumed_ckv = {}
        consumed_cq = {}
        consumed_kr = {}
        bank_free = {}
        s_acc = p.sem("acc")
        sfree = {0: 0, 1: 0}
        ptcons = {0: 0, 1: 0}
        acccons = {0: 0, 1: 0}
        pofree = {0: None, 1: None}
        bc_free = None
        psm = BCB[0:1, 0:512]
        nslab = [0]

        for h in range(2):
            def load_slab(js, h=h):
                if js >= 16:
                    return
                k = h * 16 + js
                b = k % 2
                gate = lat_gate(js) if lat_gate is not None else []
                wait = ([(s_mm, consumed_ckv[k - 2])] if k >= 2 else []) + gate
                p.dma("sync", ckvs[b][:].rearrange("p (a c) -> p a c", a=4),
                      ckv_src(js), s_x1[b], wait=wait)
                wait = ([(s_mm, consumed_cq[k - 2])] if k >= 2 else []) + gate
                p.dma("sync", cqs[b][:].rearrange("p (a c) -> p a c", a=4),
                      cq_src(js), s_x2[b], wait=wait)
                wait = ([(s_ev, consumed_kr[k - 2])] if k >= 2 else []) + gate
                p.dma("gpsimd", krs[b][0:64, :], kr_src(js), s_kr[b], wait=wait)
                p.dma("gpsimd", cos64[b][0:64, :], COS64[0:64, js * 512:(js + 1) * 512], s_kr[b], wait=wait)
                p.dma("gpsimd", sin64[b][0:64, :], SIN64[0:64, js * 512:(js + 1) * 512], s_kr[b], wait=wait)

            load_slab(0)
            load_slab(1)

            for js in range(16):
                k = h * 16 + js
                b = k % 2
                x1v = (s_x1[b], (k // 2 + 1) * 16)
                x2v = (s_x2[b], (k // 2 + 1) * 16)
                krv = (s_kr[b], (k // 2 + 1) * 48)
                t0 = js * 512
                if h == 0:
                    w = [cv, krv, s_ev.at()]
                    if "rh" in bank_free:
                        w.append((s_ev, bank_free["rh"]))
                    p.emit("tensor", lambda e, b=b: e.matmul(BNK[4][0:64, 0:512], pit64[:], krs[b][0:64, :],
                                                             start=True, stop=True, skip_group_check=True),
                           wait=w, inc=[(s_mm, 1)])
                    rhv = s_mm.at()
                    p.emit("gpsimd", lambda e, b=b: e.tensor_mul(rp1[0][0:64, :], krs[b][0:64, :], cos64[b][0:64, :]),
                           wait=[cv, krv, s_ev.at()], inc=[(s_ev, 1)])
                    p.emit("vector", lambda e, b=b: e.tensor_mul(rp2[0][0:64, :], BNK[4][0:64, 0:512], sin64[b][0:64, :]),
                           wait=[rhv, s_ev.at()], inc=[(s_ev, 1)])
                    bank_free["rh"] = s_ev.n
                    p.emit("vector", lambda e, t0=t0: e.tensor_add(krr[0:64, t0:t0 + 512], rp1[0][0:64, :], rp2[0][0:64, :]),
                           wait=[s_ev.at()], inc=[(s_ev, 1)])

                bk_ = (2 * js) % 4
                w = [cv, x1v, ld0, gld]
                if ("p", bk_) in bank_free:
                    w.append((s_ev, bank_free[("p", bk_)]))
                pairs = [(wslice(ck, 0, h * 128, (h + 1) * 128), ckvs[b][:, ck * 512:(ck + 1) * 512])
                         for ck in range(4)]
                emit_mm_acc(p, BNK[bk_], pairs, wait=w, inc=[(s_mm, 1)])
                mmv = s_mm.at()
                p.emit("vector", lambda e, t0=t0, bk_=bk_:
                       e.tensor_copy(kc[:, t0:t0 + 512], BNK[bk_]),
                       wait=[mmv, s_ev.at()], inc=[(s_ev, 1)])
                bank_free[("p", bk_)] = s_ev.n
                bv_ = (2 * js + 1) % 4
                w = [cv, x1v]
                if ("p", bv_) in bank_free:
                    w.append((s_ev, bank_free[("p", bv_)]))
                pairs = [(wslice(ck, 256, h * 128, (h + 1) * 128), ckvs[b][:, ck * 512:(ck + 1) * 512])
                         for ck in range(4)]
                emit_mm_acc(p, BNK[bv_], pairs, wait=w, inc=[(s_mm, 1)])
                mmv = s_mm.at()
                p.emit("scalar", lambda e, b=b, bv_=bv_: e.copy(praw[b][:], BNK[bv_]),
                       wait=[mmv, s_ev.at()], inc=[(s_ev, 1)])
                bank_free[("p", bv_)] = s_ev.n
                for jt in range(4):
                    w = [s_ev.at()]
                    if "prt" in bank_free:
                        w.append((s_ev, bank_free["prt"]))
                    p.emit("tensor",
                           lambda e, jt=jt, b=b: e.transpose(prt[:, 0:128], praw[b][:, jt * 128:(jt + 1) * 128], idt[:]),
                           wait=w, inc=[(s_mm, 1)])
                    mmt = s_mm.at()
                    p.emit("vector",
                           lambda e, jt=jt, t0=t0: e.tensor_copy(vt[:, t0 + jt * 128: t0 + (jt + 1) * 128], prt[:, 0:128]),
                           wait=[mmt, s_ev.at()], inc=[(s_ev, 1)])
                    bank_free["prt"] = s_ev.n
                bq_ = (2 * js) % 4
                w = [cv, x2v]
                if ("p", bq_) in bank_free:
                    w.append((s_ev, bank_free[("p", bq_)]))
                pairs = [(wslice(ck, 512, h * 128, (h + 1) * 128), cqs[b][:, ck * 512:(ck + 1) * 512])
                         for ck in range(4)]
                emit_mm_acc(p, BNK[bq_], pairs, wait=w, inc=[(s_mm, 1)])
                mmv = s_mm.at()
                p.emit("vector", lambda e, t0=t0, bq_=bq_:
                       e.tensor_copy(qc[:, t0:t0 + 512], BNK[bq_]),
                       wait=[mmv, s_ev.at()], inc=[(s_ev, 1)])
                bank_free[("p", bq_)] = s_ev.n
                consumed_ckv[k] = s_mm.n
                bqr = (2 * js + 1) % 4
                w = [cv, x2v]
                if ("p", bqr) in bank_free:
                    w.append((s_ev, bank_free[("p", bqr)]))
                pairs = [(wslice(ck, 768, h * 64, (h + 1) * 64), cqs[b][:, ck * 512:(ck + 1) * 512])
                         for ck in range(4)]
                emit_mm_acc(p, BNK[bqr][0:64, 0:512], pairs, wait=w, inc=[(s_mm, 1)])
                mmv = s_mm.at()
                consumed_cq[k] = s_mm.n
                p.emit("scalar", lambda e, b=b, bqr=bqr: e.copy(praw[b][0:64, :], BNK[bqr][0:64, 0:512]),
                       wait=[mmv, s_ev.at()], inc=[(s_ev, 1)])
                bank_free[("p", bqr)] = s_ev.n
                w = [s_ev.at()]
                if "rh" in bank_free:
                    w.append((s_ev, bank_free["rh"]))
                p.emit("tensor", lambda e, b=b: e.matmul(BNK[4][0:64, 0:512], pit64[:], praw[b][0:64, :],
                                                         start=True, stop=True, skip_group_check=True),
                       wait=w, inc=[(s_mm, 1)])
                rhv = s_mm.at()
                p.emit("gpsimd", lambda e, b=b: e.tensor_mul(rp1[1][0:64, :], praw[b][0:64, :], cos64[b][0:64, :]),
                       wait=[krv, s_ev.at()], inc=[(s_ev, 1)])
                p.emit("vector", lambda e, b=b: e.tensor_mul(rp2[1][0:64, :], BNK[4][0:64, 0:512], sin64[b][0:64, :]),
                       wait=[rhv, s_ev.at()], inc=[(s_ev, 1)])
                bank_free["rh"] = s_ev.n
                p.emit("vector", lambda e, t0=t0: e.tensor_add(qr[0:64, t0:t0 + 512], rp1[1][0:64, :], rp2[1][0:64, :]),
                       wait=[s_ev.at()], inc=[(s_ev, 1)])
                consumed_kr[k] = s_ev.n
                load_slab(js + 2)

            barrier = s_ev.at()
            barrier_g = gld
            for gq in range(16):
                pi = h * 16 + gq
                jn = 4 * (gq + 1)
                nu = jn // 2
                po = POB[pi % 2][:]
                qcap = qc[:, gq * 512:(gq + 1) * 512]
                qrap = qr[0:64, gq * 512:(gq + 1) * 512]

                def emit_S(u, qcap=qcap, qrap=qrap, barrier=barrier):
                    w = [barrier, barrier_g, (s_exp, sfree[u % 2])]
                    for hf in range(2):
                        jt = 2 * u + hf
                        dst = S2[u % 2][:, hf * 512:(hf + 1) * 512]
                        p.emit("tensor",
                               lambda e, jt=jt, dst=dst, qcap=qcap:
                                   e.matmul(dst, kc[:, jt * 128:(jt + 1) * 128],
                                            qcap, start=True, stop=False, skip_group_check=True),
                               wait=(w if hf == 0 else ()), inc=[(s_mm, 1)])
                        p.emit("tensor",
                               lambda e, jt=jt, dst=dst, qrap=qrap:
                                   e.matmul(dst, krr[0:64, jt * 128:(jt + 1) * 128],
                                            qrap, start=False, stop=True, skip_group_check=True),
                               inc=[(s_mm, 1)])
                    return s_mm.n

                sv = emit_S(0)
                for u in range(nu):
                    i0 = 2 * u - 4 * gq
                    if i0 >= 0:
                        p.emit("vector",
                               lambda e, u=u, i0=i0: e.tensor_add(S2[u % 2][:], S2[u % 2][:],
                                                                  masks[:, i0 * 512:(i0 + 2) * 512]),
                               wait=[(s_mm, sv), s_msk.at()], inc=[(s_msk, 1)])
                        ewait = [(s_msk, s_msk.n)]
                    else:
                        ewait = [(s_mm, sv)]
                    p.emit("scalar",
                           lambda e, u=u: e.activation(pt[u % 2][:], S2[u % 2][:], AF.Exp,
                                                       bias=float(EXP_SHIFT), scale=float(SCALE_L1)),
                           wait=ewait + [(s_mm, ptcons[u % 2]), (s_acc, acccons[u % 2])],
                           inc=[(s_exp, 1)])
                    sfree[u % 2] = s_exp.n
                    ev = s_exp.n
                    if u + 1 < nu:
                        sv = emit_S(u + 1)
                    w = [(s_exp, ev)]
                    if u == 0 and pofree[pi % 2] is not None:
                        w.append((s_nrm, pofree[pi % 2]))
                    for hf in range(2):
                        jt = 2 * u + hf
                        p.emit("tensor",
                               lambda e, jt=jt, u=u, hf=hf, po=po: e.matmul(
                                   po, vt[:, jt * 128:(jt + 1) * 128],
                                   pt[u % 2][:, hf * 512:(hf + 1) * 512],
                                   start=(jt == 0), stop=(jt == jn - 1), skip_group_check=True),
                               wait=(w if hf == 0 else ()), inc=[(s_mm, 1)])
                    ptcons[u % 2] = s_mm.n
                    if u == 0:
                        p.emit("vector", lambda e, u=u: e.tensor_copy(acc[:], pt[u % 2][:]),
                               wait=[(s_exp, ev), s_acc.at()], inc=[(s_acc, 1)])
                    else:
                        p.emit("vector", lambda e, u=u: e.tensor_add(acc[:], acc[:], pt[u % 2][:]),
                               wait=[(s_exp, ev), s_acc.at()], inc=[(s_acc, 1)])
                    acccons[u % 2] = s_acc.n

                p.emit("vector", lambda e: e.tensor_copy(accbf[:], acc[:]),
                       wait=[s_acc.at()], inc=[(s_acc, 1)])
                w = [s_acc.at()]
                if bc_free is not None:
                    w.append((s_nrm, bc_free))
                emit_mm_acc(p, psm, [(onec[:], accbf[:, 0:512]), (onec[:], accbf[:, 512:1024])],
                            wait=w, inc=[(s_mm, 1)])
                nv = s_mm.at()
                p.emit("vector", lambda e, psm=psm: e.tensor_copy(srow[:], psm),
                       wait=[nv, s_nrm.at()], inc=[(s_nrm, 1)])
                p.emit("vector", lambda e: e.reciprocal(rrow[:], srow[:]),
                       wait=[s_nrm.at()], inc=[(s_nrm, 1)])
                p.emit("vector", lambda e: e.tensor_copy(rhl[:, 0:512], rrow[:]),
                       wait=[s_nrm.at()], inc=[(s_nrm, 1)])
                p.emit("vector", lambda e: e.tensor_sub(srow[:], rrow[:], rhl[:, 0:512]),
                       wait=[s_nrm.at()], inc=[(s_nrm, 1)])
                p.emit("vector", lambda e: e.tensor_copy(rhl[:, 512:1024], srow[:]),
                       wait=[s_nrm.at()], inc=[(s_nrm, 1)])
                emit_mm_acc(p, BCB[:], [(one1[:], rhl[:, 0:512]), (one1[:], rhl[:, 512:1024])],
                            wait=[s_nrm.at()], inc=[(s_mm, 1)])
                bcv = s_mm.at()
                p.emit("vector", lambda e: e.tensor_copy(rb[:], BCB[:]),
                       wait=[bcv, s_nrm.at()], inc=[(s_nrm, 1)])
                bc_free = s_nrm.n
                dst = obuf[:, gq * 512:(gq + 1) * 512]
                p.emit("vector",
                       lambda e, dst=dst, po=po: e.scalar_tensor_tensor(dst, po, 1.0, rb[:],
                                                                        ALU.mult, ALU.mult),
                       wait=[s_nrm.at()], inc=[(s_nrm, 1)])
                pofree[pi % 2] = s_nrm.n
                if o_store is not None and h == 1 and gq % 2 == 1:
                    k = (gq - 1) // 2
                    o_store(h, obuf[:, k * TOK:(k + 1) * TOK], [s_nrm.at()], chunk=k)

            if o_store is not None:
                if h == 0:
                    stw = o_store(h, obuf[:], [s_nrm.at()])
            else:
                p.dma("gpsimd", O[:, h * T:(h + 1) * T], obuf[:], s_st, wait=[s_nrm.at()])
                stw = (s_st, s_st.n)
            if h == 0:
                stv0 = stw
                consumed_ckv[15] = s_mm.n
                consumed_cq[15] = s_mm.n
                consumed_kr[15] = s_ev.n
                p.emit("sync", lambda e: e.nop(), wait=[stv0])
                p.emit("tensor", lambda e: e.nop(), wait=[stv0])
                p.emit("vector", lambda e: e.nop(), wait=[stv0])
                p.emit("scalar", lambda e: e.nop(), wait=[stv0])
                p.emit("gpsimd", lambda e: e.nop(), wait=[stv0])

        p.emit("gpsimd", lambda e: e.nop(), wait=[s_st.at()])
    return s_st.at()


def declare_C1(nc, o_ap):
    return dict(
        WKH=nc.dram_tensor("WKH", [4, 128, 256], F32, kind="ExternalInput").ap(),
        WVH=nc.dram_tensor("WVH", [4, 128, 256], F32, kind="ExternalInput").ap(),
        WQH=nc.dram_tensor("WQH", [4, 128, 256], F32, kind="ExternalInput").ap(),
        WQRH=nc.dram_tensor("WQRH", [4, 128, 128], F32, kind="ExternalInput").ap(),
        COS64=nc.dram_tensor("COS64", [64, T], BF16, kind="ExternalInput").ap(),
        SIN64=nc.dram_tensor("SIN64", [64, T], BF16, kind="ExternalInput").ap(),
        PIT64=nc.dram_tensor("PIT64", [64, 64], BF16, kind="ExternalInput").ap(),
        O=o_ap,
    )


def build_C1():
    nc = bass.Bass("TRN2", target_bir_lowering=False, debug=False)
    gdr = declare_globals(nc)
    dr = declare_C1(nc, nc.dram_tensor("O", [128, 2 * T], BF16, kind="ExternalOutput").ap())
    dr["CKVF"] = nc.dram_tensor("CKVF", [128, 4 * T], BF16, kind="ExternalInput").ap()
    dr["CQF"] = nc.dram_tensor("CQF", [128, 4 * T], BF16, kind="ExternalInput").ap()
    dr["KRF"] = nc.dram_tensor("KRF", [64, T], BF16, kind="ExternalInput").ap()
    with ExitStack() as ctx:
        p = Prog(nc, ctx)
        block = ctx.enter_context(nc.Block())
        g = emit_globals(p, nc, ctx, gdr)
        emit_C1(p, nc, ctx, g, dr)
        p.finalize(block)
    return nc


def stage_C1(inp, ckv_full, cq_full, kr_full):
    cst = stage_globals()
    cos, sin = rope_tables_l1()
    pit = perm_rh_T(64)
    wk_t = stage_lhsT_tiles(np.asarray(inp["L1_attn_w_k"]))
    wv_t = stage_lhsT_tiles(np.asarray(inp["L1_attn_w_v"]))
    wq_t = stage_lhsT_tiles(np.asarray(inp["L1_attn_w_q"]))
    wqr_p = permute_qr_cols(np.asarray(inp["L1_attn_w_qr"]))
    wqr_t = stage_lhsT_tiles(wqr_p)
    maps = []
    for c in range(NCORE):
        h0 = 2 * c
        def two_heads(wt):
            a = wt[h0].reshape(128, 4, 128)
            bq = wt[h0 + 1].reshape(128, 4, 128)
            out = np.stack([a, bq], axis=3)
            return np.ascontiguousarray(out.transpose(1, 0, 3, 2).reshape(4, 128, 256))
        def two_heads2(wt):
            a = wt[h0].reshape(128, 4, 128).transpose(1, 0, 2)
            bq = wt[h0 + 1].reshape(128, 4, 128).transpose(1, 0, 2)
            return np.ascontiguousarray(np.concatenate([a, bq], axis=2))
        wqr2 = np.concatenate([
            wqr_t[h0 // 2].reshape(128, 4, 128).transpose(1, 0, 2)[:, :, (h0 % 2) * 64:(h0 % 2) * 64 + 64]
            if False else None for _ in range(0)], axis=0) if False else None
        def qr_heads():
            out = np.zeros((4, 128, 128), dtype=np.float32)
            for i, h in enumerate((h0, h0 + 1)):
                oc = (h * 64) // 128
                off = (h * 64) % 128
                tile = wqr_t[oc].reshape(128, 4, 128).transpose(1, 0, 2)
                out[:, :, i * 64:(i + 1) * 64] = tile[:, :, off:off + 64]
            return np.ascontiguousarray(out)
        maps.append(dict(
            CKVF=ckv_full, CQF=cq_full, KRF=kr_full,
            WKH=two_heads2(wk_t), WVH=two_heads2(wv_t), WQH=two_heads2(wq_t),
            WQRH=qr_heads(),
            COS64=cos, SIN64=sin, PIT64=pit, **cst,
        ))
    return maps


def ckv_own_to_full(outs, nch4=4):
    full = np.empty((128, nch4, T), dtype=outs[0].dtype)
    for c in range(NCORE):
        full[:, :, c * TOK:(c + 1) * TOK] = np.asarray(outs[c]).reshape(128, nch4, TOK)
    return np.ascontiguousarray(full.reshape(128, nch4 * T))


def kr_own_to_full(outs):
    full = np.empty((64, T), dtype=outs[0].dtype)
    for c in range(NCORE):
        full[:, c * TOK:(c + 1) * TOK] = outs[c]
    return np.ascontiguousarray(full)


def emit_D(p, nc, ctx, g, dr, x_ap=None, tag="d"):
    WB, ROUT, ACAT, BSTK, EMAT, X4 = dr["WB"], dr["ROUT"], dr["ACAT"], dr["BSTK"], dr["EMAT"], dr["X4"]
    if True:
        sb = lambda n, s, d=BF16: ctx.enter_context(nc.sbuf_tensor(tag + n, s, d))
        x = sb("x", [128, NCH * TOK]) if x_ap is None else x_ap
        wf = [sb(f"wf{i}", [128, NCH * 128], F32) for i in range(2)]
        wb = [sb(f"wb{i}", [128, NCH * 128]) for i in range(3)]
        acat = sb("acat", [128, NCH * 128])
        bstk = sb("bstk", [128, NCH * 128])
        rout = sb("rout", [128, NCH * 8])
        acatf = sb("acatf", [128, NCH * 128], F32)
        bstkf = sb("bstkf", [128, NCH * 128], F32)
        routf = sb("routf", [128, NCH * 8], F32)
        emat = sb("emat", [8, 128])
        idt, one1, onec, csts = g["idt"], g["one1"], g["onec"], g["csts"]
        wfm8 = sb("wfm8", [8, TOK])
        wexp = sb("wexp", [128, TOK])
        zbuf = sb("zbuf", [128, TOK])
        wdns = sb("wdns", [128, 8], F32)
        wdb = sb("wdb", [128, 8])
        t8a = sb("t8a", [128, 8], F32)
        t8b = sb("t8b", [128, 8], F32)
        m1 = sb("m1", [128, 1], F32)
        m2 = sb("m2", [128, 1], F32)
        w1 = sb("w1", [128, 1], F32)
        w2 = sb("w2", [128, 1], F32)
        xbf = [sb(f"xbf{i}", [128, NCH * 512]) for i in range(2)]
        sq = [sb(f"sq{i}", [128, 512]) for i in range(2)]
        rows = sb("rows", [1, 3 * 512], F32)
        rhl = sb("rhl", [1, 4 * 512])
        tmp32 = sb("tmp32", [128, 512], F32)
        outsl = [sb(f"outsl{i}", [128, NCH * 512], F32) for i in range(2)]

        BNK = [ctx.enter_context(nc.psum_tensor(f"{tag}B{i}", [128, 512], F32)) for i in range(8)]
        pd = [BNK[0], BNK[1]]
        rowx = [BNK[2][0:1, 0:512], BNK[6][0:1, 0:512]]
        rowsq = [BNK[3][0:1, 0:512], BNK[7][0:1, 0:512]]
        bcr, bcb = BNK[4], BNK[5]
        prt8 = BNK[4][:].bitcast(BF16)

        s_ld = p.sem("ld"); s_wl = [p.sem("w0"), p.sem("w1")]
        s_cva = p.sem("cva"); s_cvb = p.sem("cvb"); s_cvs = p.sem("cvs")
        s_mm = p.sem("mm"); s_si = p.sem("si"); s_ep = p.sem("ep")
        s_rt = p.sem("rt"); s_row = p.sem("row"); s_bc = p.sem("bc")
        s_ap = p.sem("ap"); s_st = p.sem("st")

        if x_ap is None:
            p.dma("sync", x[:], dr["X3"][:, :], s_ld)
        p.dma("sync", emat[:], EMAT[:, :], s_ld)
        ld0 = s_ld.at()
        gld = g["ld"]
        s_lsm = p.sem("lsm")
        p.dma("sync", acatf[:], ACAT[:, :], s_lsm)
        p.dma("sync", bstkf[:].rearrange("p (a c) -> p a c", a=NCH),
              BSTK.rearrange("a p c -> p a c"), s_lsm)
        p.dma("sync", routf[:].rearrange("p (a c) -> p a c", a=NCH),
              ROUT.rearrange("a p c -> p a c"), s_lsm)
        lsm = (s_lsm, 48)
        p.emit("scalar", lambda e: e.copy(acat[:], acatf[:]), wait=[lsm], inc=[(s_cvs, 1)])
        p.emit("gpsimd", lambda e: e.tensor_copy(bstk[:], bstkf[:]), wait=[lsm], inc=[(s_cvs, 1)])
        p.emit("vector", lambda e: e.tensor_copy(rout[:], routf[:]), wait=[lsm], inc=[(s_cvs, 1)])
        cvs = s_cvs.at()

        consumed_mm = {}
        emitted_w = set()
        def record_wload(wi):
            if wi >= NCH or wi in emitted_w:
                return
            emitted_w.add(wi)
            fb = wi % 2
            buf = wi % 3
            wait = []
            if wi >= 2:
                wait.append((s_cva, wi - 1))
                wait.append((s_cvb, wi - 1))
            if wi >= 3:
                wait.append((s_mm, consumed_mm[wi - 3]))
            p.dma("sync", wf[fb][:], WB[wi, :, :], s_wl[fb], wait=wait)
            wv = (s_wl[fb], (wi // 2 + 1) * 16)
            cw = [wv] + ([(s_mm, consumed_mm[wi - 3])] if wi >= 3 else [])
            p.emit("scalar", lambda e, b=buf, f=fb: e.copy(wb[b][:, 0:1024], wf[f][:, 0:1024]),
                   wait=cw, inc=[(s_cva, 1)])
            p.emit("gpsimd", lambda e, b=buf, f=fb: e.tensor_copy(wb[b][:, 1024:2048], wf[f][:, 1024:2048]),
                   wait=cw, inc=[(s_cvb, 1)])

        for wi in range(3):
            record_wload(wi)

        bank_rt = {}
        for qt in range(8):
            t0 = qt * 128
            bnkid = 2 + qt % 2
            bnk = BNK[bnkid]
            w = [cvs, ld0, gld]
            if bnkid in bank_rt:
                w.append((s_rt, bank_rt[bnkid]))
            pairs = [(x[:, hc * TOK + t0: hc * TOK + t0 + 128], rout[:, hc * 8:(hc + 1) * 8])
                     for hc in range(NCH)]
            emit_mm_acc(p, bnk[0:128, 0:8], pairs, wait=w, inc=[(s_mm, 1)])
            mmv = s_mm.at()
            def rstep(fn, extra=()):
                p.emit("vector", fn, wait=[s_rt.at()] + list(extra), inc=[(s_rt, 1)])
            rstep(lambda e, bnk=bnk: e.reduce_max(m1[:], bnk[0:128, 0:8], axis=AX.X), [mmv])
            rstep(lambda e, bnk=bnk: e.tensor_scalar(t8a[:], bnk[0:128, 0:8], m1[:], None, ALU.is_equal))
            rstep(lambda e, bnk=bnk: e.scalar_tensor_tensor(t8b[:], t8a[:], -30000.0, bnk[0:128, 0:8], ALU.mult, ALU.add))
            bank_rt[bnkid] = s_rt.n
            rstep(lambda e: e.reduce_max(m2[:], t8b[:], axis=AX.X))
            rstep(lambda e: e.tensor_scalar(t8b[:], t8b[:], m2[:], None, ALU.is_equal))
            rstep(lambda e: e.tensor_sub(w1[:], m2[:], m1[:]))
            p.emit("scalar", lambda e: e.activation(w2[:], w1[:], AF.Exp),
                   wait=[s_rt.at()], inc=[(s_rt, 1)])
            rstep(lambda e: e.tensor_scalar_add(w2[:], w2[:], 1.0))
            rstep(lambda e: e.reciprocal(w1[:], w2[:]))
            rstep(lambda e: e.tensor_scalar(w2[:], w1[:], -1.0, 1.0, ALU.mult, ALU.add))
            rstep(lambda e: e.tensor_scalar_mul(t8a[:], t8a[:], w1[:]))
            rstep(lambda e: e.scalar_tensor_tensor(wdns[:], t8b[:], w2[:], t8a[:], ALU.mult, ALU.add))
            rstep(lambda e: e.tensor_copy(wdb[:], wdns[:]))
            w = [s_rt.at()]
            if "prt8" in bank_rt:
                w.append((s_rt, bank_rt["prt8"]))
            p.emit("tensor", lambda e: e.transpose(prt8[0:8, 0:128], wdb[:], idt[:]),
                   wait=w, inc=[(s_mm, 1)])
            mmt = s_mm.at()
            p.emit("vector", lambda e, t0=t0: e.tensor_copy(wfm8[0:8, t0:t0 + 128], prt8[0:8, 0:128]),
                   wait=[mmt, s_rt.at()], inc=[(s_rt, 1)])
            bank_rt["prt8"] = s_rt.n
        rt_done = s_rt.at()

        for sl in range(2):
            t0 = sl * 512
            bnkid = 2 + sl
            w = [rt_done, (s_rt, bank_rt[bnkid])] if bnkid in bank_rt else [rt_done]
            p.emit("tensor", lambda e, sl=sl, t0=t0, bnkid=bnkid:
                   e.matmul(BNK[bnkid][:], emat[:], wfm8[0:8, t0:t0 + 512],
                            start=True, stop=True, skip_group_check=True),
                   wait=w, inc=[(s_mm, 1)])
            mmv = s_mm.at()
            p.emit("vector", lambda e, sl=sl, t0=t0, bnkid=bnkid:
                   e.tensor_copy(wexp[:, t0:t0 + 512], BNK[bnkid][:]),
                   wait=[mmv, s_rt.at()], inc=[(s_rt, 1)])
        for sl in range(2):
            t0 = sl * 512
            bnkid = 6 + sl
            pairs = [(acat[:, hc * 128:(hc + 1) * 128], x[:, hc * TOK + t0: hc * TOK + t0 + 512])
                     for hc in range(NCH)]
            emit_mm_acc(p, BNK[bnkid][:], pairs, wait=[cvs, s_rt.at()], inc=[(s_mm, 1)])
            mmv = s_mm.at()
            p.emit("vector", lambda e, sl=sl, t0=t0, bnkid=bnkid:
                   e.scalar_tensor_tensor(zbuf[:, t0:t0 + 512], BNK[bnkid][:], 2.0,
                                          wexp[:, t0:t0 + 512], ALU.mult, ALU.mult),
                   wait=[mmv, s_rt.at()], inc=[(s_rt, 1)])
        z_done = s_rt.at()

        dg_mm = {}
        ep_after_stt = {}
        si_after_sq = {}
        def emit_rows_d(ec):
            for sl in range(2):
                emit_mm_acc(p, rowx[sl], [(onec[:], xbf[sl][:, ec * 512:(ec + 1) * 512])],
                            wait=[ep_after_stt[(ec, sl)]], inc=[(s_mm, 1)],
                            start=(ec == 0), stop=(ec == NCH - 1))
                emit_mm_acc(p, rowsq[sl], [(onec[:], sq[sl][:])],
                            wait=[si_after_sq[(ec, sl)]], inc=[(s_mm, 1)],
                            start=(ec == 0), stop=(ec == NCH - 1))
                dg_mm[("rsq", ec, sl)] = s_mm.n
        for ec in range(NCH):
            for sl in range(2):
                t0 = sl * 512
                wait = [(s_cva, ec + 1), (s_cvb, ec + 1), z_done]
                if ec >= 1:
                    wait.append((s_ep, 2 * (ec - 1) + sl + 1))
                buf = ec % 3
                pairs = [(wb[buf][:, hc * 128:(hc + 1) * 128],
                          x[:, hc * TOK + t0: hc * TOK + t0 + 512]) for hc in range(NCH)] + \
                        [(bstk[:, ec * 128:(ec + 1) * 128], zbuf[:, t0:t0 + 512])]
                emit_mm_acc(p, pd[sl][:], pairs, wait=wait, inc=[(s_mm, 1)])
                dg_mm[(ec, sl)] = s_mm.n
                if sl == 1:
                    consumed_mm[ec] = s_mm.n
            record_wload(ec + 3)
            if ec >= 1:
                emit_rows_d(ec - 1)
            for sl in range(2):
                mmv = (s_mm, dg_mm[(ec, sl)])
                dst = xbf[sl][:, ec * 512:(ec + 1) * 512]
                xres = x[:, ec * TOK + sl * 512: ec * TOK + sl * 512 + 512]
                p.emit("vector",
                       lambda e, sl=sl, dst=dst, xres=xres: e.tensor_add(dst, pd[sl][:], xres),
                       wait=[mmv], inc=[(s_ep, 1)])
                ep_after_stt[(ec, sl)] = s_ep.at()
                sqw = [ep_after_stt[(ec, sl)]]
                if ec >= 1:
                    sqw.append((s_mm, dg_mm[("rsq", ec - 1, sl)]))
                p.emit("scalar",
                       lambda e, sl=sl, ec=ec: e.square(sq[sl][:], xbf[sl][:, ec * 512:(ec + 1) * 512]),
                       wait=sqw, inc=[(s_si, 1)])
                si_after_sq[(ec, sl)] = s_si.at()

        emit_rows_d(NCH - 1)

        for sl in range(2):
            wait_rows = [s_mm.at()]
            wait_bc_extra = [s_ap.at()] if sl == 1 else []
            apv = emit_ln_apply(p,
                          xbf_tile=lambda ec, sl=sl: xbf[sl][:, ec * 512:(ec + 1) * 512],
                          put_out=lambda ec, sl=sl: outsl[sl][:, ec * 512:(ec + 1) * 512],
                          rowx=rowx[sl], rowsq=rowsq[sl], bcr=bcr[:], bcb=bcb[:],
                          rows=rows, rhl=rhl, one1=one1[:], tmp32=tmp32,
                          s_row=s_row, s_bc=s_bc, s_ap=s_ap,
                          wait_rows=wait_rows, wait_bc_extra=wait_bc_extra)
            for g4 in range(4):
                dst = X4[:].rearrange("p (c t) -> p c t", t=TOK)[
                    :, 4 * g4:4 * g4 + 4, sl * 512:(sl + 1) * 512]
                srcg = outsl[sl][:].rearrange("p (c t) -> p c t", t=512)[:, 4 * g4:4 * g4 + 4, :]
                p.dma("gpsimd", dst, srcg, s_st, wait=[(s_ap, apv[4 * g4 + 3])])

        p.emit("gpsimd", lambda e: e.nop(), wait=[s_st.at()])
    return s_st.at()


def declare_D(nc, x4_ap):
    return dict(
        WB=nc.dram_tensor("WB", [NCH, 128, NCH * 128], F32, kind="ExternalInput").ap(),
        ROUT=nc.dram_tensor("ROUT", [NCH, 128, 8], F32, kind="ExternalInput").ap(),
        ACAT=nc.dram_tensor("ACAT", [128, NCH * 128], F32, kind="ExternalInput").ap(),
        BSTK=nc.dram_tensor("BSTK", [NCH, 128, 128], F32, kind="ExternalInput").ap(),
        EMAT=nc.dram_tensor("EMAT", [8, 128], BF16, kind="ExternalInput").ap(),
        X4=x4_ap,
    )


def build_D():
    nc = bass.Bass("TRN2", target_bir_lowering=False, debug=False)
    gdr = declare_globals(nc)
    dr = declare_D(nc, nc.dram_tensor("X4", [128, NCH * TOK], F32, kind="ExternalOutput").ap())
    dr["X3"] = nc.dram_tensor("X3", [128, NCH * TOK], BF16, kind="ExternalInput").ap()
    with ExitStack() as ctx:
        p = Prog(nc, ctx)
        block = ctx.enter_context(nc.Block())
        g = emit_globals(p, nc, ctx, gdr)
        emit_D(p, nc, ctx, g, dr)
        p.finalize(block)
    return nc


def stage_D(inp, x3_own_list):
    cst = stage_globals()
    wbt = stage_lhsT_tiles(np.asarray(inp["L1_ffn_weight"]).T)
    routT = np.ascontiguousarray(np.asarray(inp["L1_ffn_router_w"]).T)
    rout = np.ascontiguousarray(routT.reshape(NCH, 128, 8))
    A = np.asarray(inp["L1_ffn_lora_A"])
    acatT = np.ascontiguousarray(A.reshape(128, 2048).T)
    acat = stage_lhsT_tiles(acatT)[0]
    Bm = np.asarray(inp["L1_ffn_lora_B"])
    bstack = np.ascontiguousarray(Bm.transpose(0, 2, 1).reshape(128, 2048))
    bstk = np.ascontiguousarray(
        bstack.reshape(128, NCH, 128).transpose(1, 0, 2))
    ematv = np.zeros((8, 128), dtype=BF)
    for e_ in range(8):
        ematv[e_, e_ * 16:(e_ + 1) * 16] = 1
    maps = []
    for c in range(NCORE):
        maps.append(dict(X3=x3_own_list[c], WB=wbt, ROUT=rout, ACAT=acat, BSTK=bstk,
                         EMAT=ematv, **cst))
    return maps


def stage_C2(inp, o_full_fm, x_own_list):
    cst = stage_globals()
    wo_t = np.ascontiguousarray(stage_lhsT_tiles(np.asarray(inp["L1_attn_w_o"])).astype(np.float32))
    maps = []
    for c in range(NCORE):
        maps.append(dict(
            OF=fm_full_slice_own(o_full_fm, c),
            XO=x_own_list[c],
            WO=wo_t,
            **cst,
        ))
    return maps


def kernel_unfused(**inputs):
    inp = {k: np.asarray(v) for k, v in inputs.items()}
    cores = list(range(NCORE))

    nc = build_A1()
    res = run_bass_kernel_spmd(nc, stage_A1(inp), core_ids=cores)
    o_full = o_heads_to_full_fm([r["O"] for r in res.results])

    x = inp["L0_attn_x"][0]
    XF = np.ascontiguousarray(fm(x).reshape(NCH, 128, T).transpose(1, 0, 2).reshape(128, NCH * T))
    x_own = [fm_full_slice_own(XF, c) for c in cores]
    nc = build_A2()
    res = run_bass_kernel_spmd(nc, stage_A2(inp, o_full, x_own), core_ids=cores)
    x1_own = [np.asarray(r["X1"]) for r in res.results]

    nc = build_B()
    res = run_bass_kernel_spmd(nc, stage_B(inp, x1_own), core_ids=cores)
    x2_own = [np.asarray(r["XOUT"]) for r in res.results]

    nc = build_C0()
    res = run_bass_kernel_spmd(nc, stage_C0(inp, x2_own), core_ids=cores)
    ckv_full = ckv_own_to_full([np.asarray(r["CKV"]) for r in res.results])
    cq_full = ckv_own_to_full([np.asarray(r["CQ"]) for r in res.results])
    kr_full = kr_own_to_full([np.asarray(r["KR"]) for r in res.results])

    nc = build_C1()
    res = run_bass_kernel_spmd(nc, stage_C1(inp, ckv_full, cq_full, kr_full), core_ids=cores)
    o1_full = o_heads_to_full_fm([r["O"] for r in res.results])

    nc = build_A2(wo_f32=True)
    res = run_bass_kernel_spmd(nc, stage_C2(inp, o1_full, x2_own), core_ids=cores)
    x3_own = [np.asarray(r["X1"]) for r in res.results]

    nc = build_D()
    res = run_bass_kernel_spmd(nc, stage_D(inp, x3_own), core_ids=cores)
    x4 = np.empty((NCH, 128, T), dtype=np.float32)
    for c in cores:
        x4[:, :, c * TOK:(c + 1) * TOK] = np.asarray(
            res.results[c]["X4"]).reshape(128, NCH, TOK).transpose(1, 0, 2)
    out = x4.reshape(D, T).T.reshape(1, T, D)
    return np.ascontiguousarray(out)

import dataclasses as _dc

def build_fused():
    nc = bass.Bass("TRN2", target_bir_lowering=False, debug=False)
    gdr = declare_globals(nc)
    OL0H = [nc.dram_tensor(f"OL0H{i}", [128, T], BF16, kind="Internal").ap() for i in range(2)]
    OL0GH = [nc.dram_tensor(f"OL0GH{i}", [1024, T], BF16, kind="Internal", addr_space="Shared").ap()
             for i in range(2)]
    OH1F0 = nc.dram_tensor("OH1F0", [8, 128, TOK], BF16, kind="Internal").ap()
    OG1F0 = nc.dram_tensor("OG1F0", [8, 1024, TOK], BF16, kind="Internal", addr_space="Shared").ap()
    OH1F1 = nc.dram_tensor("OH1F1", [8, 128, TOK], BF16, kind="Internal").ap()
    OG1F1 = nc.dram_tensor("OG1F1", [8, 1024, TOK], BF16, kind="Internal", addr_space="Shared").ap()
    LAT0 = nc.dram_tensor("LAT0", [2, 128, 9 * 512], BF16, kind="Internal").ap()
    LATG = nc.dram_tensor("LATG", [2, 1024, 9 * 512], BF16, kind="Internal", addr_space="Shared").ap()
    OL1H = [nc.dram_tensor(f"OL1H{i}", [128, T], BF16, kind="Internal").ap() for i in range(2)]
    OL1GH = [nc.dram_tensor(f"OL1GH{i}", [1024, T], BF16, kind="Internal", addr_space="Shared").ap()
             for i in range(2)]

    drA1 = declare_A1(nc, None)
    WO0 = nc.dram_tensor("WO0", [NCH, 128, NCH * 128], BF16, kind="ExternalInput").ap()
    XOWN = nc.dram_tensor("XOWN", [128, NCH * TOK], BF16, kind="ExternalInput").ap()
    drB = declare_B(nc)
    drC0 = declare_C0(nc, None, None, None)
    drC1 = declare_C1(nc, None)
    WO1 = nc.dram_tensor("WO1", [NCH, 128, NCH * 128], F32, kind="ExternalInput").ap()
    X4 = nc.dram_tensor("X4", [128, NCH * TOK], F32, kind="ExternalOutput").ap()
    drD = declare_D(nc, X4)

    RG = [list(range(NCORE))]

    with ExitStack() as ctx:
        p = Prog(nc, ctx)
        block = ctx.enter_context(nc.Block())
        g = emit_globals(p, nc, ctx, gdr)
        xres = ctx.enter_context(nc.sbuf_tensor("xres", [128, NCH * TOK], BF16))

        def allgather(name, pairs, wait):
            s_cc = p.sem(name)
            for a, bb in pairs:
                p.emit("gpsimd",
                       lambda e, a=a, bb=bb: e.collective_compute(
                           "AllGather", ALU.bypass, replica_groups=RG,
                           ins=[a.opt()], outs=[bb.opt()]),
                       wait=wait, inc=[(s_cc, 1)])
            return s_cc

        rtok_cache = {}
        def o_store_gather(OH, OGH, H1F, G1F, names):
            s_os = p.sem(names + "s")
            s_cc = p.sem(names + "g")
            st = {"n": 0, "done": {}, "sem": s_cc}
            def cb(h, src_ap, wait, chunk=None):
                st["n"] += 1
                sv = 16 * st["n"]
                if chunk is None:
                    p.dma("gpsimd", OH[h][:, :], src_ap, s_os, wait=wait)
                    ins, outs = OH[h], OGH[h]
                else:
                    p.dma("gpsimd", H1F[chunk], src_ap, s_os, wait=wait)
                    ins, outs = H1F[chunk], G1F[chunk]
                p.emit("gpsimd",
                       lambda e, ins=ins, outs=outs: e.collective_compute(
                           "AllGather", ALU.bypass, replica_groups=RG,
                           ins=[ins.opt()], outs=[outs.opt()]),
                       wait=[(s_os, sv)], inc=[(s_cc, 1)])
                st["done"][h] = s_cc.n
                return (s_os, sv)
            return cb, st

        def dyn_of_loader(OGH, G1F, st):
            def loader(p, of, s_ld):
                for fc in range(NCH):
                    r0 = (fc // 2) * 128
                    hl = fc % 2
                    if hl == 0:
                        base = OGH[0][r0:r0 + 128, 0:TOK]
                        mult = TOK
                    else:
                        base = G1F[0][r0:r0 + 128, 0:TOK]
                        mult = 1024 * TOK
                    def fn(e, base=base, fc=fc, of=of, mult=mult):
                        key = (id(e), mult)
                        if key not in rtok_cache:
                            rtok_cache[key] = e.snap(e.partition_id() * mult)
                        dyn = _dc.replace(base, offset=base.offset + rtok_cache[key])
                        return e.dma_start(out=of[:, fc * TOK:(fc + 1) * TOK], in_=dyn)
                    p.emit("sync", fn, wait=[(st["sem"], st["done"][hl])], inc=[(s_ld, 16)])
            return loader

        s_xw = p.sem("xw")
        def xres_writer(sl, slab, wait, ec0=0, ec1=NCH):
            dst = xres[:].rearrange("p (c t) -> p c t", t=TOK)[:, ec0:ec1, sl * 512:(sl + 1) * 512]
            srcap = slab[:].rearrange("p (c t) -> p c t", t=512)[:, ec0:ec1, :]
            p.dma("gpsimd", dst, srcap, s_xw, wait=wait)

        with ExitStack() as pc:
            cb0, ost0 = o_store_gather(OL0H, OL0GH, OH1F0, OG1F0, "c0o")
            emit_A1(p, nc, pc, g, drA1, o_store=cb0, tag="a1")
            p.full_barrier("b1")

        with ExitStack() as pc:
            emit_A2(p, nc, pc, g, dict(WO=WO0, XO=XOWN), wo_f32=False,
                    of_loader=dyn_of_loader(OL0GH, OG1F0, ost0), out_writer=xres_writer, tag="a2")
            p.full_barrier("b2")

        with ExitStack() as pc:
            emit_B(p, nc, pc, g, drB, x_ap=xres[:], out_writer=xres_writer, tag="b")
            p.full_barrier("b3")

        s_l0 = p.sem("l0s")
        cc1 = p.sem("ccL")
        def lat_slab_store(sl, ckv, cq, kr, wait):
            p.dma("gpsimd", LAT0[sl][:, 0:4 * 512].rearrange("p (a t) -> p a t", t=512),
                  ckv[:].rearrange("p (a t) -> p a t", t=TOK)[:, :, sl * 512:(sl + 1) * 512],
                  s_l0, wait=wait)
            p.dma("gpsimd", LAT0[sl][:, 4 * 512:8 * 512].rearrange("p (a t) -> p a t", t=512),
                  cq[:].rearrange("p (a t) -> p a t", t=TOK)[:, :, sl * 512:(sl + 1) * 512],
                  s_l0, wait=wait)
            p.dma("gpsimd", LAT0[sl][0:64, 8 * 512:9 * 512], kr[0:64, sl * 512:(sl + 1) * 512],
                  s_l0, wait=wait)
            p.dma("gpsimd", LAT0[sl][64:128, 8 * 512:9 * 512], kr[0:64, sl * 512:(sl + 1) * 512],
                  s_l0, wait=wait)
            p.emit("gpsimd",
                   lambda e, sl=sl: e.collective_compute(
                       "AllGather", ALU.bypass, replica_groups=RG,
                       ins=[LAT0[sl].opt()], outs=[LATG[sl].opt()]),
                   wait=[(s_l0, 64 * (sl + 1))], inc=[(cc1, 1)])
        with ExitStack() as pc:
            emit_C0(p, nc, pc, g, drC0, x_ap=xres[:], slab_store=lat_slab_store, tag="c0")
            p.full_barrier("b4")

        def ckv_src(js):
            r, ts = js // 2, js % 2
            return LATG[ts][r * 128:(r + 1) * 128, 0:4 * 512].rearrange(
                "p (a t) -> p a t", a=4)
        def cq_src(js):
            r, ts = js // 2, js % 2
            return LATG[ts][r * 128:(r + 1) * 128, 4 * 512:8 * 512].rearrange(
                "p (a t) -> p a t", a=4)
        def kr_src(js):
            r, ts = js // 2, js % 2
            return LATG[ts][r * 128:r * 128 + 64, 8 * 512:9 * 512]
        with ExitStack() as pc:
            cb1, ost1 = o_store_gather(OL1H, OL1GH, OH1F1, OG1F1, "c1o")
            emit_C1(p, nc, pc, g, drC1, ckv_src=ckv_src, cq_src=cq_src,
                    kr_src=kr_src, o_store=cb1,
                    lat_gate=lambda js: [(cc1, (js % 2) + 1)], tag="c1")
            p.full_barrier("b5")

        with ExitStack() as pc:
            emit_A2(p, nc, pc, g, dict(WO=WO1), wo_f32=True,
                    of_loader=dyn_of_loader(OL1GH, OG1F1, ost1), xo_ap=xres[:],
                    out_writer=xres_writer, tag="c2")
            p.full_barrier("b6")

        with ExitStack() as pc:
            emit_D(p, nc, pc, g, drD, x_ap=xres[:], tag="d")

        p.finalize(block)
    return nc


def stage_fused(inp):
    mA1 = stage_A1(inp)
    wo0 = stage_lhsT_tiles(np.asarray(inp["L0_attn_Wo"]))
    wo1 = np.ascontiguousarray(stage_lhsT_tiles(np.asarray(inp["L1_attn_w_o"])).astype(np.float32))
    gw = stage_lhsT_tiles(np.asarray(inp["L0_ffn_gate_w"]).T)
    uw = stage_lhsT_tiles(np.asarray(inp["L0_ffn_up_w"]).T)
    dwt = stage_lhsT_tiles(np.asarray(inp["L0_ffn_down_w"]).T)
    dw = np.ascontiguousarray(
        dwt.reshape(NCH, 128, 2, 22 * 128).transpose(0, 2, 1, 3).reshape(NCH * 2, 128, 22 * 128))
    gs = chunk_col(inp["L0_ffn_gate_s"]).astype(np.float32)
    us = chunk_col(inp["L0_ffn_up_s"]).astype(np.float32)
    ds = chunk_col(inp["L0_ffn_down_s"]).astype(np.float32)
    mC0 = stage_C0(inp, [None] * NCORE)
    mC1 = stage_C1(inp, None, None, None)
    mD = stage_D(inp, [None] * NCORE)
    x = np.asarray(inp["L0_attn_x"])[0]
    XF = np.ascontiguousarray(fm(x).reshape(NCH, 128, T).transpose(1, 0, 2).reshape(128, NCH * T))
    maps = []
    for c in range(NCORE):
        m = dict(mA1[c])
        m.update(WO0=wo0, WO1=wo1,
                 XOWN=fm_full_slice_own(XF, c),
                 GW=gw, UW=uw, DW=dw, GS=gs, US=us, DS=ds,
                 WCKV=mC0[c]["WCKV"], WCQ=mC0[c]["WCQ"], WKR=mC0[c]["WKR"],
                 WKH=mC1[c]["WKH"], WVH=mC1[c]["WVH"], WQH=mC1[c]["WQH"],
                 WQRH=mC1[c]["WQRH"], COS64=mC1[c]["COS64"], SIN64=mC1[c]["SIN64"],
                 PIT64=mC1[c]["PIT64"],
                 WB=mD[c]["WB"], ROUT=mD[c]["ROUT"], ACAT=mD[c]["ACAT"],
                 BSTK=mD[c]["BSTK"], EMAT=mD[c]["EMAT"])
        maps.append(m)
    return maps


def kernel(**inputs):
    inp = {k: np.asarray(v) for k, v in inputs.items()}
    cores = list(range(NCORE))
    nc = build_fused()
    res = run_bass_kernel_spmd(nc, stage_fused(inp), core_ids=cores)
    x4 = np.empty((NCH, 128, T), dtype=np.float32)
    for c in cores:
        x4[:, :, c * TOK:(c + 1) * TOK] = np.asarray(
            res.results[c]["X4"]).reshape(128, NCH, TOK).transpose(1, 0, 2)
    out = x4.reshape(D, T).T.reshape(1, T, D)
    return np.ascontiguousarray(out)
